# version 25
# baseline (speedup 1.0000x reference)
import math
import numpy as np
import ml_dtypes
from contextlib import ExitStack

import concourse.bass as bass
import concourse.mybir as mybir
from concourse.bass_utils import run_bass_kernel_spmd

F32 = mybir.dt.float32
BF16 = mybir.dt.bfloat16
AF = mybir.ActivationFunctionType
OP = mybir.AluOpType
AX = mybir.AxisListType.X

CM = 12582912.0
LAM = 0.01
EPS = 1e-5
QMAX = 127.0

ENGINES = ("tensor", "vector", "scalar", "gpsimd", "sync")


class Cfg:
    def __init__(self, B=2, S=2048, D=1024, H=16, HD=64, DL=256, I=2816, n_cores=8):
        self.B, self.S, self.D, self.H, self.HD, self.DL, self.I = B, S, D, H, HD, DL, I
        self.n_cores = n_cores
        self.QT = min(512, S)
        self.DC = D // 128
        self.LC = DL // 128
        self.IT = I // 128
        self.NKT = S // 128
        self.NQT = S // self.QT
        self.NTT = S // 128
        self.PAIRS = H // 2
        self.DMOD = 6 * D
        self.TOK = B * S
        self.NT512 = self.TOK // 512
        self.JW = min(512, D)
        self.JH = D // self.JW
        self.NQSUB = self.QT // 128
        assert S % 128 == 0 and D % 128 == 0 and DL % 128 == 0 and I % 128 == 0
        assert H % 2 == 0 and HD == 64 and S % self.QT == 0 and D % self.JW == 0


class Sem:
    def __init__(self, nc, name):
        self.h = nc.alloc_semaphore(name)
        self.n = 0
        self.emitted = 0
        self.name = name


class QueueCtr:
    R = 4

    def __init__(self, nc, name):
        self.name = name
        self.sems = [Sem(nc, f"{name}{r}") for r in range(self.R)]
        self.n = 0

    def expand(self, ndmas):
        w = []
        for r in range(self.R):
            nr = (ndmas - r + self.R - 1) // self.R if ndmas > r else 0
            if nr > 0:
                w.append((self.sems[r], 16 * nr))
        return w


class EV:
    def __init__(self):
        self.events = []

    def add(self, eng, fn, waits=(), inc=None, k=1):
        flat = []
        for (s, v) in waits:
            v = v if v is not None else s.n
            if isinstance(s, QueueCtr):
                flat.extend(s.expand(v))
            else:
                flat.append((s, v))
        if inc is not None:
            inc.n += k
        self.events.append((eng, tuple(flat), fn, inc, k))

    def add_dma(self, eng, qc, fn, waits=()):
        j = qc.n
        r = j % QueueCtr.R
        s = qc.sems[r]
        self.add(eng, fn, waits=tuple(waits) + ((s, 16 * (j // QueueCtr.R)),),
                 inc=s, k=16)
        qc.n += 1

    SELF_SERIAL = ("vector", "scalar")

    @staticmethod
    def _val_after(sem, k):
        sem.emitted += k
        return sem.emitted

    def emit(self, eng_name, engine):
        hw = {}
        last_inc = None
        for (e, waits, fn, inc, k) in self.events:
            if e != eng_name:
                continue
            allw = list(waits)
            if eng_name in self.SELF_SERIAL and last_inc is not None:
                allw.append(last_inc)
            for (s, v) in allw:
                if v > 0 and hw.get(s.name, -1) < v:
                    engine.wait_ge(s.h, v)
                    hw[s.name] = v
            instr = fn(engine)
            if inc is not None:
                instr.then_inc(inc.h, k)
                if eng_name in self.SELF_SERIAL:
                    last_inc = (inc, self._val_after(inc, k))


def build_program(cfg: Cfg):
    nc = bass.Bass('TRN2', target_bir_lowering=False)
    B, S, D, H, HD, DL, I = cfg.B, cfg.S, cfg.D, cfg.H, cfg.HD, cfg.DL, cfg.I
    QT, DC, LC, IT = cfg.QT, cfg.DC, cfg.LC, cfg.IT
    NKT, NQT, NTT, PAIRS = cfg.NKT, cfg.NQT, cfg.NTT, cfg.PAIRS
    TOK, NT512, JH, JW, NQSUB = cfg.TOK, cfg.NT512, cfg.JH, cfg.JW, cfg.NQSUB
    DMOD = cfg.DMOD
    NCORES = cfg.n_cores
    PW = 193
    KW = PAIRS * PW

    xd = nc.dram_tensor("x", [B, S, D], F32, kind="ExternalInput")
    wq_d = nc.dram_tensor("wq", [D, D], F32, kind="ExternalInput")
    wk_d = nc.dram_tensor("wk", [D, D], F32, kind="ExternalInput")
    wv_d = nc.dram_tensor("wv", [D, D], F32, kind="ExternalInput")
    wo_d = nc.dram_tensor("wo", [D, D], F32, kind="ExternalInput")
    wdn_d = nc.dram_tensor("wdn", [D, DL], F32, kind="ExternalInput")
    wuk_d = nc.dram_tensor("wuk", [DL, D], F32, kind="ExternalInput")
    wuv_d = nc.dram_tensor("wuv", [DL, D], F32, kind="ExternalInput")
    wmod_d = nc.dram_tensor("wmod", [D, DMOD], F32, kind="ExternalInput")
    cmod_d = nc.dram_tensor("cmod", [B, D], F32, kind="ExternalInput")
    clip_d = nc.dram_tensor("clip2", [1, 2], F32, kind="ExternalInput")
    wg_d = nc.dram_tensor("wg", [I, D], F32, kind="ExternalInput")
    wu_d = nc.dram_tensor("wu", [I, D], F32, kind="ExternalInput")
    wdw_d = nc.dram_tensor("wdw", [D, I], F32, kind="ExternalInput")
    id_d = nc.dram_tensor("idm", [128, 128], F32, kind="ExternalInput")
    fdn_d = nc.dram_tensor("fdn", [128, S], F32, kind="ExternalInput")
    fup_d = nc.dram_tensor("fup", [128, S], F32, kind="ExternalInput")
    gdn_d = nc.dram_tensor("gdn", [128, S], F32, kind="ExternalInput")
    gup_d = nc.dram_tensor("gup", [128, S], F32, kind="ExternalInput")
    rdiag_d = nc.dram_tensor("rdiag", [128, 128], F32, kind="ExternalInput")
    ones_d = nc.dram_tensor("onesr", [1, 128], F32, kind="ExternalInput")
    konst_d = nc.dram_tensor("konst", [128, 8], F32, kind="ExternalInput")
    outd = nc.dram_tensor("out", [B, S, D], F32, kind="ExternalOutput")

    x1_dram = nc.dram_tensor("x1_scr", [TOK, D], BF16)
    h2_dram = nc.dram_tensor("h2_scr", [TOK, D], F32)
    hpre_dram = nc.dram_tensor("hpre_scr", [I, TOK], F32)
    qwgT_dram = nc.dram_tensor("qwgT_scr", [D, I], BF16)
    qwuT_dram = nc.dram_tensor("qwuT_scr", [D, I], BF16)
    qwdT_dram = nc.dram_tensor("qwdT_scr", [I, D], BF16)
    silc_dram = nc.dram_tensor("silc_scr", [B, D], BF16)
    mod_dram = nc.dram_tensor("mod_scr", [B, DMOD], F32)
    qdn_dram = nc.dram_tensor("qdn_scr", [DC * 128, S], BF16)
    qup_dram = nc.dram_tensor("qup_scr", [DC * 128, S], BF16)
    kdn_dram = nc.dram_tensor("kdn_scr", [DC * 128, S], BF16)
    kup_dram = nc.dram_tensor("kup_scr", [DC * 128, S], BF16)
    vaug_dram = nc.dram_tensor("vaug_scr", [NKT * 128, H * HD], BF16)
    pmax_dram = nc.dram_tensor("pmax_scr", [4, 128], F32)
    lrow_dram = nc.dram_tensor("lrow_scr", [2 * PAIRS, QT], F32)
    rrow_dram = nc.dram_tensor("rrow_scr", [2 * PAIRS, QT], F32)
    sd_dram = nc.dram_tensor("sd_scr", [DC, 128], F32)
    ag1_in = nc.dram_tensor("ag1_in", [1, 8], F32)
    ag1_out = nc.dram_tensor("ag1_out", [NCORES, 8], F32)
    ag2_in = nc.dram_tensor("ag2_in", [1, 8], F32)
    ag2_out = nc.dram_tensor("ag2_out", [NCORES, 8], F32)

    esg = ExitStack()

    def sb(name, shape, dt, stack=None):
        return (stack or esg).enter_context(nc.sbuf_tensor(name, shape, dt))

    konst = sb("konstt", [128, 8], F32)
    onesr = sb("onesrt", [1, 128], F32)
    idbf = sb("idbft", [128, 128], BF16)
    wqs = sb("wqs", [128, DC * D], BF16)
    wcomb = sb("wcomb", [128, DC * DL], BF16)
    wuks = sb("wuks", [128, LC * D], BF16)
    wuvs = sb("wuvs", [128, LC * D], BF16)
    wos = sb("wos", [128, DC * D], BF16)
    fdn = sb("fdns", [128, S], BF16)
    fup = sb("fups", [128, S], BF16)
    gdn = sb("gdns", [128, S], BF16)
    gup = sb("gups", [128, S], BF16)
    rdiag = sb("rdiagt", [128, 128], F32)
    sg_cols = sb("sg_cols", [128, IT], F32)
    su_cols = sb("su_cols", [128, IT], F32)
    sd_cols = sb("sd_cols", [128, DC], F32)
    sgx_cols = sb("sgx_cols", [128, IT], F32)
    q2s_cols = sb("q2s_cols", [128, IT], F32)
    negM = sb("negM", [128, 1], F32)
    sx_col = sb("sx_col", [128, 1], F32)
    sxr_col = sb("sxr_col", [128, 1], F32)
    srow = sb("srow", [1, 16], F32)
    rowbuf = sb("rowbuf", [1, max(512, D, 2 * QT)], F32)
    clip_sb = sb("clip_sb", [1, 2], F32)
    gmax_cols = sb("gmax_cols", [128, 2], F32)
    mq_cols = sb("mq_cols", [128, 4], F32)
    hmax_cols = sb("hmax_cols", [128, IT], F32)
    agrow = sb("agrow", [1, max(128, 8 * NCORES)], F32)

    SG = QueueCtr(nc, "Qg")
    SS = QueueCtr(nc, "Qs")
    ST = QueueCtr(nc, "Qt")
    SP, SV, SC, SCC = (Sem(nc, "S" + n) for n in ["p", "v", "c", "cc"])
    SST = SS
    allsems = [SP, SV, SC, SCC] + SG.sems + SS.sems + ST.sems

    def run_phase(ev, psum_stack=None):
        snap = [(s, s.n) for s in allsems if s.n > 0]
        for e in ENGINES:
            ev.add(e, lambda eng: eng.nop(), waits=snap)
        with nc.Block() as blk:
            blk.tensor(lambda e: ev.emit("tensor", e))
            blk.vector(lambda e: ev.emit("vector", e))
            blk.scalar(lambda e: ev.emit("scalar", e))
            blk.gpsimd(lambda e: ev.emit("gpsimd", e))
            blk.sync(lambda e: ev.emit("sync", e))
        if psum_stack is not None:
            psum_stack.close()

    def dma(ev, eng, qc, out, in_, waits=()):
        ev.add_dma(eng, qc, lambda e, o=out, i=in_: e.dma_start(out=o, in_=i), waits=waits)

    def dma_t(ev, eng, qc, out, in_, waits=()):
        ev.add_dma(eng, qc, lambda e, o=out, i=in_: e.dma_start_transpose(o, i), waits=waits)

    esA = ExitStack()
    psA_ = ExitStack()
    wmj = sb("wmj", [128, 2 * DC * 512], BF16, esA)
    cmod_sb = sb("cmod_sb", [B, D], F32, esA)
    silc = sb("silc", [B, D], BF16, esA)
    silcT = sb("silcT", [128, DC * B], BF16, esA)
    grow_t = sb("grow_t", [1, 3 * D], F32, esA)
    wvtmp = sb("wvtmp", [128, DC * D], BF16, esA)
    wkv = sb("wkv", [128, DC * D], BF16, esA)
    wkvT = sb("wkvT", [128, DC * D], BF16, esA)
    wdnb = sb("wdnb", [128, DC * DL], BF16, esA)

    pmod = psA_.enter_context(nc.psum_tensor("pmod", [128, 512], F32))
    ptr = [psA_.enter_context(nc.psum_tensor(f"ptr{i}", [128, 128], BF16)) for i in range(4)]
    pcmb = psA_.enter_context(nc.psum_tensor("pcmb", [128, 512], F32))

    ev = EV()
    for (dst, src) in ((konst, konst_d), (onesr, ones_d), (rdiag, rdiag_d), (clip_sb, clip_d)):
        dma(ev, "sync", SS, dst[:, :], src.ap())
    dma(ev, "sync", SS, cmod_sb[:, :], cmod_d.ap())
    s_basic = SS.n
    dma(ev, "gpsimd", SG, idbf[:, :], id_d.ap())
    for (dst, src) in ((fdn, fdn_d), (fup, fup_d), (gdn, gdn_d), (gup, gup_d)):
        dma(ev, "gpsimd", SG, dst[:, :], src.ap())
    for c in range(DC):
        dma(ev, "gpsimd", SG, wqs[:, c * D:(c + 1) * D], wq_d.ap()[c * 128:(c + 1) * 128, :])
        dma(ev, "gpsimd", SG, wkv[:, c * D:(c + 1) * D], wk_d.ap()[c * 128:(c + 1) * 128, :])
        dma(ev, "gpsimd", SG, wvtmp[:, c * D:(c + 1) * D], wv_d.ap()[c * 128:(c + 1) * 128, :])
        dma(ev, "gpsimd", SG, wdnb[:, c * DL:(c + 1) * DL], wdn_d.ap()[c * 128:(c + 1) * 128, :])
        dma(ev, "gpsimd", SG, wos[:, c * D:(c + 1) * D], wo_d.ap()[c * 128:(c + 1) * 128, :])
    for c in range(LC):
        dma(ev, "gpsimd", SG, wuks[:, c * D:(c + 1) * D], wuk_d.ap()[c * 128:(c + 1) * 128, :])
        dma(ev, "gpsimd", SG, wuvs[:, c * D:(c + 1) * D], wuv_d.ap()[c * 128:(c + 1) * 128, :])
    g_loads = SG.n

    ev.add("scalar", lambda e: e.activation(silc[:, :], cmod_sb[:, :], AF.Silu),
           waits=[(SS, s_basic)], inc=SC)
    dma(ev, "sync", SS, silc_dram.ap(), silc[:, :], waits=[(SC, None)])
    for c in range(DC):
        for b in range(B):
            dma(ev, "sync", SS, silcT[:, c * B + b:c * B + b + 1],
                silc_dram.ap()[b, c * 128:(c + 1) * 128].rearrange("(p a) -> p a", a=1),
                waits=[(SS, None)])
    silcT_ld = SS.n
    for c in range(DC):
        ev.add("vector", lambda e, c=c: e.tensor_add(
            wkv[:, c * D:(c + 1) * D], wkv[:, c * D:(c + 1) * D], wvtmp[:, c * D:(c + 1) * D]),
            waits=[(SG, g_loads)], inc=SV)
    wkv_done_v = SV.n
    NJ = DMOD // 512
    wm_marks = {}
    for j in range(NJ):
        slot = j % 2
        w = [(SP, wm_marks[j - 2])] if j >= 2 else []
        for c in range(DC):
            dma(ev, "gpsimd", SG, wmj[:, slot * DC * 512 + c * 512: slot * DC * 512 + (c + 1) * 512],
                wmod_d.ap()[c * 128:(c + 1) * 128, j * 512:(j + 1) * 512], waits=w if c == 0 else [])
        for b in range(B):
            for c in range(DC):
                ev.add("tensor", lambda e, b=b, j=j, c=c, slot=slot: e.matmul(
                    pmod[0:1, 0:512], silcT[:, c * B + b: c * B + b + 1],
                    wmj[:, slot * DC * 512 + c * 512: slot * DC * 512 + (c + 1) * 512],
                    start=(c == 0), stop=(c == DC - 1)),
                    waits=([(SS, silcT_ld), (SG, None), (SV, None)] if c == 0 else []),
                    inc=(SP if c == DC - 1 else None))
            ev.add("vector", lambda e: e.tensor_copy(rowbuf[0:1, 0:512], pmod[0:1, 0:512]),
                   waits=[(SP, None), (SS, None)], inc=SV)
            dma(ev, "sync", SS, mod_dram.ap()[b, j * 512:(j + 1) * 512].rearrange("(a c) -> a c", a=1),
                rowbuf[0:1, 0:512], waits=[(SV, None)])
        wm_marks[j] = SP.n
    mod_rows_done = SS.n
    for b in range(B):
        for gofs in (2 * D, 5 * D):
            dma(ev, "sync", SS, grow_t[0:1, 2 * D:3 * D],
                mod_dram.ap()[b, gofs:gofs + D].rearrange("(a c) -> a c", a=1),
                waits=[(SS, mod_rows_done), (SV, None)])
            src_row = grow_t[0:1, 2 * D:3 * D]
            ev.add("vector", lambda e, src_row=src_row: e.tensor_scalar_max(
                grow_t[0:1, 0:D], src_row, 0.25), waits=[(SS, None)], inc=SV)
            ev.add("vector", lambda e, src_row=src_row: e.tensor_scalar_min(
                grow_t[0:1, D:2 * D], src_row, -0.25), inc=SV)
            ev.add("vector", lambda e, src_row=src_row: e.tensor_single_scalar(
                grow_t[0:1, 2 * D:3 * D], src_row, 0.0, OP.is_ge), inc=SV)
            ev.add("vector", lambda e: e.tensor_sub(grow_t[0:1, 0:D], grow_t[0:1, 0:D], grow_t[0:1, D:2 * D]), inc=SV)
            ev.add("vector", lambda e: e.tensor_mul(grow_t[0:1, 0:D], grow_t[0:1, 0:D], grow_t[0:1, 2 * D:3 * D]), inc=SV)
            ev.add("vector", lambda e: e.tensor_add(grow_t[0:1, 0:D], grow_t[0:1, D:2 * D], grow_t[0:1, 0:D]), inc=SV)
            dma(ev, "sync", SS, mod_dram.ap()[b, gofs:gofs + D].rearrange("(a c) -> a c", a=1),
                grow_t[0:1, 0:D], waits=[(SV, None)])
    for c in range(DC):
        for cc in range(DC):
            idx = c * DC + cc
            pt = ptr[idx % 4]
            ev.add("tensor", lambda e, c=c, cc=cc, pt=pt: e.transpose(
                pt[:, :], wkv[:, c * D + cc * 128: c * D + (cc + 1) * 128], idbf[:, :]),
                waits=[(SV, wkv_done_v), (SG, None), (SV, None)], inc=SP)
            ev.add("vector", lambda e, c=c, cc=cc, pt=pt: e.tensor_copy(
                wkvT[:, cc * D + c * 128: cc * D + (c + 1) * 128], pt[:, :]),
                waits=[(SP, None)], inc=SV)
    for dt in range(DC):
        for ic in range(DC):
            ev.add("tensor", lambda e, dt=dt, ic=ic: e.matmul(
                pcmb[:, 0:DL], wkvT[:, ic * D + dt * 128: ic * D + (dt + 1) * 128],
                wdnb[:, ic * DL:(ic + 1) * DL], start=(ic == 0), stop=(ic == DC - 1)),
                waits=[(SV, None)] if ic == 0 else [], inc=(SP if ic == DC - 1 else None))
        ev.add("vector", lambda e, dt=dt: e.tensor_copy(wcomb[:, dt * DL:(dt + 1) * DL], pcmb[:, 0:DL]),
               waits=[(SP, None)], inc=SV)
    run_phase(ev)
    esA.close()
    psA_.close()

    esA2 = ExitStack()
    psA2_ = ExitStack()
    WMAXF = max(D, I)
    wfp = sb("wfp", [128, 3 * WMAXF], F32, esA2)
    tq_st = sb("tq_st", [128, 2 * WMAXF], F32, esA2)
    wint = sb("wint", [128, WMAXF], BF16, esA2)
    qT_stage = sb("qT_stage", [128, DC * I], BF16, esA2)
    acols = sb("acols", [128, 4], F32, esA2)
    ptr2 = [psA2_.enter_context(nc.psum_tensor(f"pt2_{i}", [128, 1024], BF16)) for i in range(2)]

    ev = EV()
    qtiles = []

    def quant_weight(src_d, nrows, ncols, scol, qT_dram):
        ntiles = nrows // 128
        for it in range(ntiles):
            qtiles.append((src_d, it, nrows, ncols, scol, qT_dram,
                           it == 0, it == ntiles - 1))

    def emit_quant_all():
        qgrp = [0]
        gev = {}
        gtp = {}
        id_mark = {}
        vq_mark = {}

        def head(gt):
            (src_d, it, nrows, ncols, scol, qT_dram, first, last) = qtiles[gt]
            buf = gt % 3
            par = (gt % 2) * 2
            dma(ev, "gpsimd", SG, wfp[:, buf * WMAXF: buf * WMAXF + ncols],
                src_d.ap()[it * 128:(it + 1) * 128, :],
                waits=[(SP, gtp[gt - 3])] if gt >= 3 else [])
            ev.add("vector", lambda e, buf=buf, ncols=ncols, par=par: e.reduce_max(
                acols[:, par:par + 1], wfp[:, buf * WMAXF: buf * WMAXF + ncols], axis=AX,
                apply_absolute_value=True),
                waits=[(SG, None), (SC, id_mark.get(gt - 2, 0))], inc=SV)
            ev.add("vector", lambda e, par=par: e.reciprocal(
                acols[:, par + 1:par + 2], acols[:, par:par + 1]), inc=SV)
            ev.add("vector", lambda e, par=par: e.tensor_scalar_mul(
                acols[:, par + 1:par + 2], acols[:, par + 1:par + 2], QMAX), inc=SV)
            ev.add("vector", lambda e, scol=scol, it=it, par=par: e.tensor_scalar_mul(
                scol[:, it:it + 1], acols[:, par:par + 1], 1.0 / QMAX), inc=SV)
            ev.add("scalar", lambda e, buf=buf, ncols=ncols, par=par, gt=gt: e.activation(
                tq_st[:, (gt % 2) * WMAXF:(gt % 2) * WMAXF + ncols],
                wfp[:, buf * WMAXF: buf * WMAXF + ncols], AF.Identity,
                bias=konst[:, 0:1], scale=acols[:, par + 1:par + 2]),
                waits=[(SV, None)], inc=SC)
            id_mark[gt] = SC.n

        def tail(gt):
            (src_d, it, nrows, ncols, scol, qT_dram, first, last) = qtiles[gt]
            nblk = ncols // 128
            ev.add("vector", lambda e, ncols=ncols, gt=gt: e.tensor_scalar(
                wint[:, 0:ncols], tq_st[:, (gt % 2) * WMAXF:(gt % 2) * WMAXF + ncols],
                CM, None, OP.subtract),
                waits=[(SC, id_mark[gt]), (SP, gtp.get(gt - 1, 0))], inc=SV)
            vq = SV.n
            done = 0
            while done < nblk:
                g = qgrp[0]
                qgrp[0] += 1
                nb_ = min(8, nblk - done)
                pt = ptr2[g % 2]
                for j in range(nb_):
                    bkt = done + j
                    ev.add("tensor", lambda e, bkt=bkt, j=j, pt=pt: e.transpose(
                        pt[:, j * 128:(j + 1) * 128], wint[:, bkt * 128:(bkt + 1) * 128], idbf[:, :]),
                        waits=([(SV, vq), (SV, gev.get(g - 2, 0))] if j == 0 else []),
                        inc=(SP if j == nb_ - 1 else None))
                dstap = bass.AP(qT_stage[:, 0:1].tensor, done * nrows + it * 128,
                                [[DC * I, 128], [nrows, nb_], [1, 128]])
                evw2 = [(SP, None)] + ([(SS, None)] if (first and done == 0) else [])
                ev.add("vector", lambda e, dstap=dstap, pt=pt, nb_=nb_: e.tensor_copy(
                    dstap, pt[:, 0:nb_ * 128]), waits=evw2, inc=SV)
                gev[g] = SV.n
                done += nb_
            gtp[gt] = SP.n
            if last:
                for bkt in range(nblk):
                    dma(ev, "sync", SS, qT_dram.ap()[bkt * 128:(bkt + 1) * 128, :],
                        qT_stage[:, bkt * nrows:(bkt + 1) * nrows], waits=[(SV, None)])

        for gt in range(len(qtiles)):
            head(gt)
            if gt >= 1:
                tail(gt - 1)
        tail(len(qtiles) - 1)

    quant_weight(wg_d, I, D, sg_cols, qwgT_dram)
    quant_weight(wu_d, I, D, su_cols, qwuT_dram)
    quant_weight(wdw_d, D, I, sd_cols, qwdT_dram)
    emit_quant_all()
    for c in range(DC):
        dma(ev, "sync", SS, sd_dram.ap()[c, :].rearrange("(p a) -> p a", a=1),
            sd_cols[:, c:c + 1], waits=[(SV, None)])
    run_phase(ev)
    esA2.close()
    psA2_.close()

    esB = ExitStack()
    h1T = sb("h1T", [128, 2 * DC * 512], BF16, esB)
    xbt = sb("xbt", [128, 4 * D], BF16, esB)
    h1t = sb("h1t", [128, 2 * D], BF16, esB)
    h2f = sb("h2f", [128, 2 * D], F32, esB)
    st_sum = sb("st_sum", [128, NTT], F32, esB)
    st_sq = sb("st_sq", [128, NTT], F32, esB)
    st_tmp = sb("st_tmp", [128, NTT], F32, esB)
    st_negm = sb("st_negm", [128, NTT], F32, esB)
    st_inv = sb("st_inv", [128, NTT], F32, esB)
    car = sb("car", [128, D], BF16, esB)
    sar = sb("sar", [128, D], BF16, esB)
    cbr = sb("cbr", [128, D], BF16, esB)
    sbr = sb("sbr", [128, D], BF16, esB)
    gar = sb("gar", [128, D], BF16, esB)
    qstage = sb("qstage", [128, DC * 512], BF16, esB)
    kstage = sb("kstage", [128, DC * 512], BF16, esB)
    latr = sb("latr", [128, 2 * LC * 512], BF16, esB)
    vstage = sb("vstage", [128, 2 * JW], BF16, esB)
    qdnwin = sb("qdnwin", [128, DC * 512], BF16, esB)
    qupwin = sb("qupwin", [128, DC * 512], BF16, esB)
    VW = NKT * 193
    vwin = sb("vwin", [128, VW], BF16, esB)
    kdn_c = sb("kdn_c", [128, 2 * S], BF16, esB)
    kup_c = sb("kup_c", [128, 2 * S], BF16, esB)
    PAb = sb("PAb", [128, 2 * 2 * QT], BF16, esB)
    PBb = sb("PBb", [128, 2 * 2 * QT], BF16, esB)
    attnraw = sb("attnraw", [128, PAIRS * QT], BF16, esB)
    lrows = sb("lrows", [128, QT], F32, esB)
    sqb = sb("sqb", [128, 2 * 512], BF16, esB)
    lst = None
    rip = lrows
    rbuf = sb("rbuf", [128, 2 * QT], F32, esB)
    lst = rbuf
    x1t = sb("x1t", [128, 2 * D], BF16, esB)
    ot_t = None
    qsq = h2f
    ot_t = h2f[:, D:2 * D]

    psB_ = ExitStack()
    psS = [psB_.enter_context(nc.psum_tensor(f"psS{i}", [128, 1024], F32)) for i in range(3)]
    psS_A, psS_B = psS[0], psS[1]
    pv2 = [psB_.enter_context(nc.psum_tensor(f"pv2_{i}", [128, 512], F32)) for i in range(2)]
    pvA, pvB = pv2[0], pv2[1]
    prc = [psS[2][:, 0:512], psS[2][:, 512:1024]]

    for b in range(B):
        ev = EV()
        brow = b * S

        NB = D // JW
        for (dst, ofs, p1) in ((car, 1 * D, True), (sar, 0 * D, False), (cbr, 4 * D, True),
                               (sbr, 3 * D, False), (gar, 2 * D, False)):
            for j in range(NB):
                dma(ev, "sync", SS, rowbuf[0:1, 0:JW],
                    mod_dram.ap()[b, ofs + j * JW: ofs + (j + 1) * JW].rearrange("(a c) -> a c", a=1),
                    waits=[(SP, None)])
                ev.add("tensor", lambda e: e.matmul(
                    prc[0][:, 0:JW], onesr[0:1, 0:128], rowbuf[0:1, 0:JW],
                    start=True, stop=True), waits=[(SS, None), (SV, None)], inc=SP)
                ev.add("vector", (lambda e, dst=dst, j=j: e.tensor_scalar(
                    dst[:, j * JW:(j + 1) * JW], prc[0][:, 0:JW], 1.0, None, OP.add)) if p1 else
                    (lambda e, dst=dst, j=j: e.tensor_copy(
                        dst[:, j * JW:(j + 1) * JW], prc[0][:, 0:JW])),
                    waits=[(SP, None)], inc=SV)
        ev.add("vector", lambda e: e.memset(
            bass.AP(vwin[:, 0:1].tensor, 64, [[VW, 128], [193, NKT], [1, 2]]), 1.0), inc=SV)
        ev.add("vector", lambda e: e.memset(
            bass.AP(vwin[:, 0:1].tensor, 66, [[VW, 128], [193, NKT], [1, 63]]), 0.0), inc=SV)

        ln_marks = {}
        for tt in range(NTT):
            buf = tt % 4
            w = []
            if tt >= 4:
                w = [(SV, ln_marks[tt - 4][0]), (SC, ln_marks[tt - 4][1])]
            dma(ev, "gpsimd", SG, xbt[:, buf * D:(buf + 1) * D],
                xd.ap()[b, tt * 128:(tt + 1) * 128, :], waits=w)
            ev.add("vector", lambda e, tt=tt, buf=buf: e.reduce_sum(
                st_sum[:, tt:tt + 1], xbt[:, buf * D:(buf + 1) * D], axis=AX),
                waits=[(SG, None)], inc=SV)
            ev.add("scalar", lambda e, tt=tt, buf=buf: e.activation(
                h2f[:, 0:D], xbt[:, buf * D:(buf + 1) * D], AF.Square,
                accum_out=st_sq[:, tt:tt + 1]), waits=[(SG, None), (SV, None)], inc=SC)
            ln_marks[tt] = (SV.n, SC.n)
        ev.add("vector", lambda e: e.tensor_scalar_mul(st_negm[:, 0:NTT], st_sum[:, 0:NTT], -1.0 / D),
               waits=[(SC, None)], inc=SV)
        ev.add("vector", lambda e: e.tensor_mul(st_tmp[:, 0:NTT], st_negm[:, 0:NTT], st_negm[:, 0:NTT]), inc=SV)
        ev.add("vector", lambda e: e.scalar_tensor_tensor(
            st_tmp[:, 0:NTT], st_sq[:, 0:NTT], 1.0 / D, st_tmp[:, 0:NTT], OP.mult, OP.subtract), inc=SV)
        ev.add("scalar", lambda e: e.activation(st_tmp[:, 0:NTT], st_tmp[:, 0:NTT], AF.Sqrt,
                                                bias=konst[:, 1:2], scale=1.0),
               waits=[(SV, None)], inc=SC)
        ev.add("vector", lambda e: e.reciprocal(st_inv[:, 0:NTT], st_tmp[:, 0:NTT]),
               waits=[(SC, None)], inc=SV)
        stats1 = SV.n

        NS = S // 512
        ts_marks = {}
        ap_marks = {}
        nrm_first = [True, True]
        nbm, nbx, nbi = {}, {}, [0]
        onesc_bf = vwin[:, 64:65]

        def norm_bound(stage, scol, nw):
            for dt in range(DC):
                gi = nbi[0]
                nbi[0] += 1
                sq = sqb[:, (gi % 2) * 512:(gi % 2 + 1) * 512]
                pr = prc[gi % 2]
                ev.add("scalar", lambda e, dt=dt, stage=stage, sq=sq: e.activation(
                    sq, stage[:, dt * 512:(dt + 1) * 512], AF.Square),
                    waits=[(SV, nw), (SP, nbm.get(gi - 2, 0))], inc=SC)
                ev.add("tensor", lambda e, sq=sq, pr=pr: e.matmul(
                    pr[0:1, 0:512], onesc_bf, sq,
                    start=True, stop=True),
                    waits=[(SC, None), (SV, nbx.get(gi - 2, 0))], inc=SP)
                nbm[gi] = SP.n
                ev.add("vector", lambda e, pr=pr: e.reduce_max(srow[0:1, 5:6], pr[0:1, 0:512], axis=AX),
                       waits=[(SP, None)], inc=SV)
                first = nrm_first[scol]
                ev.add("vector", (lambda e, scol=scol: e.tensor_copy(
                    srow[0:1, scol:scol + 1], srow[0:1, 5:6])) if first else
                    (lambda e, scol=scol: e.tensor_max(
                        srow[0:1, scol:scol + 1], srow[0:1, scol:scol + 1], srow[0:1, 5:6])), inc=SV)
                nbx[gi] = SV.n
                nrm_first[scol] = False

        b2m = {}
        trm = {}
        for ts in range(NS):
            slot = ts % 2
            t0 = ts * 512
            for tq in range(4):
                tt = ts * 4 + tq
                buf = tt % 4
                w = [(SV, ap_marks[tt - 4])] if tt >= 4 else [(SV, stats1)]
                dma(ev, "gpsimd", SG, xbt[:, buf * D:(buf + 1) * D],
                    xd.ap()[b, tt * 128:(tt + 1) * 128, :], waits=w)
                w2 = [(SG, None), (SV, stats1)] + (
                    [(ST, trm[tt - 2][0]), (SS, trm[tt - 2][1])] if tt >= 2 else [])
                ev.add("vector", lambda e, tt=tt, buf=buf: e.tensor_scalar(
                    h2f[:, 0:D], xbt[:, buf * D:(buf + 1) * D],
                    st_negm[:, tt:tt + 1], st_inv[:, tt:tt + 1], OP.add, OP.mult),
                    waits=w2, inc=SV)
                ev.add("vector", lambda e: e.tensor_mul(h2f[:, D:2 * D], h2f[:, 0:D], car[:, 0:D]), inc=SV)
                ev.add("vector", lambda e, hb2=tt % 2: e.tensor_add(
                    h1t[:, hb2 * D:(hb2 + 1) * D], h2f[:, D:2 * D], sar[:, 0:D]), inc=SV)
                ap_marks[tt] = SV.n
                wtr = [(SV, None)] + ([(SP, ts_marks[ts - 2])] if ts >= 2 else [])
                hb = tt % 2
                for c in range(DC):
                    eng2, qc2 = ("scalar", ST) if c % 2 == 0 else ("sync", SS)
                    dma_t(ev, eng2, qc2, h1T[:, slot * DC * 512 + c * 512 + tq * 128:
                                              slot * DC * 512 + c * 512 + (tq + 1) * 128],
                          h1t[:, hb * D + c * 128:hb * D + (c + 1) * 128], waits=wtr)
                trm[tt] = (ST.n, SS.n)
            h1T_ts = (ST.n, SS.n)
            psum_last = {0: None, 1: None, 2: None, 3: None}
            pquads = [psS[0][:, 0:512], psS[0][:, 512:1024],
                      psS[1][:, 0:512], psS[1][:, 512:1024]]
            acc_gi = [0]

            def acc_proj(groups, nchunks, mm_fn, evac_fn, first_waits, evw=()):
                for gkey in groups:
                    par = acc_gi[0] % 4
                    acc_gi[0] += 1
                    pj = pquads[par]
                    for c in range(nchunks):
                        w3 = []
                        if c == 0:
                            if psum_last[par] is not None:
                                w3.append((SV, psum_last[par]))
                            w3 += first_waits
                        ev.add("tensor", lambda e, gkey=gkey, c=c, pj=pj: mm_fn(e, gkey, c, pj),
                               waits=w3, inc=(SP if c == nchunks - 1 else None))
                    ev.add("vector", lambda e, gkey=gkey, pj=pj: evac_fn(e, gkey, pj),
                           waits=[(SP, None)] + list(evw), inc=SV)
                    psum_last[par] = SV.n

            fw = [(ST, h1T_ts[0]), (SS, h1T_ts[1])]
            sof = slot * DC * 512
            acc_proj(list(range(DC)), DC,
                     lambda e, dt, c, pj, sof=sof: e.matmul(
                         pj[:, 0:512], wqs[:, c * D + dt * 128:c * D + (dt + 1) * 128],
                         h1T[:, sof + c * 512: sof + (c + 1) * 512],
                         start=(c == 0), stop=(c == DC - 1)),
                     lambda e, dt, pj: e.tensor_copy(
                         qstage[:, dt * 512:(dt + 1) * 512], pj[:, 0:512]), fw,
                     evw=[(SS, b2m.get(('k', ts - 1), 0)), (SC, b2m.get(('nq', ts - 1), 0))])
            q_ev = SV.n
            for (fr, qd_) in ((fdn, qdn_dram), (fup, qup_dram)):
                ev.add("vector", lambda e, fr=fr, t0=t0: e.tensor_tensor(
                    kstage[:, 0:DC * 512], qstage[:, 0:DC * 512],
                    bass.AP(fr[:, 0:1].tensor, t0, [[S, 128], [0, DC], [1, 512]]),
                    OP.mult), waits=[(SV, q_ev), (SS, None), (SC, b2m.get(('nk', ts - 1), 0))], inc=SV)
                dma(ev, "sync", SS,
                    qd_.ap().rearrange("(c p) s -> p c s", p=128)[:, :, t0:t0 + 512],
                    kstage[:, 0:DC * 512], waits=[(SV, None)])
            b2m[('q', ts)] = SS.n
            norm_bound(qstage, 0, q_ev)
            b2m[('nq', ts)] = SC.n
            acc_proj(list(range(LC)), DC,
                     lambda e, lt, c, pj, sof=sof: e.matmul(
                         pj[:, 0:512], wcomb[:, c * DL + lt * 128:c * DL + (lt + 1) * 128],
                         h1T[:, sof + c * 512: sof + (c + 1) * 512],
                         start=(c == 0), stop=(c == DC - 1)),
                     lambda e, lt, pj, slot=slot: e.tensor_copy(
                         latr[:, slot * LC * 512 + lt * 512: slot * LC * 512 + (lt + 1) * 512],
                         pj[:, 0:512]), [])
            acc_proj(list(range(DC)), LC,
                     lambda e, dt, c, pj, slot=slot: e.matmul(
                         pj[:, 0:512], wuks[:, c * D + dt * 128:c * D + (dt + 1) * 128],
                         latr[:, slot * LC * 512 + c * 512: slot * LC * 512 + (c + 1) * 512],
                         start=(c == 0), stop=(c == LC - 1)),
                     lambda e, dt, pj: e.tensor_copy(
                         kstage[:, dt * 512:(dt + 1) * 512], pj[:, 0:512]), [(SV, None)],
                     evw=[(SS, b2m[('q', ts)]), (SC, b2m.get(('nk', ts - 1), 0))])
            k_ev = SV.n
            for (gr, kd_) in ((gdn, kdn_dram), (gup, kup_dram)):
                ev.add("vector", lambda e, gr=gr, t0=t0: e.tensor_tensor(
                    qstage[:, 0:DC * 512], kstage[:, 0:DC * 512],
                    bass.AP(gr[:, 0:1].tensor, t0, [[S, 128], [0, DC], [1, 512]]),
                    OP.mult), waits=[(SV, k_ev), (SS, None), (SP, None), (SC, None)], inc=SV)
                dma(ev, "sync", SS,
                    kd_.ap().rearrange("(c p) s -> p c s", p=128)[:, :, t0:t0 + 512],
                    qstage[:, 0:DC * 512], waits=[(SV, None)])
            b2m[('k', ts)] = SS.n
            norm_bound(kstage, 1, k_ev)
            b2m[('nk', ts)] = SC.n
            for tq in range(4):
                kt = ts * 4 + tq
                for jh in range(JH):
                    vb = (tq * JH + jh) % 2
                    acc_proj([(tq, jh)], LC,
                             lambda e, k_, c, pj, slot=slot: e.matmul(
                                 pj[:, 0:JW],
                                 latr[:, slot * LC * 512 + c * 512 + k_[0] * 128:
                                      slot * LC * 512 + c * 512 + k_[0] * 128 + 128],
                                 wuvs[:, c * D + k_[1] * JW:c * D + (k_[1] + 1) * JW],
                                 start=(c == 0), stop=(c == LC - 1)),
                             lambda e, k_, pj, vb=vb: e.tensor_copy(
                                 vstage[:, vb * JW:(vb + 1) * JW], pj[:, 0:JW]),
                             [], evw=[(SS, b2m.get(('v', vb), 0))])
                    dma(ev, "sync", SS,
                        vaug_dram.ap().rearrange("(k p) hd -> p k hd", p=128)[:, kt, jh * JW:(jh + 1) * JW],
                        vstage[:, vb * JW:(vb + 1) * JW], waits=[(SV, None)])
                    b2m[('v', vb)] = SS.n
            ts_marks[ts] = SP.n
        ev.add("vector", lambda e: e.tensor_mul(srow[0:1, 2:3], srow[0:1, 0:1], srow[0:1, 1:2]), inc=SV)
        ev.add("scalar", lambda e: e.activation(srow[0:1, 2:3], srow[0:1, 2:3], AF.Sqrt,
                                                bias=konst[:, 1:2][0:1, :], scale=1.0),
               waits=[(SV, None)], inc=SC)
        ev.add("vector", lambda e: e.tensor_scalar_mul(srow[0:1, 2:3], srow[0:1, 2:3], -(1.0 / math.sqrt(D))),
               waits=[(SC, None)], inc=SV)
        ev.add("tensor", lambda e: e.matmul(prc[0][:, 0:1], onesr[0:1, 0:128], srow[0:1, 2:3],
                                            start=True, stop=True), waits=[(SV, None)], inc=SP)
        ev.add("vector", lambda e: e.tensor_copy(negM[:, 0:1], prc[0][:, 0:1]),
               waits=[(SP, None)], inc=SV)

        win_marks = {}
        pv_g = {}
        next_lds = {}
        last_rbc = [0]

        def load_qwin(qi_i2, w):
            qi2 = qi_i2 * QT
            dma(ev, "sync", SS, qdnwin[:, 0:DC * 512],
                qdn_dram.ap().rearrange("(c p) s -> p c s", p=128)[:, :, qi2:qi2 + QT], waits=w)
            dma(ev, "sync", SS, qupwin[:, 0:DC * 512],
                qup_dram.ap().rearrange("(c p) s -> p c s", p=128)[:, :, qi2:qi2 + QT])
            return SS.n

        def load_pair_kv(pair, w):
            kb = pair % 2
            dma(ev, "sync", SS, kdn_c[:, kb * S:(kb + 1) * S],
                kdn_dram.ap()[pair * 128:(pair + 1) * 128, :], waits=w)
            dma(ev, "sync", SS, kup_c[:, kb * S:(kb + 1) * S],
                kup_dram.ap()[pair * 128:(pair + 1) * 128, :])
            return SS.n

        for qi_i in range(NQT):
            qi = qi_i * QT
            if qi_i == 0:
                qwin_ld = load_qwin(0, [(SS, None)])
                kv_marks = {0: load_pair_kv(0, [(SS, None)])}
            else:
                qwin_ld = next_lds['q']
                kv_marks = {0: next_lds['k']}
            nkp = NKT // 2
            slot_free = {}
            exp_marks = {}
            vfix = {}
            pctx = {}
            vscale = SV.n

            def emit_stile(pair, kp):
                kv_ld = kv_marks[pair]
                kb = pair % 2
                qo = pair * 512
                g = pair * nkp + kp
                for half in range(2):
                    kt = kp * 2 + half
                    kj = kt * 128
                    for hh in range(2):
                        sl = (2 * g + hh) % 3
                        ps = psS[sl]
                        rl, rh = (0, 64) if hh == 0 else (64, 128)
                        tp = (0, 0) if hh == 0 else (64, 0)
                        co = half * QT
                        ko = kb * S
                        mmw = [(SS, kv_ld), (SS, qwin_ld), (SC, slot_free.get(sl, 0))] + (
                            [(SV, vscale)] if pair == 0 else [])
                        def mk(e, kd, qd, n0, n1, ps=ps, rl=rl, rh=rh, tp=tp, co=co, kj=kj, ko=ko, qo=qo):
                            return e.matmul(ps[:, co + n0:co + n1],
                                            kd[rl:rh, ko + kj:ko + kj + 128], qd[rl:rh, qo + n0:qo + n1],
                                            start=True, stop=True, tile_position=tp)
                        if kj < qi:
                            ev.add("tensor", lambda e, f=mk: f(e, kdn_c, qdnwin, 0, QT), waits=mmw, inc=SP)
                        elif kj >= qi + QT:
                            ev.add("tensor", lambda e, f=mk: f(e, kup_c, qupwin, 0, QT), waits=mmw, inc=SP)
                        else:
                            j = (kj - qi) // 128
                            if j > 0:
                                ev.add("tensor", lambda e, f=mk, j=j: f(e, kup_c, qupwin, 0, j * 128),
                                       waits=mmw, inc=SP)
                            ev.add("tensor", lambda e, f=mk, j=j: f(e, kdn_c, qdnwin, j * 128, (j + 1) * 128),
                                   waits=mmw, inc=SP)
                            if j < NQSUB - 1:
                                ev.add("tensor", lambda e, f=mk, j=j: f(e, kdn_c, qdnwin, (j + 1) * 128, QT),
                                       waits=mmw, inc=SP)
                            ev.add("vector", lambda e, ps=ps, co=co, j=j: e.tensor_mul(
                                ps[:, co + j * 128:co + (j + 1) * 128],
                                ps[:, co + j * 128:co + (j + 1) * 128], rdiag[:, :]),
                                waits=[(SP, None)], inc=SV)
                            vfix[sl] = SV.n

            def emit_exp(pair, kp, smark):
                g = pair * nkp + kp
                pb = (g % 2) * 2 * QT
                gkp = (qi_i * PAIRS + pair) * nkp + kp
                bw = pv_g.get(gkp - 2, 0)
                for hh in range(2):
                    sl = (2 * g + hh) % 3
                    Pb = PAb if hh == 0 else PBb
                    ev.add("scalar", lambda e, sl=sl, Pb=Pb, pb=pb: e.activation(
                        Pb[:, pb:pb + 2 * QT], psS[sl][:, 0:2 * QT], AF.Exp,
                        bias=negM[:, 0:1], scale=1.0),
                        waits=[(SP, smark), (SP, bw), (SV, vfix.get(sl, 0))], inc=SC)
                    slot_free[sl] = SC.n
                exp_marks[(pair, kp)] = SC.n

            def emit_pv(pair, kp):
                g = pair * nkp + kp
                pb = (g % 2) * 2 * QT
                vv = pctx[pair][1]
                for half in range(2):
                    kt = kp * 2 + half
                    vw_w = []
                    if kt == 0:
                        vw_w = [(SV, None), (SS, vv[0])]
                    elif kt == NKT // 2:
                        vw_w = [(SS, vv[1])]
                    for hh in range(2):
                        Pb = PAb if hh == 0 else PBb
                        base = kt * 193 + (0 if hh == 0 else 65)
                        wid = 65 if hh == 0 else 128
                        pv = pvA if hh == 0 else pvB
                        pvsl = pv[0:65, 0:QT] if hh == 0 else pv[:, 0:QT]
                        ev.add("tensor", lambda e, base=base, wid=wid, pvsl=pvsl, Pb=Pb, half=half, kt=kt, pb=pb: e.matmul(
                            pvsl, vwin[:, base:base + wid],
                            Pb[:, pb + half * QT: pb + (half + 1) * QT],
                            start=(kt == 0), stop=(kt == NKT - 1)),
                            waits=[(SC, exp_marks[(pair, kp)])] + vw_w, inc=SP)
                pv_g[(qi_i * PAIRS + pair) * nkp + kp] = SP.n

            def emit_ltail(pair):
                pv_done = SP.n
                win_marks[(qi_i * PAIRS + pair, 'k')] = SP.n
                ev.add("vector", lambda e: e.tensor_copy(
                    lst[64:65, 0:QT], pvA[64:65, 0:QT]),
                    waits=[(SP, pv_done), (SS, None)], inc=SV)
                dma(ev, "sync", SS, lrow_dram.ap()[2 * pair:2 * pair + 1, :],
                    lst[64:65, 0:QT], waits=[(SV, None)])
                ev.add("vector", lambda e: e.tensor_copy(
                    lst[0:1, 0:QT], pvB[0:1, 0:QT]), inc=SV)
                dma(ev, "sync", SS, lrow_dram.ap()[2 * pair + 1:2 * pair + 2, :],
                    lst[0:1, 0:QT], waits=[(SV, None)])
                dma(ev, "sync", SS, lrows[2 * pair:2 * pair + 2, 0:QT],
                    lrow_dram.ap()[2 * pair:2 * pair + 2, :], waits=[(SS, None)])
                ev.add("vector", lambda e, pair=pair: e.tensor_copy(
                    attnraw[0:64, pair * QT:(pair + 1) * QT], pvA[0:64, 0:QT]), inc=SV)
                ev.add("vector", lambda e, pair=pair: e.tensor_copy(
                    attnraw[64:128, pair * QT:(pair + 1) * QT], pvB[64:128, 0:QT]), inc=SV)

            def load_vwin_half(pair, h, w):
                NH = NKT // 2
                k0 = h * NH
                dma(ev, "sync", SS,
                    bass.AP(vwin[:, 0:1].tensor, k0 * 193,
                            [[VW, 128], [193, NH], [1, 64]]),
                    vaug_dram.ap().rearrange("(k p) hd -> p k hd", p=128)[:, k0:k0 + NH, pair * 128:pair * 128 + 64],
                    waits=w)
                dma(ev, "sync", SS,
                    bass.AP(vwin[:, 0:1].tensor, k0 * 193 + 129,
                            [[VW, 128], [193, NH], [1, 64]]),
                    vaug_dram.ap().rearrange("(k p) hd -> p k hd", p=128)[:, k0:k0 + NH, pair * 128 + 64:(pair + 1) * 128])
                return SS.n

            for pair in range(PAIRS):
                pidx = qi_i * PAIRS + pair
                kb = pair % 2
                if pair == 0:
                    wk_w = [(SP, win_marks[(pidx - 1, 'k')])] if pidx >= 1 else [(SS, None)]
                    v0 = load_vwin_half(pair, 0, wk_w)
                    v1 = load_vwin_half(pair, 1, [])
                    pctx[pair] = (kv_marks[pair], [v0, v1], kb)
                for kp in range(nkp):
                    emit_stile(pair, kp)
                    smark = SP.n
                    if kp > 0:
                        emit_pv(pair, kp - 1)
                        if kp == nkp // 2 and pair + 1 < PAIRS:
                            v0n = load_vwin_half(pair + 1, 0,
                                                 [(SP, pv_g[pidx * nkp + nkp // 2 - 1])])
                            pctx[pair + 1] = (None, [v0n, None], kb ^ 1)
                    elif pair > 0:
                        emit_pv(pair - 1, nkp - 1)
                        emit_ltail(pair - 1)
                        v1n = load_vwin_half(pair, 1,
                                             [(SP, pv_g[(pidx - 1) * nkp + nkp - 1])])
                        pctx[pair] = (kv_marks[pair], [pctx[pair][1][0], v1n], kb)
                    if kp == 0 and pair + 1 < PAIRS:
                        kv_marks[pair + 1] = load_pair_kv(
                            pair + 1, [(SP, win_marks.get((pidx - 1, 'k'), 0) if pair >= 1 else smark)])
                    emit_exp(pair, kp, smark)
            emit_pv(PAIRS - 1, nkp - 1)
            emit_ltail(PAIRS - 1)
            win_marks[(qi_i, 'q')] = SP.n
            if qi_i + 1 < NQT:
                next_lds['q'] = load_qwin(qi_i + 1, [(SP, win_marks[(qi_i, 'q')])])
                next_lds['k'] = load_pair_kv(0, [(SP, win_marks[((qi_i + 1) * PAIRS - 2, 'k')])])
            ev.add("vector", lambda e: e.reciprocal(rip[0:2 * PAIRS, 0:QT], lrows[0:2 * PAIRS, 0:QT]),
                   waits=[(SS, None)], inc=SV)
            dma(ev, "sync", SS, rrow_dram.ap()[:, :], rip[0:2 * PAIRS, 0:QT], waits=[(SV, None)])
            fin_marks = {}
            rbc_marks = {}
            for pair in range(PAIRS):
                rsl = rbuf if pair % 2 == 0 else rowbuf
                w4 = [(SP, rbc_marks[pair - 2])] if pair >= 2 else [(SP, last_rbc[0])]
                dma(ev, "sync", SS, rsl[0:1, 0:QT],
                    rrow_dram.ap()[2 * pair:2 * pair + 1, :],
                    waits=([(SS, None)] if pair < 2 else []) + w4)
                dma(ev, "sync", SS, rsl[0:1, QT:2 * QT],
                    rrow_dram.ap()[2 * pair + 1:2 * pair + 2, :])
                pr = prc[pair % 2]
                w4b = [(SV, fin_marks[pair - 2])] if pair >= 2 else []
                ev.add("tensor", lambda e, pr=pr, rsl=rsl: e.matmul(
                    pr[0:64, 0:QT], onesr[0:1, 0:64], rsl[0:1, 0:QT],
                    start=True, stop=True, tile_position=(0, 0)),
                    waits=[(SS, None)] + w4b, inc=SP)
                ev.add("tensor", lambda e, pr=pr, rsl=rsl: e.matmul(
                    pr[64:128, 0:QT], onesr[0:1, 64:128], rsl[0:1, QT:2 * QT],
                    start=True, stop=True, tile_position=(0, 64)), inc=SP)
                rbc_marks[pair] = SP.n
                ev.add("vector", lambda e, pr=pr, pair=pair: e.tensor_mul(
                    attnraw[:, pair * QT:(pair + 1) * QT], attnraw[:, pair * QT:(pair + 1) * QT],
                    pr[:, 0:QT]), waits=[(SP, None)], inc=SV)
                fin_marks[pair] = SV.n
            last_rbc[0] = SP.n
            attn_done = SV.n
            ep_marks = {}
            for tq in range(NQSUB):
                tt = qi_i * NQSUB + tq
                buf = tq % 2
                if tq >= 2:
                    w5 = [(SV, ep_marks[(tq - 2, 'x')])]
                elif (tq + NQSUB - 2, 'x') in ep_marks:
                    w5 = [(SV, ep_marks[(tq + NQSUB - 2, 'x')])]
                else:
                    w5 = [(SV, ap_marks[NTT - 2 + tq])]
                dma(ev, "gpsimd", SG, xbt[:, buf * D:(buf + 1) * D],
                    xd.ap()[b, qi + tq * 128: qi + (tq + 1) * 128, :], waits=w5)
                for jh in range(JH):
                    po = prc[jh % 2] if JH > 1 else prc[tq % 2]
                    for c in range(DC):
                        w2 = []
                        if c == 0:
                            w2 = [(SV, attn_done)]
                            prevk = (tq - 1, 'e' + str(jh)) if JH >= 2 else (tq - 2, 'e' + str(jh))
                            if prevk in ep_marks:
                                w2.append((SV, ep_marks[prevk]))
                        ev.add("tensor", lambda e, tq=tq, jh=jh, c=c, po=po: e.matmul(
                            po[:, 0:JW], attnraw[:, c * QT + tq * 128: c * QT + (tq + 1) * 128],
                            wos[:, c * D + jh * JW: c * D + (jh + 1) * JW],
                            start=(c == 0), stop=(c == DC - 1)),
                            waits=w2, inc=(SP if c == DC - 1 else None))
                    ev.add("vector", lambda e, jh=jh, po=po: e.tensor_mul(
                        ot_t[:, jh * JW:(jh + 1) * JW], po[:, 0:JW],
                        gar[:, jh * JW:(jh + 1) * JW]),
                        waits=[(SP, None)], inc=SV)
                    ev.add("vector", lambda e, jh=jh, buf=buf: e.tensor_add(
                        x1t[:, buf * D + jh * JW: buf * D + (jh + 1) * JW],
                        ot_t[:, jh * JW:(jh + 1) * JW], xbt[:, buf * D + jh * JW: buf * D + (jh + 1) * JW]),
                        waits=[(SG, None)], inc=SV)
                    ep_marks[(tq, 'e' + str(jh))] = SV.n
                ev.add("vector", lambda e, tt=tt, buf=buf: e.reduce_sum(
                    st_sum[:, tt:tt + 1], x1t[:, buf * D:(buf + 1) * D], axis=AX), inc=SV)
                ep_marks[(tq, 'x')] = SV.n
                ev.add("scalar", lambda e, tt=tt, buf=buf: e.activation(
                    h2f[:, 0:D], x1t[:, buf * D:(buf + 1) * D], AF.Square,
                    accum_out=st_sq[:, tt:tt + 1]), waits=[(SV, None)], inc=SC)
                dma(ev, "gpsimd", SG, x1_dram.ap()[brow + qi + tq * 128: brow + qi + (tq + 1) * 128, :],
                    x1t[:, buf * D:(buf + 1) * D], waits=[(SV, None), (SC, None)])
        ev.add("vector", lambda e: e.tensor_scalar_mul(st_negm[:, 0:NTT], st_sum[:, 0:NTT], -1.0 / D),
               waits=[(SC, None), (SG, None)], inc=SV)
        ev.add("vector", lambda e: e.tensor_mul(st_tmp[:, 0:NTT], st_negm[:, 0:NTT], st_negm[:, 0:NTT]), inc=SV)
        ev.add("vector", lambda e: e.scalar_tensor_tensor(
            st_tmp[:, 0:NTT], st_sq[:, 0:NTT], 1.0 / D, st_tmp[:, 0:NTT], OP.mult, OP.subtract), inc=SV)
        ev.add("scalar", lambda e: e.activation(st_tmp[:, 0:NTT], st_tmp[:, 0:NTT], AF.Sqrt,
                                                bias=konst[:, 1:2], scale=1.0), waits=[(SV, None)], inc=SC)
        ev.add("vector", lambda e: e.reciprocal(st_inv[:, 0:NTT], st_tmp[:, 0:NTT]),
               waits=[(SC, None)], inc=SV)
        stats2 = SV.n
        h2_marks = {}
        h2_store = {}
        for tt in range(NTT):
            buf = tt % 2
            rs = h2f[:, buf * D:(buf + 1) * D]
            w6 = [(SV, h2_marks[tt - 2])] if tt >= 2 else [(SC, None), (SG, None), (SV, None)]
            dma(ev, "gpsimd", SG, x1t[:, buf * D:(buf + 1) * D],
                x1_dram.ap()[brow + tt * 128: brow + (tt + 1) * 128, :], waits=w6)
            ev.add("vector", lambda e, tt=tt, buf=buf, rs=rs: e.tensor_scalar(
                rs, x1t[:, buf * D:(buf + 1) * D],
                st_negm[:, tt:tt + 1], st_inv[:, tt:tt + 1], OP.add, OP.mult),
                waits=[(SG, h2_store.get(tt - 2, 0)), (SG, None), (SV, stats2), (SC, None)], inc=SV)
            ev.add("vector", lambda e, rs=rs: e.tensor_mul(rs, rs, cbr[:, 0:D]), inc=SV)
            ev.add("vector", lambda e, rs=rs: e.tensor_add(rs, rs, sbr[:, 0:D]), inc=SV)
            ev.add("vector", lambda e, rs=rs: e.reduce_max(mq_cols[:, 2:3], rs, axis=AX,
                                                           apply_absolute_value=True), inc=SV)
            first = (b == 0 and tt == 0)
            ev.add("vector", (lambda e: e.tensor_copy(gmax_cols[:, 0:1], mq_cols[:, 2:3])) if first
                   else (lambda e: e.tensor_max(gmax_cols[:, 0:1], gmax_cols[:, 0:1], mq_cols[:, 2:3])), inc=SV)
            h2_marks[tt] = SV.n
            dma(ev, "gpsimd", SG, h2_dram.ap()[brow + tt * 128: brow + (tt + 1) * 128, :],
                rs, waits=[(SV, None)])
            h2_store[tt] = SG.n
        run_phase(ev)

    esB.close()
    psB_.close()

    esD = ExitStack()
    psD_ = ExitStack()
    qwgT = sb("qwgT", [128, DC * I], BF16, esD)
    qwuT = sb("qwuT", [128, DC * I], BF16, esD)
    h2t = sb("h2t", [128, 2 * D], F32, esD)
    xqd = sb("xqd", [128, 2 * D], BF16, esD)
    xqT = sb("xqT", [128, 2 * DC * 512], BF16, esD)
    gsil = sb("gsil", [128, 4 * 512], F32, esD)
    hpre_t = sb("hpre_t", [128, 4 * 512], F32, esD)
    pg = [psD_.enter_context(nc.psum_tensor(f"pg{i}", [128, 512], F32)) for i in range(4)]
    pu = [psD_.enter_context(nc.psum_tensor(f"pu{i}", [128, 512], F32)) for i in range(4)]
    pbc2 = pg[0]
    ev = EV()
    for c in range(DC):
        dma(ev, "gpsimd", SG, qwgT[:, c * I:(c + 1) * I], qwgT_dram.ap()[c * 128:(c + 1) * 128, :])
        dma(ev, "gpsimd", SG, qwuT[:, c * I:(c + 1) * I], qwuT_dram.ap()[c * 128:(c + 1) * 128, :])
    gw_loaded = SG.n
    dma(ev, "sync", SS, pmax_dram.ap()[0:1, :].rearrange("a b -> (a b)"), gmax_cols[:, 0:1])
    dma(ev, "sync", SS, agrow[0:1, 0:128], pmax_dram.ap()[0:1, :], waits=[(SS, None)])
    ev.add("vector", lambda e: e.reduce_max(srow[0:1, 0:1], agrow[0:1, 0:128], axis=AX),
           waits=[(SS, None)], inc=SV)
    ev.add("vector", lambda e: e.memset(srow[0:1, 1:8], 0.0), inc=SV)
    dma(ev, "sync", SS, ag1_in.ap(), srow[0:1, 0:8], waits=[(SV, None)])
    ev.add("gpsimd", lambda e: e.collective_compute(
        "AllGather", OP.bypass, replica_groups=[list(range(NCORES))],
        ins=[ag1_in.ap().opt()], outs=[ag1_out.ap().opt()]),
        waits=[(SS, None)], inc=SCC)
    dma(ev, "sync", SS, agrow[0:1, 0:8 * NCORES],
        ag1_out.ap().rearrange("a b -> (a b)").rearrange("(a b) -> a b", a=1), waits=[(SCC, None)])
    ev.add("vector", lambda e: e.reduce_max(srow[0:1, 1:2], agrow[0:1, 0:8 * NCORES], axis=AX),
           waits=[(SS, None)], inc=SV)
    ev.add("scalar", lambda e: e.activation(srow[0:1, 2:3], clip_sb[0:1, 0:1], AF.Sigmoid),
           waits=[(SV, None)], inc=SC)
    ev.add("vector", lambda e: e.tensor_mul(srow[0:1, 3:4], srow[0:1, 1:2], srow[0:1, 2:3]),
           waits=[(SC, None)], inc=SV)
    ev.add("vector", lambda e: e.tensor_scalar_mul(srow[0:1, 3:4], srow[0:1, 3:4], 1.0 / QMAX), inc=SV)
    ev.add("vector", lambda e: e.reciprocal(srow[0:1, 4:5], srow[0:1, 3:4]), inc=SV)
    for (colt, sofs) in ((sx_col, 3), (sxr_col, 4)):
        ev.add("tensor", lambda e, sofs=sofs: e.matmul(pbc2[:, 0:1], onesr[0:1, 0:128],
                                                       srow[0:1, sofs:sofs + 1], start=True, stop=True),
               waits=[(SV, None)], inc=SP)
        ev.add("vector", lambda e, colt=colt: e.tensor_copy(colt[:, 0:1], pbc2[:, 0:1]),
               waits=[(SP, None)], inc=SV)
    ev.add("vector", lambda e: e.tensor_scalar(sgx_cols[:, 0:IT], sg_cols[:, 0:IT],
                                               sx_col[:, 0:1], None, OP.mult), inc=SV)
    ev.add("vector", lambda e: e.memset(hmax_cols[:, 0:IT], 0.0), inc=SV)
    d_marks = {}

    def d_quant(t5):
        r0 = t5 * 512
        sl = (t5 % 2) * DC * 512
        for sub in range(4):
            gs = t5 * 4 + sub
            bs = gs % 2
            dma(ev, "gpsimd", SG, h2t[:, bs * D:(bs + 1) * D],
                h2_dram.ap()[r0 + sub * 128: r0 + (sub + 1) * 128, :],
                waits=[(SV, d_marks.get(('dq', gs - 2), 0))])
            ev.add("vector", lambda e, bs=bs: e.tensor_scalar(
                h2t[:, bs * D:(bs + 1) * D], h2t[:, bs * D:(bs + 1) * D],
                sxr_col[:, 0:1], CM, OP.mult, OP.add), waits=[(SG, None)], inc=SV)
            ev.add("vector", lambda e, bs=bs: e.tensor_scalar(
                h2t[:, bs * D:(bs + 1) * D], h2t[:, bs * D:(bs + 1) * D],
                CM - 128.0, CM + 127.0, OP.max, OP.min), inc=SV)
            ev.add("vector", lambda e, bs=bs: e.tensor_scalar(
                xqd[:, bs * D:(bs + 1) * D], h2t[:, bs * D:(bs + 1) * D],
                CM, None, OP.subtract),
                waits=[(SS, d_marks.get(('dxT', gs - 2), (0, 0))[0]),
                       (ST, d_marks.get(('dxT', gs - 2), (0, 0))[1])], inc=SV)
            d_marks[('dq', gs)] = SV.n
            for c in range(DC):
                eng3, qc3 = ("sync", SS) if c % 2 == 0 else ("scalar", ST)
                dma_t(ev, eng3, qc3, xqT[:, sl + c * 512 + sub * 128: sl + c * 512 + (sub + 1) * 128],
                      xqd[:, bs * D + c * 128: bs * D + (c + 1) * 128],
                      waits=[(SV, None)] + ([(SP, d_marks[(t5 - 2, 'p')])] if t5 >= 2 else []))
            d_marks[('dxT', gs)] = (SS.n, ST.n)
        d_marks[(t5, 'hq')] = SV.n
        d_marks[(t5, 'xT')] = (SS.n, ST.n)

    def d_mms(t5):
        r0 = t5 * 512
        sl = (t5 % 2) * DC * 512
        gu_marks = d_marks
        def gprev(it, tag):
            return gu_marks.get(('g', t5 * IT + it - 4, tag), 0)
        for it in range(IT):
            buf = (t5 * IT + it) % 4
            for c in range(DC):
                w = []
                if c == 0:
                    w = [(SS, d_marks[(t5, 'xT')][0]), (ST, d_marks[(t5, 'xT')][1]), (SG, gw_loaded),
                         (SC, gprev(it, 's')), (SV, gprev(it, 'v'))]
                ev.add("tensor", lambda e, it=it, c=c, buf=buf, sl=sl: e.matmul(
                    pg[buf][:, 0:512], qwgT[:, c * I + it * 128: c * I + (it + 1) * 128],
                    xqT[:, sl + c * 512:sl + (c + 1) * 512], start=(c == 0), stop=(c == DC - 1)),
                    waits=w, inc=(SP if c == DC - 1 else None))
            for c in range(DC):
                ev.add("tensor", lambda e, it=it, c=c, buf=buf, sl=sl: e.matmul(
                    pu[buf][:, 0:512], qwuT[:, c * I + it * 128: c * I + (it + 1) * 128],
                    xqT[:, sl + c * 512:sl + (c + 1) * 512], start=(c == 0), stop=(c == DC - 1)),
                    inc=(SP if c == DC - 1 else None))
            ev.add("scalar", lambda e, it=it, buf=buf: e.activation(
                gsil[:, buf * 512:(buf + 1) * 512], pg[buf][:, 0:512], AF.Silu,
                bias=0.0, scale=sgx_cols[:, it:it + 1]),
                waits=[(SP, None), (SV, gprev(it, 'v'))], inc=SC)
            gu_marks[('g', t5 * IT + it, 's')] = SC.n
            hw = [(SC, None), (SG, gprev(it, 'g'))]
            ev.add("vector", lambda e, it=it, buf=buf: e.scalar_tensor_tensor(
                hpre_t[:, buf * 512:(buf + 1) * 512], pu[buf][:, 0:512], 1.0,
                gsil[:, buf * 512:(buf + 1) * 512], OP.bypass, OP.mult),
                waits=hw, inc=SV)
            ev.add("vector", lambda e, buf=buf: e.reduce_max(
                mq_cols[:, 2:3], hpre_t[:, buf * 512:(buf + 1) * 512], axis=AX,
                apply_absolute_value=True), inc=SV)
            ev.add("vector", lambda e, it=it: e.tensor_max(
                hmax_cols[:, it:it + 1], hmax_cols[:, it:it + 1], mq_cols[:, 2:3]), inc=SV)
            gu_marks[('g', t5 * IT + it, 'v')] = SV.n
            dma(ev, "gpsimd", SG, hpre_dram.ap()[it * 128:(it + 1) * 128, r0:r0 + 512],
                hpre_t[:, buf * 512:(buf + 1) * 512], waits=[(SV, None)])
            gu_marks[('g', t5 * IT + it, 'g')] = SG.n
        d_marks[(t5, 'p')] = SP.n

    for t5 in range(NT512 + 1):
        if t5 < NT512:
            d_quant(t5)
        if t5 >= 1:
            d_mms(t5 - 1)
    ev.add("vector", lambda e: e.tensor_mul(hmax_cols[:, 0:IT], hmax_cols[:, 0:IT], su_cols[:, 0:IT]),
           waits=[(SG, None)], inc=SV)
    ev.add("vector", lambda e: e.tensor_scalar(hmax_cols[:, 0:IT], hmax_cols[:, 0:IT],
                                               sx_col[:, 0:1], None, OP.mult), inc=SV)
    ev.add("vector", lambda e: e.reduce_max(mq_cols[:, 2:3], hmax_cols[:, 0:IT], axis=AX), inc=SV)
    run_phase(ev, psD_)
    esD.close()

    esE = ExitStack()
    esF = ExitStack()
    gbeff = sb("gbeff", [128, B * D], F32, esF)
    psF_ = ExitStack()
    qwdT = sb("qwdT", [128, IT * D], BF16, esF)
    hpf = sb("hpf", [128, 4 * 512], F32, esF)
    hqf = sb("hqf", [128, 2 * IT * 512], BF16, esF)
    x1rt = sb("x1rt", [128, 2 * D], BF16, esF)
    outt = sb("outt", [128, 2 * D], F32, esF)
    tqf = sb("tqf", [128, 4 * 512], F32, esF)
    pd_ = [psF_.enter_context(nc.psum_tensor(f"pd{i}", [128, 512], F32)) for i in range(4)]
    pbc3 = psF_.enter_context(nc.psum_tensor("pbc3", [128, 512], F32))
    sdrow = sb("sdrow", [1, D], F32, esE)
    gerow = sb("gerow", [1, D], F32, esE)
    ev = EV()
    for c in range(IT):
        dma(ev, "gpsimd", SG, qwdT[:, c * D:(c + 1) * D], qwdT_dram.ap()[c * 128:(c + 1) * 128, :])
    dw_loaded = SG.n
    dma(ev, "sync", SS, pmax_dram.ap()[0:1, :].rearrange("a b -> (a b)"), mq_cols[:, 2:3])
    dma(ev, "sync", SS, agrow[0:1, 0:128], pmax_dram.ap()[0:1, :], waits=[(SS, None)])
    ev.add("vector", lambda e: e.reduce_max(srow[0:1, 0:1], agrow[0:1, 0:128], axis=AX),
           waits=[(SS, None)], inc=SV)
    ev.add("vector", lambda e: e.memset(srow[0:1, 1:8], 0.0), inc=SV)
    dma(ev, "sync", SS, ag2_in.ap(), srow[0:1, 0:8], waits=[(SV, None)])
    ev.add("gpsimd", lambda e: e.collective_compute(
        "AllGather", OP.bypass, replica_groups=[list(range(NCORES))],
        ins=[ag2_in.ap().opt()], outs=[ag2_out.ap().opt()]),
        waits=[(SS, None)], inc=SCC)
    dma(ev, "sync", SS, agrow[0:1, 0:8 * NCORES],
        ag2_out.ap().rearrange("a b -> (a b)").rearrange("(a b) -> a b", a=1), waits=[(SCC, None)])
    ev.add("vector", lambda e: e.reduce_max(srow[0:1, 1:2], agrow[0:1, 0:8 * NCORES], axis=AX),
           waits=[(SS, None)], inc=SV)
    ev.add("scalar", lambda e: e.activation(srow[0:1, 2:3], clip_sb[0:1, 1:2], AF.Sigmoid),
           waits=[(SV, None)], inc=SC)
    ev.add("vector", lambda e: e.tensor_mul(srow[0:1, 3:4], srow[0:1, 1:2], srow[0:1, 2:3]),
           waits=[(SC, None)], inc=SV)
    ev.add("vector", lambda e: e.tensor_scalar_mul(srow[0:1, 3:4], srow[0:1, 3:4], 1.0 / QMAX), inc=SV)
    ev.add("vector", lambda e: e.reciprocal(srow[0:1, 4:5], srow[0:1, 3:4]), inc=SV)
    ev.add("tensor", lambda e: e.matmul(pbc3[:, 0:1], onesr[0:1, 0:128], srow[0:1, 4:5],
                                        start=True, stop=True), waits=[(SV, None)], inc=SP)
    ev.add("vector", lambda e: e.tensor_copy(mq_cols[:, 3:4], pbc3[:, 0:1]), waits=[(SP, None)], inc=SV)
    ev.add("vector", lambda e: e.tensor_scalar(q2s_cols[:, 0:IT], su_cols[:, 0:IT],
                                               sx_col[:, 0:1], None, OP.mult), inc=SV)
    ev.add("vector", lambda e: e.tensor_scalar(q2s_cols[:, 0:IT], q2s_cols[:, 0:IT],
                                               mq_cols[:, 3:4], None, OP.mult), inc=SV)
    q2s_done = SV.n
    dma(ev, "sync", SS, sdrow[0:1, 0:D],
        sd_dram.ap().rearrange("c p -> (c p)").rearrange("(a b) -> a b", a=1))
    for b in range(B):
        dma(ev, "sync", SS, rowbuf[0:1, 0:D],
            mod_dram.ap()[b, 5 * D:6 * D].rearrange("(a c) -> a c", a=1), waits=[(SV, None)])
        ev.add("vector", lambda e: e.tensor_mul(gerow[0:1, 0:D], rowbuf[0:1, 0:D], sdrow[0:1, 0:D]),
               waits=[(SS, None), (SP, None)], inc=SV)
        ev.add("vector", lambda e: e.tensor_scalar(gerow[0:1, 0:D], gerow[0:1, 0:D],
                                                   srow[0:1, 3:4], None, OP.mult), inc=SV)
        for j in range(D // JW):
            ev.add("tensor", lambda e, j=j: e.matmul(
                pbc3[:, 0:JW], onesr[0:1, 0:128], gerow[0:1, j * JW:(j + 1) * JW],
                start=True, stop=True), waits=[(SV, None)], inc=SP)
            ev.add("vector", lambda e, b=b, j=j: e.tensor_copy(
                gbeff[:, b * D + j * JW: b * D + (j + 1) * JW], pbc3[:, 0:JW]),
                waits=[(SP, None)], inc=SV)
    f_marks = {}

    def f_quant(t5):
        r0 = t5 * 512
        sl = (t5 % 2) * IT * 512
        for it in range(IT):
            g = t5 * IT + it
            buf = g % 4
            w = [(SC, f_marks.get(('fc', g - 4), 0))]
            dma(ev, "gpsimd", SG, hpf[:, buf * 512:(buf + 1) * 512],
                hpre_dram.ap()[it * 128:(it + 1) * 128, r0:r0 + 512], waits=w)
            wv = [(SV, f_marks.get(('fv', g - 4), 0))]
            ev.add("scalar", lambda e, it=it, buf=buf: e.activation(
                tqf[:, buf * 512:(buf + 1) * 512], hpf[:, buf * 512:(buf + 1) * 512], AF.Identity,
                bias=konst[:, 0:1], scale=q2s_cols[:, it:it + 1]),
                waits=[(SG, None), (SV, q2s_done)] + wv, inc=SC)
            f_marks[('fc', g)] = SC.n
            ev.add("vector", lambda e, buf=buf: e.tensor_scalar(
                tqf[:, buf * 512:(buf + 1) * 512], tqf[:, buf * 512:(buf + 1) * 512],
                CM - 128.0, CM + 127.0, OP.max, OP.min), waits=[(SC, None)], inc=SV)
            ev.add("vector", lambda e, it=it, buf=buf, sl=sl: e.tensor_scalar(
                hqf[:, sl + it * 512: sl + (it + 1) * 512], tqf[:, buf * 512:(buf + 1) * 512],
                CM, None, OP.subtract),
                waits=[(SP, f_marks[(t5 - 2, 'p')])] if (t5 >= 2 and it < 2) else [], inc=SV)
            f_marks[('fv', g)] = SV.n
        f_marks[(t5, 'hq')] = SV.n

    def f_mms(t5):
        r0 = t5 * 512
        b = r0 // S
        sl = (t5 % 2) * IT * 512
        for tq in range(4):
            bufx = tq % 2
            wx = [(SV, f_marks[(t5, tq - 2, 'a')])] if (tq >= 2) else (
                [(SV, f_marks[(t5 - 1, tq + 2, 'a')])] if t5 >= 1 else [])
            dma(ev, "gpsimd", SG, x1rt[:, bufx * D:(bufx + 1) * D],
                x1_dram.ap()[r0 + tq * 128: r0 + (tq + 1) * 128, :], waits=wx)
            for jh in range(JH):
                gj = (t5 * 4 + tq) * JH + jh
                po = pd_[gj % 4]
                for it in range(IT):
                    w2 = []
                    if it == 0:
                        w2 = [(SV, f_marks[(t5, 'hq')]), (SG, dw_loaded),
                              (SV, f_marks.get(('fm', gj - 4), 0))]
                    ev.add("tensor", lambda e, tq=tq, jh=jh, it=it, po=po, sl=sl: e.matmul(
                        po[:, 0:JW], hqf[:, sl + it * 512 + tq * 128: sl + it * 512 + (tq + 1) * 128],
                        qwdT[:, it * D + jh * JW: it * D + (jh + 1) * JW],
                        start=(it == 0), stop=(it == IT - 1)),
                        waits=w2, inc=(SP if it == IT - 1 else None))
                ev.add("vector", lambda e, jh=jh, po=po, b=b: e.tensor_mul(
                    outt[:, jh * JW:(jh + 1) * JW], po[:, 0:JW],
                    gbeff[:, b * D + jh * JW: b * D + (jh + 1) * JW]),
                    waits=[(SP, None), (SST, None)], inc=SV)
                ev.add("vector", lambda e, jh=jh, bufx=bufx: e.tensor_add(
                    outt[:, D + jh * JW: D + (jh + 1) * JW], outt[:, jh * JW:(jh + 1) * JW],
                    x1rt[:, bufx * D + jh * JW: bufx * D + (jh + 1) * JW]),
                    waits=[(SG, None)], inc=SV)
                f_marks[('fm', gj)] = SV.n
            f_marks[(t5, tq, 'a')] = SV.n
            ev.add_dma("sync", SST, lambda e, b=b, r0=r0, tq=tq: e.dma_start(
                out=outd.ap()[b, (r0 % S) + tq * 128: (r0 % S) + (tq + 1) * 128, :],
                in_=outt[:, D:2 * D]), waits=[(SV, None)])
        f_marks[(t5, 'p')] = SP.n

    for t5 in range(NT512 + 1):
        if t5 < NT512:
            f_quant(t5)
        if t5 >= 1:
            f_mms(t5 - 1)
    ev.add("sync", lambda e: e.nop(), waits=[(SST, None)])
    run_phase(ev, psF_)
    esE.close()
    esF.close()
    esg.close()
    return nc


def _host_consts(cfg: Cfg):
    S = cfg.S
    t = np.arange(S, dtype=np.float64)
    fdn = np.exp(-LAM * t).astype(np.float32)
    fup = np.exp(LAM * t).astype(np.float32)
    isq = 1.0 / math.sqrt(cfg.D)
    gdn = (np.exp(LAM * t) * isq).astype(np.float32)
    gup = (np.exp(-LAM * t) * isq).astype(np.float32)
    rep = lambda v: np.broadcast_to(v[None, :], (128, S)).copy()
    p = np.arange(128.0)[:, None]
    f = np.arange(128.0)[None, :]
    rdiag = np.exp(-2 * LAM * np.maximum(0.0, p - f)).astype(np.float32)
    konst = np.zeros((128, 8), np.float32)
    konst[:, 0] = CM
    konst[:, 1] = EPS
    konst[:, 2] = 1.0
    return {
        "idm": np.eye(128, dtype=np.float32),
        "fdn": rep(fdn), "fup": rep(fup), "gdn": rep(gdn), "gup": rep(gup),
        "rdiag": rdiag,
        "onesr": np.ones((1, 128), np.float32),
        "konst": konst,
    }


_NC_CACHE = {}


def kernel(L0_attn_x, L0_attn_W_q, L0_attn_W_k, L0_attn_W_v, L0_attn_W_o,
           L0_attn_W_down_kv, L0_attn_W_up_k, L0_attn_W_up_v,
           L0_ffn_clip_x, L0_ffn_clip_down, L0_ffn_w_gate, L0_ffn_w_up,
           L0_ffn_w_down, L0_c, L0_wmod):
    cfg = Cfg()
    if "main" not in _NC_CACHE:
        _NC_CACHE["main"] = build_program(cfg)
    nc = _NC_CACHE["main"]
    x = np.ascontiguousarray(np.asarray(L0_attn_x, dtype=np.float32))
    shared = {
        "wq": np.ascontiguousarray(L0_attn_W_q, dtype=np.float32),
        "wk": np.ascontiguousarray(L0_attn_W_k, dtype=np.float32),
        "wv": np.ascontiguousarray(L0_attn_W_v, dtype=np.float32),
        "wo": np.ascontiguousarray(L0_attn_W_o, dtype=np.float32),
        "wdn": np.ascontiguousarray(L0_attn_W_down_kv, dtype=np.float32),
        "wuk": np.ascontiguousarray(L0_attn_W_up_k, dtype=np.float32),
        "wuv": np.ascontiguousarray(L0_attn_W_up_v, dtype=np.float32),
        "wmod": np.ascontiguousarray(L0_wmod, dtype=np.float32),
        "wg": np.ascontiguousarray(L0_ffn_w_gate, dtype=np.float32),
        "wu": np.ascontiguousarray(L0_ffn_w_up, dtype=np.float32),
        "wdw": np.ascontiguousarray(L0_ffn_w_down, dtype=np.float32),
        "clip2": np.stack([np.asarray(L0_ffn_clip_x, np.float32).reshape(1),
                           np.asarray(L0_ffn_clip_down, np.float32).reshape(1)],
                          axis=1).reshape(1, 2),
        **_host_consts(cfg),
    }
    cvals = np.ascontiguousarray(L0_c, dtype=np.float32)
    in_maps = []
    for i in range(cfg.n_cores):
        m = dict(shared)
        m["x"] = x[2 * i:2 * i + 2]
        m["cmod"] = cvals[2 * i:2 * i + 2]
        in_maps.append(m)
    res = run_bass_kernel_spmd(nc, in_maps, core_ids=list(range(cfg.n_cores)))
    out = np.concatenate([res.results[i]["out"] for i in range(cfg.n_cores)], axis=0)
    return out.astype(np.float32)


# version 28
# speedup vs baseline: 1.0252x; 1.0252x over previous
import math
import numpy as np
import ml_dtypes
from contextlib import ExitStack

import concourse.bass as bass
import concourse.mybir as mybir
from concourse.bass_utils import run_bass_kernel_spmd

F32 = mybir.dt.float32
BF16 = mybir.dt.bfloat16
AF = mybir.ActivationFunctionType
OP = mybir.AluOpType
AX = mybir.AxisListType.X

CM = 12582912.0
LAM = 0.01
EPS = 1e-5
QMAX = 127.0

ENGINES = ("tensor", "vector", "scalar", "gpsimd", "sync")


class Cfg:
    def __init__(self, B=2, S=2048, D=1024, H=16, HD=64, DL=256, I=2816, n_cores=8):
        self.B, self.S, self.D, self.H, self.HD, self.DL, self.I = B, S, D, H, HD, DL, I
        self.n_cores = n_cores
        self.QT = min(512, S)
        self.DC = D // 128
        self.LC = DL // 128
        self.IT = I // 128
        self.NKT = S // 128
        self.NQT = S // self.QT
        self.NTT = S // 128
        self.PAIRS = H // 2
        self.DMOD = 6 * D
        self.TOK = B * S
        self.NT512 = self.TOK // 512
        self.JW = min(512, D)
        self.JH = D // self.JW
        self.NQSUB = self.QT // 128
        assert S % 128 == 0 and D % 128 == 0 and DL % 128 == 0 and I % 128 == 0
        assert H % 2 == 0 and HD == 64 and S % self.QT == 0 and D % self.JW == 0


class Sem:
    def __init__(self, nc, name):
        self.h = nc.alloc_semaphore(name)
        self.n = 0
        self.emitted = 0
        self.name = name


class QueueCtr:
    R = 4

    def __init__(self, nc, name):
        self.name = name
        self.sems = [Sem(nc, f"{name}{r}") for r in range(self.R)]
        self.n = 0

    def expand(self, ndmas):
        w = []
        for r in range(self.R):
            nr = (ndmas - r + self.R - 1) // self.R if ndmas > r else 0
            if nr > 0:
                w.append((self.sems[r], 16 * nr))
        return w


class EV:
    def __init__(self):
        self.events = []

    def add(self, eng, fn, waits=(), inc=None, k=1):
        flat = []
        for (s, v) in waits:
            v = v if v is not None else s.n
            if isinstance(s, QueueCtr):
                flat.extend(s.expand(v))
            else:
                flat.append((s, v))
        if inc is not None:
            inc.n += k
        self.events.append((eng, tuple(flat), fn, inc, k))

    def add_dma(self, eng, qc, fn, waits=()):
        j = qc.n
        r = j % QueueCtr.R
        s = qc.sems[r]
        self.add(eng, fn, waits=tuple(waits) + ((s, 16 * (j // QueueCtr.R)),),
                 inc=s, k=16)
        qc.n += 1

    SELF_SERIAL = ("vector", "scalar")

    @staticmethod
    def _val_after(sem, k):
        sem.emitted += k
        return sem.emitted

    def emit(self, eng_name, engine):
        hw = {}
        last_inc = None
        for (e, waits, fn, inc, k) in self.events:
            if e != eng_name:
                continue
            allw = list(waits)
            if eng_name in self.SELF_SERIAL and last_inc is not None:
                allw.append(last_inc)
            for (s, v) in allw:
                if v > 0 and hw.get(s.name, -1) < v:
                    engine.wait_ge(s.h, v)
                    hw[s.name] = v
            instr = fn(engine)
            if inc is not None:
                instr.then_inc(inc.h, k)
                if eng_name in self.SELF_SERIAL:
                    last_inc = (inc, self._val_after(inc, k))


def build_program(cfg: Cfg):
    nc = bass.Bass('TRN2', target_bir_lowering=False)
    B, S, D, H, HD, DL, I = cfg.B, cfg.S, cfg.D, cfg.H, cfg.HD, cfg.DL, cfg.I
    QT, DC, LC, IT = cfg.QT, cfg.DC, cfg.LC, cfg.IT
    NKT, NQT, NTT, PAIRS = cfg.NKT, cfg.NQT, cfg.NTT, cfg.PAIRS
    TOK, NT512, JH, JW, NQSUB = cfg.TOK, cfg.NT512, cfg.JH, cfg.JW, cfg.NQSUB
    DMOD = cfg.DMOD
    NCORES = cfg.n_cores
    PW = 193
    KW = PAIRS * PW

    xd = nc.dram_tensor("x", [B, S, D], F32, kind="ExternalInput")
    wq_d = nc.dram_tensor("wq", [D, D], F32, kind="ExternalInput")
    wk_d = nc.dram_tensor("wk", [D, D], F32, kind="ExternalInput")
    wv_d = nc.dram_tensor("wv", [D, D], F32, kind="ExternalInput")
    wo_d = nc.dram_tensor("wo", [D, D], F32, kind="ExternalInput")
    wdn_d = nc.dram_tensor("wdn", [D, DL], F32, kind="ExternalInput")
    wuk_d = nc.dram_tensor("wuk", [DL, D], F32, kind="ExternalInput")
    wuv_d = nc.dram_tensor("wuv", [DL, D], F32, kind="ExternalInput")
    wmod_d = nc.dram_tensor("wmod", [D, DMOD], F32, kind="ExternalInput")
    cmod_d = nc.dram_tensor("cmod", [B, D], F32, kind="ExternalInput")
    clip_d = nc.dram_tensor("clip2", [1, 2], F32, kind="ExternalInput")
    wg_d = nc.dram_tensor("wg", [I, D], F32, kind="ExternalInput")
    wu_d = nc.dram_tensor("wu", [I, D], F32, kind="ExternalInput")
    wdw_d = nc.dram_tensor("wdw", [D, I], F32, kind="ExternalInput")
    id_d = nc.dram_tensor("idm", [128, 128], F32, kind="ExternalInput")
    fdn_d = nc.dram_tensor("fdn", [128, S], F32, kind="ExternalInput")
    fup_d = nc.dram_tensor("fup", [128, S], F32, kind="ExternalInput")
    gdn_d = nc.dram_tensor("gdn", [128, S], F32, kind="ExternalInput")
    gup_d = nc.dram_tensor("gup", [128, S], F32, kind="ExternalInput")
    rdiag_d = nc.dram_tensor("rdiag", [128, 128], F32, kind="ExternalInput")
    ones_d = nc.dram_tensor("onesr", [1, 128], F32, kind="ExternalInput")
    konst_d = nc.dram_tensor("konst", [128, 8], F32, kind="ExternalInput")
    outd = nc.dram_tensor("out", [B, S, D], F32, kind="ExternalOutput")

    x1_dram = nc.dram_tensor("x1_scr", [TOK, D], BF16)
    h2_dram = nc.dram_tensor("h2_scr", [TOK, D], F32)
    hpre_dram = nc.dram_tensor("hpre_scr", [I, TOK], F32)
    qwgT_dram = nc.dram_tensor("qwgT_scr", [D, I], BF16)
    qwuT_dram = nc.dram_tensor("qwuT_scr", [D, I], BF16)
    qwdT_dram = nc.dram_tensor("qwdT_scr", [I, D], BF16)
    silc_dram = nc.dram_tensor("silc_scr", [B, D], BF16)
    mod_dram = nc.dram_tensor("mod_scr", [B, DMOD], F32)
    qdn_dram = nc.dram_tensor("qdn_scr", [DC * 128, S], BF16)
    qup_dram = nc.dram_tensor("qup_scr", [DC * 128, S], BF16)
    kdn_dram = nc.dram_tensor("kdn_scr", [DC * 128, S], BF16)
    kup_dram = nc.dram_tensor("kup_scr", [DC * 128, S], BF16)
    vaug_dram = nc.dram_tensor("vaug_scr", [NKT * 128, H * HD], BF16)
    pmax_dram = nc.dram_tensor("pmax_scr", [4, 128], F32)
    lrow_dram = nc.dram_tensor("lrow_scr", [2 * PAIRS, QT], F32)
    rrow_dram = nc.dram_tensor("rrow_scr", [2 * PAIRS, QT], F32)
    sd_dram = nc.dram_tensor("sd_scr", [DC, 128], F32)
    ag1_in = nc.dram_tensor("ag1_in", [1, 8], F32)
    ag1_out = nc.dram_tensor("ag1_out", [NCORES, 8], F32)
    ag2_in = nc.dram_tensor("ag2_in", [1, 8], F32)
    ag2_out = nc.dram_tensor("ag2_out", [NCORES, 8], F32)

    esg = ExitStack()

    def sb(name, shape, dt, stack=None):
        return (stack or esg).enter_context(nc.sbuf_tensor(name, shape, dt))

    konst = sb("konstt", [128, 8], F32)
    idmf = sb("idmf", [128, 128], F32)
    onesr = sb("onesrt", [1, 128], F32)
    idbf = sb("idbft", [128, 128], BF16)
    wqs = sb("wqs", [128, DC * D], BF16)
    wcomb = sb("wcomb", [128, DC * DL], BF16)
    wuks = sb("wuks", [128, LC * D], BF16)
    wuvs = sb("wuvs", [128, LC * D], BF16)
    wos = sb("wos", [128, DC * D], BF16)
    fdn = sb("fdns", [128, S], BF16)
    fup = sb("fups", [128, S], BF16)
    gdn = sb("gdns", [128, S], BF16)
    gup = sb("gups", [128, S], BF16)
    rdiag = sb("rdiagt", [128, 128], F32)
    sg_cols = sb("sg_cols", [128, IT], F32)
    su_cols = sb("su_cols", [128, IT], F32)
    sd_cols = sb("sd_cols", [128, DC], F32)
    sgx_cols = sb("sgx_cols", [128, IT], F32)
    q2s_cols = sb("q2s_cols", [128, IT], F32)
    negM = sb("negM", [128, 1], F32)
    sx_col = sb("sx_col", [128, 1], F32)
    sxr_col = sb("sxr_col", [128, 1], F32)
    srow = sb("srow", [1, 16], F32)
    rowbuf = sb("rowbuf", [1, max(512, D, 2 * QT)], F32)
    clip_sb = sb("clip_sb", [1, 2], F32)
    gmax_cols = sb("gmax_cols", [128, 2], F32)
    mq_cols = sb("mq_cols", [128, 4], F32)
    hmax_cols = sb("hmax_cols", [128, IT], F32)
    agrow = sb("agrow", [1, max(128, 8 * NCORES)], F32)

    SG = QueueCtr(nc, "Qg")
    SS = QueueCtr(nc, "Qs")
    ST = QueueCtr(nc, "Qt")
    SP, SV, SC, SCC = (Sem(nc, "S" + n) for n in ["p", "v", "c", "cc"])
    SST = SS
    allsems = [SP, SV, SC, SCC] + SG.sems + SS.sems + ST.sems

    def run_phase(ev, psum_stack=None):
        snap = [(s, s.n) for s in allsems if s.n > 0]
        for e in ENGINES:
            ev.add(e, lambda eng: eng.nop(), waits=snap)
        with nc.Block() as blk:
            blk.tensor(lambda e: ev.emit("tensor", e))
            blk.vector(lambda e: ev.emit("vector", e))
            blk.scalar(lambda e: ev.emit("scalar", e))
            blk.gpsimd(lambda e: ev.emit("gpsimd", e))
            blk.sync(lambda e: ev.emit("sync", e))
        if psum_stack is not None:
            psum_stack.close()

    def dma(ev, eng, qc, out, in_, waits=()):
        ev.add_dma(eng, qc, lambda e, o=out, i=in_: e.dma_start(out=o, in_=i), waits=waits)

    def dma_t(ev, eng, qc, out, in_, waits=()):
        ev.add_dma(eng, qc, lambda e, o=out, i=in_: e.dma_start_transpose(o, i), waits=waits)

    esA = ExitStack()
    psA_ = ExitStack()
    wmj = sb("wmj", [128, 2 * DC * 512], BF16, esA)
    cmod_sb = sb("cmod_sb", [B, D], F32, esA)
    silc = sb("silc", [B, D], BF16, esA)
    silcT = sb("silcT", [128, DC * B], BF16, esA)
    grow_t = sb("grow_t", [1, 3 * D], F32, esA)
    wvtmp = sb("wvtmp", [128, DC * D], BF16, esA)
    wkv = sb("wkv", [128, DC * D], BF16, esA)
    wkvT = sb("wkvT", [128, DC * D], BF16, esA)
    wdnb = sb("wdnb", [128, DC * DL], BF16, esA)

    pmod = psA_.enter_context(nc.psum_tensor("pmod", [128, 512], F32))
    ptr = [psA_.enter_context(nc.psum_tensor(f"ptr{i}", [128, 128], BF16)) for i in range(4)]
    pcmb = psA_.enter_context(nc.psum_tensor("pcmb", [128, 512], F32))

    ev = EV()
    for (dst, src) in ((konst, konst_d), (onesr, ones_d), (rdiag, rdiag_d), (clip_sb, clip_d)):
        dma(ev, "sync", SS, dst[:, :], src.ap())
    dma(ev, "sync", SS, cmod_sb[:, :], cmod_d.ap())
    s_basic = SS.n
    dma(ev, "gpsimd", SG, idbf[:, :], id_d.ap())
    dma(ev, "gpsimd", SG, idmf[:, :], id_d.ap())
    for (dst, src) in ((fdn, fdn_d), (fup, fup_d), (gdn, gdn_d), (gup, gup_d)):
        dma(ev, "gpsimd", SG, dst[:, :], src.ap())
    for c in range(DC):
        dma(ev, "gpsimd", SG, wqs[:, c * D:(c + 1) * D], wq_d.ap()[c * 128:(c + 1) * 128, :])
        dma(ev, "gpsimd", SG, wkv[:, c * D:(c + 1) * D], wk_d.ap()[c * 128:(c + 1) * 128, :])
        dma(ev, "gpsimd", SG, wvtmp[:, c * D:(c + 1) * D], wv_d.ap()[c * 128:(c + 1) * 128, :])
        dma(ev, "gpsimd", SG, wdnb[:, c * DL:(c + 1) * DL], wdn_d.ap()[c * 128:(c + 1) * 128, :])
        dma(ev, "gpsimd", SG, wos[:, c * D:(c + 1) * D], wo_d.ap()[c * 128:(c + 1) * 128, :])
    for c in range(LC):
        dma(ev, "gpsimd", SG, wuks[:, c * D:(c + 1) * D], wuk_d.ap()[c * 128:(c + 1) * 128, :])
        dma(ev, "gpsimd", SG, wuvs[:, c * D:(c + 1) * D], wuv_d.ap()[c * 128:(c + 1) * 128, :])
    g_loads = SG.n

    ev.add("scalar", lambda e: e.activation(silc[:, :], cmod_sb[:, :], AF.Silu),
           waits=[(SS, s_basic)], inc=SC)
    dma(ev, "sync", SS, silc_dram.ap(), silc[:, :], waits=[(SC, None)])
    for c in range(DC):
        for b in range(B):
            dma(ev, "sync", SS, silcT[:, c * B + b:c * B + b + 1],
                silc_dram.ap()[b, c * 128:(c + 1) * 128].rearrange("(p a) -> p a", a=1),
                waits=[(SS, None)])
    silcT_ld = SS.n
    for c in range(DC):
        ev.add("vector", lambda e, c=c: e.tensor_add(
            wkv[:, c * D:(c + 1) * D], wkv[:, c * D:(c + 1) * D], wvtmp[:, c * D:(c + 1) * D]),
            waits=[(SG, g_loads)], inc=SV)
    wkv_done_v = SV.n
    NJ = DMOD // 512
    wm_marks = {}
    for j in range(NJ):
        slot = j % 2
        w = [(SP, wm_marks[j - 2])] if j >= 2 else []
        for c in range(DC):
            dma(ev, "gpsimd", SG, wmj[:, slot * DC * 512 + c * 512: slot * DC * 512 + (c + 1) * 512],
                wmod_d.ap()[c * 128:(c + 1) * 128, j * 512:(j + 1) * 512], waits=w if c == 0 else [])
        for b in range(B):
            for c in range(DC):
                ev.add("tensor", lambda e, b=b, j=j, c=c, slot=slot: e.matmul(
                    pmod[0:1, 0:512], silcT[:, c * B + b: c * B + b + 1],
                    wmj[:, slot * DC * 512 + c * 512: slot * DC * 512 + (c + 1) * 512],
                    start=(c == 0), stop=(c == DC - 1)),
                    waits=([(SS, silcT_ld), (SG, None), (SV, None)] if c == 0 else []),
                    inc=(SP if c == DC - 1 else None))
            ev.add("vector", lambda e: e.tensor_copy(rowbuf[0:1, 0:512], pmod[0:1, 0:512]),
                   waits=[(SP, None), (SS, None)], inc=SV)
            dma(ev, "sync", SS, mod_dram.ap()[b, j * 512:(j + 1) * 512].rearrange("(a c) -> a c", a=1),
                rowbuf[0:1, 0:512], waits=[(SV, None)])
        wm_marks[j] = SP.n
    mod_rows_done = SS.n
    for b in range(B):
        for gofs in (2 * D, 5 * D):
            dma(ev, "sync", SS, grow_t[0:1, 2 * D:3 * D],
                mod_dram.ap()[b, gofs:gofs + D].rearrange("(a c) -> a c", a=1),
                waits=[(SS, mod_rows_done), (SV, None)])
            src_row = grow_t[0:1, 2 * D:3 * D]
            ev.add("vector", lambda e, src_row=src_row: e.tensor_scalar_max(
                grow_t[0:1, 0:D], src_row, 0.25), waits=[(SS, None)], inc=SV)
            ev.add("vector", lambda e, src_row=src_row: e.tensor_scalar_min(
                grow_t[0:1, D:2 * D], src_row, -0.25), inc=SV)
            ev.add("vector", lambda e, src_row=src_row: e.tensor_single_scalar(
                grow_t[0:1, 2 * D:3 * D], src_row, 0.0, OP.is_ge), inc=SV)
            ev.add("vector", lambda e: e.tensor_sub(grow_t[0:1, 0:D], grow_t[0:1, 0:D], grow_t[0:1, D:2 * D]), inc=SV)
            ev.add("vector", lambda e: e.tensor_mul(grow_t[0:1, 0:D], grow_t[0:1, 0:D], grow_t[0:1, 2 * D:3 * D]), inc=SV)
            ev.add("vector", lambda e: e.tensor_add(grow_t[0:1, 0:D], grow_t[0:1, D:2 * D], grow_t[0:1, 0:D]), inc=SV)
            dma(ev, "sync", SS, mod_dram.ap()[b, gofs:gofs + D].rearrange("(a c) -> a c", a=1),
                grow_t[0:1, 0:D], waits=[(SV, None)])
    for c in range(DC):
        for cc in range(DC):
            idx = c * DC + cc
            pt = ptr[idx % 4]
            ev.add("tensor", lambda e, c=c, cc=cc, pt=pt: e.transpose(
                pt[:, :], wkv[:, c * D + cc * 128: c * D + (cc + 1) * 128], idbf[:, :]),
                waits=[(SV, wkv_done_v), (SG, None), (SV, None)], inc=SP)
            ev.add("vector", lambda e, c=c, cc=cc, pt=pt: e.tensor_copy(
                wkvT[:, cc * D + c * 128: cc * D + (c + 1) * 128], pt[:, :]),
                waits=[(SP, None)], inc=SV)
    for dt in range(DC):
        for ic in range(DC):
            ev.add("tensor", lambda e, dt=dt, ic=ic: e.matmul(
                pcmb[:, 0:DL], wkvT[:, ic * D + dt * 128: ic * D + (dt + 1) * 128],
                wdnb[:, ic * DL:(ic + 1) * DL], start=(ic == 0), stop=(ic == DC - 1)),
                waits=[(SV, None)] if ic == 0 else [], inc=(SP if ic == DC - 1 else None))
        ev.add("vector", lambda e, dt=dt: e.tensor_copy(wcomb[:, dt * DL:(dt + 1) * DL], pcmb[:, 0:DL]),
               waits=[(SP, None)], inc=SV)
    run_phase(ev)
    esA.close()
    psA_.close()

    esA2 = ExitStack()
    psA2_ = ExitStack()
    WMAXF = max(D, I)
    wfp = sb("wfp", [128, 3 * WMAXF], F32, esA2)
    tq_st = sb("tq_st", [128, 2 * WMAXF], F32, esA2)
    wint = sb("wint", [128, WMAXF], BF16, esA2)
    qT_stage = sb("qT_stage", [128, DC * I], BF16, esA2)
    acols = sb("acols", [128, 4], F32, esA2)
    ptr2 = [psA2_.enter_context(nc.psum_tensor(f"pt2_{i}", [128, 1024], BF16)) for i in range(2)]

    ev = EV()
    qtiles = []

    def quant_weight(src_d, nrows, ncols, scol, qT_dram):
        ntiles = nrows // 128
        for it in range(ntiles):
            qtiles.append((src_d, it, nrows, ncols, scol, qT_dram,
                           it == 0, it == ntiles - 1))

    def emit_quant_all():
        qgrp = [0]
        gev = {}
        gtp = {}
        id_mark = {}
        vq_mark = {}

        def head(gt):
            (src_d, it, nrows, ncols, scol, qT_dram, first, last) = qtiles[gt]
            buf = gt % 3
            par = (gt % 2) * 2
            dma(ev, "gpsimd", SG, wfp[:, buf * WMAXF: buf * WMAXF + ncols],
                src_d.ap()[it * 128:(it + 1) * 128, :],
                waits=[(SP, gtp[gt - 3])] if gt >= 3 else [])
            ev.add("vector", lambda e, buf=buf, ncols=ncols, par=par: e.reduce_max(
                acols[:, par:par + 1], wfp[:, buf * WMAXF: buf * WMAXF + ncols], axis=AX,
                apply_absolute_value=True),
                waits=[(SG, None), (SC, id_mark.get(gt - 2, 0))], inc=SV)
            ev.add("vector", lambda e, par=par: e.reciprocal(
                acols[:, par + 1:par + 2], acols[:, par:par + 1]), inc=SV)
            ev.add("vector", lambda e, par=par: e.tensor_scalar_mul(
                acols[:, par + 1:par + 2], acols[:, par + 1:par + 2], QMAX), inc=SV)
            ev.add("vector", lambda e, scol=scol, it=it, par=par: e.tensor_scalar_mul(
                scol[:, it:it + 1], acols[:, par:par + 1], 1.0 / QMAX), inc=SV)
            ev.add("scalar", lambda e, buf=buf, ncols=ncols, par=par, gt=gt: e.activation(
                tq_st[:, (gt % 2) * WMAXF:(gt % 2) * WMAXF + ncols],
                wfp[:, buf * WMAXF: buf * WMAXF + ncols], AF.Identity,
                bias=konst[:, 0:1], scale=acols[:, par + 1:par + 2]),
                waits=[(SV, None)], inc=SC)
            id_mark[gt] = SC.n

        def tail(gt):
            (src_d, it, nrows, ncols, scol, qT_dram, first, last) = qtiles[gt]
            nblk = ncols // 128
            ev.add("vector", lambda e, ncols=ncols, gt=gt: e.tensor_scalar(
                wint[:, 0:ncols], tq_st[:, (gt % 2) * WMAXF:(gt % 2) * WMAXF + ncols],
                CM, None, OP.subtract),
                waits=[(SC, id_mark[gt]), (SP, gtp.get(gt - 1, 0))], inc=SV)
            vq = SV.n
            done = 0
            while done < nblk:
                g = qgrp[0]
                qgrp[0] += 1
                nb_ = min(8, nblk - done)
                pt = ptr2[g % 2]
                for j in range(nb_):
                    bkt = done + j
                    ev.add("tensor", lambda e, bkt=bkt, j=j, pt=pt: e.transpose(
                        pt[:, j * 128:(j + 1) * 128], wint[:, bkt * 128:(bkt + 1) * 128], idbf[:, :]),
                        waits=([(SV, vq), (SV, gev.get(g - 2, 0))] if j == 0 else []),
                        inc=(SP if j == nb_ - 1 else None))
                dstap = bass.AP(qT_stage[:, 0:1].tensor, done * nrows + it * 128,
                                [[DC * I, 128], [nrows, nb_], [1, 128]])
                evw2 = [(SP, None)] + ([(SS, None)] if (first and done == 0) else [])
                ev.add("vector", lambda e, dstap=dstap, pt=pt, nb_=nb_: e.tensor_copy(
                    dstap, pt[:, 0:nb_ * 128]), waits=evw2, inc=SV)
                gev[g] = SV.n
                done += nb_
            gtp[gt] = SP.n
            if last:
                for bkt in range(nblk):
                    dma(ev, "sync", SS, qT_dram.ap()[bkt * 128:(bkt + 1) * 128, :],
                        qT_stage[:, bkt * nrows:(bkt + 1) * nrows], waits=[(SV, None)])

        for gt in range(len(qtiles)):
            head(gt)
            if gt >= 1:
                tail(gt - 1)
        tail(len(qtiles) - 1)

    quant_weight(wg_d, I, D, sg_cols, qwgT_dram)
    quant_weight(wu_d, I, D, su_cols, qwuT_dram)
    quant_weight(wdw_d, D, I, sd_cols, qwdT_dram)
    emit_quant_all()
    for c in range(DC):
        dma(ev, "sync", SS, sd_dram.ap()[c, :].rearrange("(p a) -> p a", a=1),
            sd_cols[:, c:c + 1], waits=[(SV, None)])
    run_phase(ev)
    esA2.close()
    psA2_.close()

    esB = ExitStack()
    h1T = sb("h1T", [128, 2 * DC * 512], BF16, esB)
    xbt = sb("xbt", [128, 4 * D], BF16, esB)
    h2f = sb("h2f", [128, 2 * D], F32, esB)
    st_sum = sb("st_sum", [128, NTT], F32, esB)
    st_sq = sb("st_sq", [128, NTT], F32, esB)
    st_tmp = sb("st_tmp", [128, NTT], F32, esB)
    st_negm = sb("st_negm", [128, NTT], F32, esB)
    st_inv = sb("st_inv", [128, NTT], F32, esB)
    car = sb("car", [128, D], BF16, esB)
    sar = sb("sar", [128, D], BF16, esB)
    cbr = sb("cbr", [128, D], BF16, esB)
    sbr = sb("sbr", [128, D], BF16, esB)
    gar = sb("gar", [128, D], BF16, esB)
    qstage = sb("qstage", [128, DC * 512], BF16, esB)
    kstage = sb("kstage", [128, DC * 512], BF16, esB)
    latr = sb("latr", [128, 2 * LC * 512], BF16, esB)
    vstage = sb("vstage", [128, 2 * JW], BF16, esB)
    qdnwin = sb("qdnwin", [128, DC * 512], BF16, esB)
    qupwin = sb("qupwin", [128, DC * 512], BF16, esB)
    VW = NKT * 193
    vwin = sb("vwin", [128, VW], BF16, esB)
    kdn_c = sb("kdn_c", [128, 2 * S], BF16, esB)
    kup_c = sb("kup_c", [128, 2 * S], BF16, esB)
    PAb = sb("PAb", [128, 2 * 2 * QT], BF16, esB)
    PBb = sb("PBb", [128, 2 * 2 * QT], BF16, esB)
    attnraw = sb("attnraw", [128, PAIRS * QT], BF16, esB)
    lrows = sb("lrows", [128, QT], F32, esB)
    sqb = sb("sqb", [128, 2 * 512], BF16, esB)
    lst = None
    rip = lrows
    rbuf = sb("rbuf", [128, 2 * QT], F32, esB)
    lst = rbuf
    x1t = sb("x1t", [128, 2 * D], BF16, esB)
    ot_t = None
    qsq = h2f
    ot_t = h2f[:, D:2 * D]

    psB_ = ExitStack()
    psS = [psB_.enter_context(nc.psum_tensor(f"psS{i}", [128, 1024], F32)) for i in range(3)]
    psS_A, psS_B = psS[0], psS[1]
    pv2 = [psB_.enter_context(nc.psum_tensor(f"pv2_{i}", [128, 512], F32)) for i in range(2)]
    pvA, pvB = pv2[0], pv2[1]
    prc = [psS[2][:, 0:512], psS[2][:, 512:1024]]

    for b in range(B):
        ev = EV()
        brow = b * S

        NB = D // JW
        for (dst, ofs, p1) in ((car, 1 * D, True), (sar, 0 * D, False), (cbr, 4 * D, True),
                               (sbr, 3 * D, False), (gar, 2 * D, False)):
            for j in range(NB):
                dma(ev, "sync", SS, rowbuf[0:1, 0:JW],
                    mod_dram.ap()[b, ofs + j * JW: ofs + (j + 1) * JW].rearrange("(a c) -> a c", a=1),
                    waits=[(SP, None)])
                ev.add("tensor", lambda e: e.matmul(
                    prc[0][:, 0:JW], onesr[0:1, 0:128], rowbuf[0:1, 0:JW],
                    start=True, stop=True), waits=[(SS, None), (SV, None)], inc=SP)
                ev.add("vector", (lambda e, dst=dst, j=j: e.tensor_scalar(
                    dst[:, j * JW:(j + 1) * JW], prc[0][:, 0:JW], 1.0, None, OP.add)) if p1 else
                    (lambda e, dst=dst, j=j: e.tensor_copy(
                        dst[:, j * JW:(j + 1) * JW], prc[0][:, 0:JW])),
                    waits=[(SP, None)], inc=SV)
        ev.add("vector", lambda e: e.memset(
            bass.AP(vwin[:, 0:1].tensor, 64, [[VW, 128], [193, NKT], [1, 2]]), 1.0), inc=SV)
        ev.add("vector", lambda e: e.memset(
            bass.AP(vwin[:, 0:1].tensor, 66, [[VW, 128], [193, NKT], [1, 63]]), 0.0), inc=SV)

        ln_marks = {}
        for tt in range(NTT):
            buf = tt % 4
            w = []
            if tt >= 4:
                w = [(SV, ln_marks[tt - 4][0]), (SC, ln_marks[tt - 4][1])]
            dma(ev, "gpsimd", SG, xbt[:, buf * D:(buf + 1) * D],
                xd.ap()[b, tt * 128:(tt + 1) * 128, :], waits=w)
            ev.add("vector", lambda e, tt=tt, buf=buf: e.reduce_sum(
                st_sum[:, tt:tt + 1], xbt[:, buf * D:(buf + 1) * D], axis=AX),
                waits=[(SG, None)], inc=SV)
            ev.add("scalar", lambda e, tt=tt, buf=buf: e.activation(
                h2f[:, 0:D], xbt[:, buf * D:(buf + 1) * D], AF.Square,
                accum_out=st_sq[:, tt:tt + 1]), waits=[(SG, None), (SV, None)], inc=SC)
            ln_marks[tt] = (SV.n, SC.n)
        ev.add("vector", lambda e: e.tensor_scalar_mul(st_negm[:, 0:NTT], st_sum[:, 0:NTT], -1.0 / D),
               waits=[(SC, None)], inc=SV)
        ev.add("vector", lambda e: e.tensor_mul(st_tmp[:, 0:NTT], st_negm[:, 0:NTT], st_negm[:, 0:NTT]), inc=SV)
        ev.add("vector", lambda e: e.scalar_tensor_tensor(
            st_tmp[:, 0:NTT], st_sq[:, 0:NTT], 1.0 / D, st_tmp[:, 0:NTT], OP.mult, OP.subtract), inc=SV)
        ev.add("scalar", lambda e: e.activation(st_tmp[:, 0:NTT], st_tmp[:, 0:NTT], AF.Sqrt,
                                                bias=konst[:, 1:2], scale=1.0),
               waits=[(SV, None)], inc=SC)
        ev.add("vector", lambda e: e.reciprocal(st_inv[:, 0:NTT], st_tmp[:, 0:NTT]),
               waits=[(SC, None)], inc=SV)
        stats1 = SV.n

        NS = S // 512
        ts_marks = {}
        ap_marks = {}
        nrm_first = [True, True]
        nbm, nbx, nbi = {}, {}, [0]
        onesc_bf = vwin[:, 64:65]

        def norm_bound(stage, scol, nw):
            for dt in range(DC):
                gi = nbi[0]
                nbi[0] += 1
                sq = sqb[:, (gi % 2) * 512:(gi % 2 + 1) * 512]
                pr = prc[gi % 2]
                ev.add("scalar", lambda e, dt=dt, stage=stage, sq=sq: e.activation(
                    sq, stage[:, dt * 512:(dt + 1) * 512], AF.Square),
                    waits=[(SV, nw), (SP, nbm.get(gi - 2, 0))], inc=SC)
                ev.add("tensor", lambda e, sq=sq, pr=pr: e.matmul(
                    pr[0:1, 0:512], onesc_bf, sq,
                    start=True, stop=True),
                    waits=[(SC, None), (SV, nbx.get(gi - 2, 0))], inc=SP)
                nbm[gi] = SP.n
                ev.add("vector", lambda e, pr=pr: e.reduce_max(srow[0:1, 5:6], pr[0:1, 0:512], axis=AX),
                       waits=[(SP, None)], inc=SV)
                first = nrm_first[scol]
                ev.add("vector", (lambda e, scol=scol: e.tensor_copy(
                    srow[0:1, scol:scol + 1], srow[0:1, 5:6])) if first else
                    (lambda e, scol=scol: e.tensor_max(
                        srow[0:1, scol:scol + 1], srow[0:1, scol:scol + 1], srow[0:1, 5:6])), inc=SV)
                nbx[gi] = SV.n
                nrm_first[scol] = False

        b2m = {}
        trm = {}
        tr_ev = {}
        for ts in range(NS):
            slot = ts % 2
            t0 = ts * 512
            for tq in range(4):
                tt = ts * 4 + tq
                buf = tt % 4
                w = [(SV, ap_marks[tt - 4])] if tt >= 4 else [(SV, stats1)]
                dma(ev, "gpsimd", SG, xbt[:, buf * D:(buf + 1) * D],
                    xd.ap()[b, tt * 128:(tt + 1) * 128, :], waits=w)
                rs = h2f[:, D:2 * D] if tt % 2 == 0 else rbuf[:, 0:D]
                w2 = [(SG, None), (SV, stats1)]
                ev.add("vector", lambda e, tt=tt, buf=buf: e.tensor_scalar(
                    h2f[:, 0:D], xbt[:, buf * D:(buf + 1) * D],
                    st_negm[:, tt:tt + 1], st_inv[:, tt:tt + 1], OP.add, OP.mult),
                    waits=w2, inc=SV)
                ev.add("vector", lambda e, rs=rs: e.tensor_mul(rs, h2f[:, 0:D], car[:, 0:D]),
                       waits=[(SP, trm.get(tt - 2, 0))], inc=SV)
                ev.add("vector", lambda e, rs=rs: e.tensor_add(rs, rs, sar[:, 0:D]), inc=SV)
                ap_marks[tt] = SV.n
                wtr = ([(SP, ts_marks[ts - 2])] if ts >= 2 else [])
                NGRP = (DC + 3) // 4
                for grp in range(NGRP):
                    nbk = min(4, DC - grp * 4)
                    g2 = tt * NGRP + grp
                    pvt = pv2[g2 % 2]
                    for j in range(nbk):
                        c = grp * 4 + j
                        ev.add("tensor", lambda e, j=j, c=c, pvt=pvt, rs=rs: e.transpose(
                            pvt[:, j * 128:(j + 1) * 128], rs[:, c * 128:(c + 1) * 128], idmf[:, :]),
                            waits=([(SV, None), (SV, tr_ev.get(g2 - 2, 0))] + wtr) if j == 0 else [],
                            inc=(SP if j == nbk - 1 else None))
                    dstap = bass.AP(h1T[:, 0:1].tensor, slot * DC * 512 + grp * 4 * 512 + tq * 128,
                                    [[2 * DC * 512, 128], [512, nbk], [1, 128]])
                    ev.add("vector", lambda e, dstap=dstap, pvt=pvt, nbk=nbk: e.tensor_copy(
                        dstap, pvt[:, 0:nbk * 128]), waits=[(SP, None)], inc=SV)
                    tr_ev[g2] = SV.n
                trm[tt] = SP.n
            h1T_ts = SV.n
            psum_last = {0: None, 1: None, 2: None, 3: None}
            pquads = [psS[0][:, 0:512], psS[0][:, 512:1024],
                      psS[1][:, 0:512], psS[1][:, 512:1024]]
            acc_gi = [0]

            def acc_proj(groups, nchunks, mm_fn, evac_fn, first_waits, evw=()):
                for gkey in groups:
                    par = acc_gi[0] % 4
                    acc_gi[0] += 1
                    pj = pquads[par]
                    for c in range(nchunks):
                        w3 = []
                        if c == 0:
                            if psum_last[par] is not None:
                                w3.append((SV, psum_last[par]))
                            w3 += first_waits
                        ev.add("tensor", lambda e, gkey=gkey, c=c, pj=pj: mm_fn(e, gkey, c, pj),
                               waits=w3, inc=(SP if c == nchunks - 1 else None))
                    ev.add("vector", lambda e, gkey=gkey, pj=pj: evac_fn(e, gkey, pj),
                           waits=[(SP, None)] + list(evw), inc=SV)
                    psum_last[par] = SV.n

            fw = [(SV, h1T_ts)]
            sof = slot * DC * 512
            acc_proj(list(range(DC)), DC,
                     lambda e, dt, c, pj, sof=sof: e.matmul(
                         pj[:, 0:512], wqs[:, c * D + dt * 128:c * D + (dt + 1) * 128],
                         h1T[:, sof + c * 512: sof + (c + 1) * 512],
                         start=(c == 0), stop=(c == DC - 1)),
                     lambda e, dt, pj: e.tensor_copy(
                         qstage[:, dt * 512:(dt + 1) * 512], pj[:, 0:512]), fw,
                     evw=[(SS, b2m.get(('k', ts - 1), 0)), (SC, b2m.get(('nq', ts - 1), 0))])
            q_ev = SV.n
            for (fr, qd_) in ((fdn, qdn_dram), (fup, qup_dram)):
                ev.add("vector", lambda e, fr=fr, t0=t0: e.tensor_tensor(
                    kstage[:, 0:DC * 512], qstage[:, 0:DC * 512],
                    bass.AP(fr[:, 0:1].tensor, t0, [[S, 128], [0, DC], [1, 512]]),
                    OP.mult), waits=[(SV, q_ev), (SS, None), (SC, b2m.get(('nk', ts - 1), 0))], inc=SV)
                dma(ev, "sync", SS,
                    qd_.ap().rearrange("(c p) s -> p c s", p=128)[:, :, t0:t0 + 512],
                    kstage[:, 0:DC * 512], waits=[(SV, None)])
            b2m[('q', ts)] = SS.n
            norm_bound(qstage, 0, q_ev)
            b2m[('nq', ts)] = SC.n
            acc_proj(list(range(LC)), DC,
                     lambda e, lt, c, pj, sof=sof: e.matmul(
                         pj[:, 0:512], wcomb[:, c * DL + lt * 128:c * DL + (lt + 1) * 128],
                         h1T[:, sof + c * 512: sof + (c + 1) * 512],
                         start=(c == 0), stop=(c == DC - 1)),
                     lambda e, lt, pj, slot=slot: e.tensor_copy(
                         latr[:, slot * LC * 512 + lt * 512: slot * LC * 512 + (lt + 1) * 512],
                         pj[:, 0:512]), [])
            acc_proj(list(range(DC)), LC,
                     lambda e, dt, c, pj, slot=slot: e.matmul(
                         pj[:, 0:512], wuks[:, c * D + dt * 128:c * D + (dt + 1) * 128],
                         latr[:, slot * LC * 512 + c * 512: slot * LC * 512 + (c + 1) * 512],
                         start=(c == 0), stop=(c == LC - 1)),
                     lambda e, dt, pj: e.tensor_copy(
                         kstage[:, dt * 512:(dt + 1) * 512], pj[:, 0:512]), [(SV, None)],
                     evw=[(SS, b2m[('q', ts)]), (SC, b2m.get(('nk', ts - 1), 0))])
            k_ev = SV.n
            for (gr, kd_) in ((gdn, kdn_dram), (gup, kup_dram)):
                ev.add("vector", lambda e, gr=gr, t0=t0: e.tensor_tensor(
                    qstage[:, 0:DC * 512], kstage[:, 0:DC * 512],
                    bass.AP(gr[:, 0:1].tensor, t0, [[S, 128], [0, DC], [1, 512]]),
                    OP.mult), waits=[(SV, k_ev), (SS, None), (SP, None), (SC, None)], inc=SV)
                dma(ev, "sync", SS,
                    kd_.ap().rearrange("(c p) s -> p c s", p=128)[:, :, t0:t0 + 512],
                    qstage[:, 0:DC * 512], waits=[(SV, None)])
            b2m[('k', ts)] = SS.n
            norm_bound(kstage, 1, k_ev)
            b2m[('nk', ts)] = SC.n
            for tq in range(4):
                kt = ts * 4 + tq
                for jh in range(JH):
                    vb = (tq * JH + jh) % 2
                    acc_proj([(tq, jh)], LC,
                             lambda e, k_, c, pj, slot=slot: e.matmul(
                                 pj[:, 0:JW],
                                 latr[:, slot * LC * 512 + c * 512 + k_[0] * 128:
                                      slot * LC * 512 + c * 512 + k_[0] * 128 + 128],
                                 wuvs[:, c * D + k_[1] * JW:c * D + (k_[1] + 1) * JW],
                                 start=(c == 0), stop=(c == LC - 1)),
                             lambda e, k_, pj, vb=vb: e.tensor_copy(
                                 vstage[:, vb * JW:(vb + 1) * JW], pj[:, 0:JW]),
                             [], evw=[(SS, b2m.get(('v', vb), 0))])
                    dma(ev, "sync", SS,
                        vaug_dram.ap().rearrange("(k p) hd -> p k hd", p=128)[:, kt, jh * JW:(jh + 1) * JW],
                        vstage[:, vb * JW:(vb + 1) * JW], waits=[(SV, None)])
                    b2m[('v', vb)] = SS.n
            ts_marks[ts] = SP.n
        ev.add("vector", lambda e: e.tensor_mul(srow[0:1, 2:3], srow[0:1, 0:1], srow[0:1, 1:2]), inc=SV)
        ev.add("scalar", lambda e: e.activation(srow[0:1, 2:3], srow[0:1, 2:3], AF.Sqrt,
                                                bias=konst[:, 1:2][0:1, :], scale=1.0),
               waits=[(SV, None)], inc=SC)
        ev.add("vector", lambda e: e.tensor_scalar_mul(srow[0:1, 2:3], srow[0:1, 2:3], -(1.0 / math.sqrt(D))),
               waits=[(SC, None)], inc=SV)
        ev.add("tensor", lambda e: e.matmul(prc[0][:, 0:1], onesr[0:1, 0:128], srow[0:1, 2:3],
                                            start=True, stop=True), waits=[(SV, None)], inc=SP)
        ev.add("vector", lambda e: e.tensor_copy(negM[:, 0:1], prc[0][:, 0:1]),
               waits=[(SP, None)], inc=SV)

        win_marks = {}
        pv_g = {}
        next_lds = {}
        last_rbc = [None]

        def load_qwin(qi_i2, w):
            qi2 = qi_i2 * QT
            dma(ev, "sync", SS, qdnwin[:, 0:DC * 512],
                qdn_dram.ap().rearrange("(c p) s -> p c s", p=128)[:, :, qi2:qi2 + QT], waits=w)
            dma(ev, "sync", SS, qupwin[:, 0:DC * 512],
                qup_dram.ap().rearrange("(c p) s -> p c s", p=128)[:, :, qi2:qi2 + QT])
            return SS.n

        def load_pair_kv(pair, w):
            kb = pair % 2
            dma(ev, "sync", SS, kdn_c[:, kb * S:(kb + 1) * S],
                kdn_dram.ap()[pair * 128:(pair + 1) * 128, :], waits=w)
            dma(ev, "sync", SS, kup_c[:, kb * S:(kb + 1) * S],
                kup_dram.ap()[pair * 128:(pair + 1) * 128, :])
            return SS.n

        for qi_i in range(NQT):
            qi = qi_i * QT
            if last_rbc[0] is None:
                last_rbc[0] = SP.n
            if qi_i == 0:
                qwin_ld = load_qwin(0, [(SS, None)])
                kv_marks = {0: load_pair_kv(0, [(SS, None)])}
            else:
                qwin_ld = next_lds['q']
                kv_marks = {0: next_lds['k']}
            nkp = NKT // 2
            slot_free = {}
            exp_marks = {}
            vfix = {}
            pctx = {}
            vscale = SV.n

            def emit_stile(pair, kp):
                kv_ld = kv_marks[pair]
                kb = pair % 2
                qo = pair * 512
                g = pair * nkp + kp
                for half in range(2):
                    kt = kp * 2 + half
                    kj = kt * 128
                    for hh in range(2):
                        sl = (2 * g + hh) % 3
                        ps = psS[sl]
                        rl, rh = (0, 64) if hh == 0 else (64, 128)
                        tp = (0, 0) if hh == 0 else (64, 0)
                        co = half * QT
                        ko = kb * S
                        mmw = [(SS, kv_ld), (SS, qwin_ld), (SC, slot_free.get(sl, 0))] + (
                            [(SV, vscale)] if pair == 0 else [])
                        def mk(e, kd, qd, n0, n1, ps=ps, rl=rl, rh=rh, tp=tp, co=co, kj=kj, ko=ko, qo=qo):
                            return e.matmul(ps[:, co + n0:co + n1],
                                            kd[rl:rh, ko + kj:ko + kj + 128], qd[rl:rh, qo + n0:qo + n1],
                                            start=True, stop=True, tile_position=tp)
                        if kj < qi:
                            ev.add("tensor", lambda e, f=mk: f(e, kdn_c, qdnwin, 0, QT), waits=mmw, inc=SP)
                        elif kj >= qi + QT:
                            ev.add("tensor", lambda e, f=mk: f(e, kup_c, qupwin, 0, QT), waits=mmw, inc=SP)
                        else:
                            j = (kj - qi) // 128
                            if j > 0:
                                ev.add("tensor", lambda e, f=mk, j=j: f(e, kup_c, qupwin, 0, j * 128),
                                       waits=mmw, inc=SP)
                            ev.add("tensor", lambda e, f=mk, j=j: f(e, kdn_c, qdnwin, j * 128, (j + 1) * 128),
                                   waits=mmw, inc=SP)
                            if j < NQSUB - 1:
                                ev.add("tensor", lambda e, f=mk, j=j: f(e, kdn_c, qdnwin, (j + 1) * 128, QT),
                                       waits=mmw, inc=SP)
                            ev.add("vector", lambda e, ps=ps, co=co, j=j: e.tensor_mul(
                                ps[:, co + j * 128:co + (j + 1) * 128],
                                ps[:, co + j * 128:co + (j + 1) * 128], rdiag[:, :]),
                                waits=[(SP, None)], inc=SV)
                            vfix[sl] = SV.n

            def emit_exp(pair, kp, smark):
                g = pair * nkp + kp
                pb = (g % 2) * 2 * QT
                gkp = (qi_i * PAIRS + pair) * nkp + kp
                bw = pv_g.get(gkp - 2, 0)
                for hh in range(2):
                    sl = (2 * g + hh) % 3
                    Pb = PAb if hh == 0 else PBb
                    ev.add("scalar", lambda e, sl=sl, Pb=Pb, pb=pb: e.activation(
                        Pb[:, pb:pb + 2 * QT], psS[sl][:, 0:2 * QT], AF.Exp,
                        bias=negM[:, 0:1], scale=1.0),
                        waits=[(SP, smark), (SP, bw), (SV, vfix.get(sl, 0))], inc=SC)
                    slot_free[sl] = SC.n
                exp_marks[(pair, kp)] = SC.n

            def emit_pv(pair, kp):
                g = pair * nkp + kp
                pb = (g % 2) * 2 * QT
                vv = pctx[pair][1]
                for half in range(2):
                    kt = kp * 2 + half
                    vw_w = []
                    if kt == 0:
                        vw_w = [(SV, None), (SS, vv[0])]
                    elif kt == NKT // 2:
                        vw_w = [(SS, vv[1])]
                    for hh in range(2):
                        Pb = PAb if hh == 0 else PBb
                        base = kt * 193 + (0 if hh == 0 else 65)
                        wid = 65 if hh == 0 else 128
                        pv = pvA if hh == 0 else pvB
                        pvsl = pv[0:65, 0:QT] if hh == 0 else pv[:, 0:QT]
                        ev.add("tensor", lambda e, base=base, wid=wid, pvsl=pvsl, Pb=Pb, half=half, kt=kt, pb=pb: e.matmul(
                            pvsl, vwin[:, base:base + wid],
                            Pb[:, pb + half * QT: pb + (half + 1) * QT],
                            start=(kt == 0), stop=(kt == NKT - 1)),
                            waits=[(SC, exp_marks[(pair, kp)])] + vw_w, inc=SP)
                pv_g[(qi_i * PAIRS + pair) * nkp + kp] = SP.n

            def emit_ltail(pair):
                pv_done = SP.n
                win_marks[(qi_i * PAIRS + pair, 'k')] = SP.n
                ev.add("vector", lambda e: e.tensor_copy(
                    lst[64:65, 0:QT], pvA[64:65, 0:QT]),
                    waits=[(SP, pv_done), (SS, None)], inc=SV)
                dma(ev, "sync", SS, lrow_dram.ap()[2 * pair:2 * pair + 1, :],
                    lst[64:65, 0:QT], waits=[(SV, None)])
                ev.add("vector", lambda e: e.tensor_copy(
                    lst[0:1, 0:QT], pvB[0:1, 0:QT]), inc=SV)
                dma(ev, "sync", SS, lrow_dram.ap()[2 * pair + 1:2 * pair + 2, :],
                    lst[0:1, 0:QT], waits=[(SV, None)])
                dma(ev, "sync", SS, lrows[2 * pair:2 * pair + 2, 0:QT],
                    lrow_dram.ap()[2 * pair:2 * pair + 2, :], waits=[(SS, None)])
                ev.add("vector", lambda e, pair=pair: e.tensor_copy(
                    attnraw[0:64, pair * QT:(pair + 1) * QT], pvA[0:64, 0:QT]), inc=SV)
                ev.add("vector", lambda e, pair=pair: e.tensor_copy(
                    attnraw[64:128, pair * QT:(pair + 1) * QT], pvB[64:128, 0:QT]), inc=SV)

            def load_vwin_half(pair, h, w):
                NH = NKT // 2
                k0 = h * NH
                dma(ev, "sync", SS,
                    bass.AP(vwin[:, 0:1].tensor, k0 * 193,
                            [[VW, 128], [193, NH], [1, 64]]),
                    vaug_dram.ap().rearrange("(k p) hd -> p k hd", p=128)[:, k0:k0 + NH, pair * 128:pair * 128 + 64],
                    waits=w)
                dma(ev, "sync", SS,
                    bass.AP(vwin[:, 0:1].tensor, k0 * 193 + 129,
                            [[VW, 128], [193, NH], [1, 64]]),
                    vaug_dram.ap().rearrange("(k p) hd -> p k hd", p=128)[:, k0:k0 + NH, pair * 128 + 64:(pair + 1) * 128])
                return SS.n

            for pair in range(PAIRS):
                pidx = qi_i * PAIRS + pair
                kb = pair % 2
                if pair == 0:
                    wk_w = [(SP, win_marks[(pidx - 1, 'k')])] if pidx >= 1 else [(SS, None)]
                    v0 = load_vwin_half(pair, 0, wk_w)
                    v1 = load_vwin_half(pair, 1, [])
                    pctx[pair] = (kv_marks[pair], [v0, v1], kb)
                for kp in range(nkp):
                    emit_stile(pair, kp)
                    smark = SP.n
                    if kp > 0:
                        emit_pv(pair, kp - 1)
                        if kp == nkp // 2 and pair + 1 < PAIRS:
                            v0n = load_vwin_half(pair + 1, 0,
                                                 [(SP, pv_g[pidx * nkp + nkp // 2 - 1])])
                            pctx[pair + 1] = (None, [v0n, None], kb ^ 1)
                    elif pair > 0:
                        emit_pv(pair - 1, nkp - 1)
                        emit_ltail(pair - 1)
                        v1n = load_vwin_half(pair, 1,
                                             [(SP, pv_g[(pidx - 1) * nkp + nkp - 1])])
                        pctx[pair] = (kv_marks[pair], [pctx[pair][1][0], v1n], kb)
                    if kp == 0 and pair + 1 < PAIRS:
                        kv_marks[pair + 1] = load_pair_kv(
                            pair + 1, [(SP, win_marks.get((pidx - 1, 'k'), 0) if pair >= 1 else smark)])
                    emit_exp(pair, kp, smark)
            emit_pv(PAIRS - 1, nkp - 1)
            emit_ltail(PAIRS - 1)
            win_marks[(qi_i, 'q')] = SP.n
            if qi_i + 1 < NQT:
                next_lds['q'] = load_qwin(qi_i + 1, [(SP, win_marks[(qi_i, 'q')])])
                next_lds['k'] = load_pair_kv(0, [(SP, win_marks[((qi_i + 1) * PAIRS - 2, 'k')])])
            ev.add("vector", lambda e: e.reciprocal(rip[0:2 * PAIRS, 0:QT], lrows[0:2 * PAIRS, 0:QT]),
                   waits=[(SS, None)], inc=SV)
            dma(ev, "sync", SS, rrow_dram.ap()[:, :], rip[0:2 * PAIRS, 0:QT], waits=[(SV, None)])
            fin_marks = {}
            rbc_marks = {}
            for pair in range(PAIRS):
                rsl = rbuf if pair % 2 == 0 else rowbuf
                w4 = [(SP, rbc_marks[pair - 2])] if pair >= 2 else [(SP, last_rbc[0])]
                dma(ev, "sync", SS, rsl[0:1, 0:QT],
                    rrow_dram.ap()[2 * pair:2 * pair + 1, :],
                    waits=([(SS, None)] if pair < 2 else []) + w4)
                dma(ev, "sync", SS, rsl[0:1, QT:2 * QT],
                    rrow_dram.ap()[2 * pair + 1:2 * pair + 2, :])
                pr = prc[pair % 2]
                w4b = [(SV, fin_marks[pair - 2])] if pair >= 2 else []
                ev.add("tensor", lambda e, pr=pr, rsl=rsl: e.matmul(
                    pr[0:64, 0:QT], onesr[0:1, 0:64], rsl[0:1, 0:QT],
                    start=True, stop=True, tile_position=(0, 0)),
                    waits=[(SS, None)] + w4b, inc=SP)
                ev.add("tensor", lambda e, pr=pr, rsl=rsl: e.matmul(
                    pr[64:128, 0:QT], onesr[0:1, 64:128], rsl[0:1, QT:2 * QT],
                    start=True, stop=True, tile_position=(0, 64)), inc=SP)
                rbc_marks[pair] = SP.n
                ev.add("vector", lambda e, pr=pr, pair=pair: e.tensor_mul(
                    attnraw[:, pair * QT:(pair + 1) * QT], attnraw[:, pair * QT:(pair + 1) * QT],
                    pr[:, 0:QT]), waits=[(SP, None)], inc=SV)
                fin_marks[pair] = SV.n
            last_rbc[0] = SP.n
            attn_done = SV.n
            ep_marks = {}
            for tq in range(NQSUB):
                tt = qi_i * NQSUB + tq
                buf = tq % 2
                if tq >= 2:
                    w5 = [(SV, ep_marks[(tq - 2, 'x')])]
                elif (tq + NQSUB - 2, 'x') in ep_marks:
                    w5 = [(SV, ep_marks[(tq + NQSUB - 2, 'x')])]
                else:
                    w5 = [(SV, ap_marks[NTT - 2 + tq])]
                dma(ev, "gpsimd", SG, xbt[:, buf * D:(buf + 1) * D],
                    xd.ap()[b, qi + tq * 128: qi + (tq + 1) * 128, :], waits=w5)
                for jh in range(JH):
                    po = prc[jh % 2] if JH > 1 else prc[tq % 2]
                    for c in range(DC):
                        w2 = []
                        if c == 0:
                            w2 = [(SV, attn_done)]
                            prevk = (tq - 1, 'e' + str(jh)) if JH >= 2 else (tq - 2, 'e' + str(jh))
                            if prevk in ep_marks:
                                w2.append((SV, ep_marks[prevk]))
                        ev.add("tensor", lambda e, tq=tq, jh=jh, c=c, po=po: e.matmul(
                            po[:, 0:JW], attnraw[:, c * QT + tq * 128: c * QT + (tq + 1) * 128],
                            wos[:, c * D + jh * JW: c * D + (jh + 1) * JW],
                            start=(c == 0), stop=(c == DC - 1)),
                            waits=w2, inc=(SP if c == DC - 1 else None))
                    ev.add("vector", lambda e, jh=jh, po=po: e.tensor_mul(
                        ot_t[:, jh * JW:(jh + 1) * JW], po[:, 0:JW],
                        gar[:, jh * JW:(jh + 1) * JW]),
                        waits=[(SP, None)], inc=SV)
                    ev.add("vector", lambda e, jh=jh, buf=buf: e.tensor_add(
                        x1t[:, buf * D + jh * JW: buf * D + (jh + 1) * JW],
                        ot_t[:, jh * JW:(jh + 1) * JW], xbt[:, buf * D + jh * JW: buf * D + (jh + 1) * JW]),
                        waits=[(SG, None)], inc=SV)
                    ep_marks[(tq, 'e' + str(jh))] = SV.n
                ev.add("vector", lambda e, tt=tt, buf=buf: e.reduce_sum(
                    st_sum[:, tt:tt + 1], x1t[:, buf * D:(buf + 1) * D], axis=AX), inc=SV)
                ep_marks[(tq, 'x')] = SV.n
                ev.add("scalar", lambda e, tt=tt, buf=buf: e.activation(
                    h2f[:, 0:D], x1t[:, buf * D:(buf + 1) * D], AF.Square,
                    accum_out=st_sq[:, tt:tt + 1]), waits=[(SV, None)], inc=SC)
                dma(ev, "gpsimd", SG, x1_dram.ap()[brow + qi + tq * 128: brow + qi + (tq + 1) * 128, :],
                    x1t[:, buf * D:(buf + 1) * D], waits=[(SV, None), (SC, None)])
        ev.add("vector", lambda e: e.tensor_scalar_mul(st_negm[:, 0:NTT], st_sum[:, 0:NTT], -1.0 / D),
               waits=[(SC, None), (SG, None)], inc=SV)
        ev.add("vector", lambda e: e.tensor_mul(st_tmp[:, 0:NTT], st_negm[:, 0:NTT], st_negm[:, 0:NTT]), inc=SV)
        ev.add("vector", lambda e: e.scalar_tensor_tensor(
            st_tmp[:, 0:NTT], st_sq[:, 0:NTT], 1.0 / D, st_tmp[:, 0:NTT], OP.mult, OP.subtract), inc=SV)
        ev.add("scalar", lambda e: e.activation(st_tmp[:, 0:NTT], st_tmp[:, 0:NTT], AF.Sqrt,
                                                bias=konst[:, 1:2], scale=1.0), waits=[(SV, None)], inc=SC)
        ev.add("vector", lambda e: e.reciprocal(st_inv[:, 0:NTT], st_tmp[:, 0:NTT]),
               waits=[(SC, None)], inc=SV)
        stats2 = SV.n
        h2_marks = {}
        h2_store = {}
        for tt in range(NTT):
            buf = tt % 2
            rs = h2f[:, buf * D:(buf + 1) * D]
            w6 = [(SV, h2_marks[tt - 2])] if tt >= 2 else [(SC, None), (SG, None), (SV, None)]
            dma(ev, "gpsimd", SG, x1t[:, buf * D:(buf + 1) * D],
                x1_dram.ap()[brow + tt * 128: brow + (tt + 1) * 128, :], waits=w6)
            ev.add("vector", lambda e, tt=tt, buf=buf, rs=rs: e.tensor_scalar(
                rs, x1t[:, buf * D:(buf + 1) * D],
                st_negm[:, tt:tt + 1], st_inv[:, tt:tt + 1], OP.add, OP.mult),
                waits=[(SG, h2_store.get(tt - 2, 0)), (SG, None), (SV, stats2), (SC, None)], inc=SV)
            ev.add("vector", lambda e, rs=rs: e.tensor_mul(rs, rs, cbr[:, 0:D]), inc=SV)
            ev.add("vector", lambda e, rs=rs: e.tensor_add(rs, rs, sbr[:, 0:D]), inc=SV)
            ev.add("vector", lambda e, rs=rs: e.reduce_max(mq_cols[:, 2:3], rs, axis=AX,
                                                           apply_absolute_value=True), inc=SV)
            first = (b == 0 and tt == 0)
            ev.add("vector", (lambda e: e.tensor_copy(gmax_cols[:, 0:1], mq_cols[:, 2:3])) if first
                   else (lambda e: e.tensor_max(gmax_cols[:, 0:1], gmax_cols[:, 0:1], mq_cols[:, 2:3])), inc=SV)
            h2_marks[tt] = SV.n
            dma(ev, "gpsimd", SG, h2_dram.ap()[brow + tt * 128: brow + (tt + 1) * 128, :],
                rs, waits=[(SV, None)])
            h2_store[tt] = SG.n
        run_phase(ev)

    esB.close()
    psB_.close()

    esD = ExitStack()
    psD_ = ExitStack()
    qwgT = sb("qwgT", [128, DC * I], BF16, esD)
    qwuT = sb("qwuT", [128, DC * I], BF16, esD)
    h2t = sb("h2t", [128, 2 * D], F32, esD)
    xqd = sb("xqd", [128, 2 * D], BF16, esD)
    xqT = sb("xqT", [128, 2 * DC * 512], BF16, esD)
    gsil = sb("gsil", [128, 4 * 512], F32, esD)
    hpre_t = sb("hpre_t", [128, 4 * 512], F32, esD)
    pg = [psD_.enter_context(nc.psum_tensor(f"pg{i}", [128, 512], F32)) for i in range(4)]
    pu = [psD_.enter_context(nc.psum_tensor(f"pu{i}", [128, 512], F32)) for i in range(4)]
    pbc2 = pg[0]
    ev = EV()
    for c in range(DC):
        dma(ev, "gpsimd", SG, qwgT[:, c * I:(c + 1) * I], qwgT_dram.ap()[c * 128:(c + 1) * 128, :])
        dma(ev, "gpsimd", SG, qwuT[:, c * I:(c + 1) * I], qwuT_dram.ap()[c * 128:(c + 1) * 128, :])
    gw_loaded = SG.n
    dma(ev, "sync", SS, pmax_dram.ap()[0:1, :].rearrange("a b -> (a b)"), gmax_cols[:, 0:1])
    dma(ev, "sync", SS, agrow[0:1, 0:128], pmax_dram.ap()[0:1, :], waits=[(SS, None)])
    ev.add("vector", lambda e: e.reduce_max(srow[0:1, 0:1], agrow[0:1, 0:128], axis=AX),
           waits=[(SS, None)], inc=SV)
    ev.add("vector", lambda e: e.memset(srow[0:1, 1:8], 0.0), inc=SV)
    dma(ev, "sync", SS, ag1_in.ap(), srow[0:1, 0:8], waits=[(SV, None)])
    ev.add("gpsimd", lambda e: e.collective_compute(
        "AllGather", OP.bypass, replica_groups=[list(range(NCORES))],
        ins=[ag1_in.ap().opt()], outs=[ag1_out.ap().opt()]),
        waits=[(SS, None)], inc=SCC)
    dma(ev, "sync", SS, agrow[0:1, 0:8 * NCORES],
        ag1_out.ap().rearrange("a b -> (a b)").rearrange("(a b) -> a b", a=1), waits=[(SCC, None)])
    ev.add("vector", lambda e: e.reduce_max(srow[0:1, 1:2], agrow[0:1, 0:8 * NCORES], axis=AX),
           waits=[(SS, None)], inc=SV)
    ev.add("scalar", lambda e: e.activation(srow[0:1, 2:3], clip_sb[0:1, 0:1], AF.Sigmoid),
           waits=[(SV, None)], inc=SC)
    ev.add("vector", lambda e: e.tensor_mul(srow[0:1, 3:4], srow[0:1, 1:2], srow[0:1, 2:3]),
           waits=[(SC, None)], inc=SV)
    ev.add("vector", lambda e: e.tensor_scalar_mul(srow[0:1, 3:4], srow[0:1, 3:4], 1.0 / QMAX), inc=SV)
    ev.add("vector", lambda e: e.reciprocal(srow[0:1, 4:5], srow[0:1, 3:4]), inc=SV)
    for (colt, sofs) in ((sx_col, 3), (sxr_col, 4)):
        ev.add("tensor", lambda e, sofs=sofs: e.matmul(pbc2[:, 0:1], onesr[0:1, 0:128],
                                                       srow[0:1, sofs:sofs + 1], start=True, stop=True),
               waits=[(SV, None)], inc=SP)
        ev.add("vector", lambda e, colt=colt: e.tensor_copy(colt[:, 0:1], pbc2[:, 0:1]),
               waits=[(SP, None)], inc=SV)
    ev.add("vector", lambda e: e.tensor_scalar(sgx_cols[:, 0:IT], sg_cols[:, 0:IT],
                                               sx_col[:, 0:1], None, OP.mult), inc=SV)
    ev.add("vector", lambda e: e.memset(hmax_cols[:, 0:IT], 0.0), inc=SV)
    d_marks = {}

    def d_quant(t5):
        r0 = t5 * 512
        sl = (t5 % 2) * DC * 512
        for sub in range(4):
            gs = t5 * 4 + sub
            bs = gs % 2
            dma(ev, "gpsimd", SG, h2t[:, bs * D:(bs + 1) * D],
                h2_dram.ap()[r0 + sub * 128: r0 + (sub + 1) * 128, :],
                waits=[(SV, d_marks.get(('dq', gs - 2), 0))])
            ev.add("vector", lambda e, bs=bs: e.tensor_scalar(
                h2t[:, bs * D:(bs + 1) * D], h2t[:, bs * D:(bs + 1) * D],
                sxr_col[:, 0:1], CM, OP.mult, OP.add), waits=[(SG, None)], inc=SV)
            ev.add("vector", lambda e, bs=bs: e.tensor_scalar(
                h2t[:, bs * D:(bs + 1) * D], h2t[:, bs * D:(bs + 1) * D],
                CM - 128.0, CM + 127.0, OP.max, OP.min), inc=SV)
            ev.add("vector", lambda e, bs=bs: e.tensor_scalar(
                xqd[:, bs * D:(bs + 1) * D], h2t[:, bs * D:(bs + 1) * D],
                CM, None, OP.subtract),
                waits=[(SS, d_marks.get(('dxT', gs - 2), 0))], inc=SV)
            d_marks[('dq', gs)] = SV.n
            for c in range(DC):
                dma_t(ev, "sync", SS, xqT[:, sl + c * 512 + sub * 128: sl + c * 512 + (sub + 1) * 128],
                      xqd[:, bs * D + c * 128: bs * D + (c + 1) * 128],
                      waits=[(SV, None)] + ([(SP, d_marks[(t5 - 2, 'p')])] if t5 >= 2 else []))
            d_marks[('dxT', gs)] = SS.n
        d_marks[(t5, 'hq')] = SV.n
        d_marks[(t5, 'xT')] = SS.n

    def d_mms(t5):
        r0 = t5 * 512
        sl = (t5 % 2) * DC * 512
        gu_marks = d_marks
        def gprev(it, tag):
            return gu_marks.get(('g', t5 * IT + it - 4, tag), 0)
        for it in range(IT):
            buf = (t5 * IT + it) % 4
            for c in range(DC):
                w = []
                if c == 0:
                    w = [(SS, d_marks[(t5, 'xT')]), (SG, gw_loaded),
                         (SC, gprev(it, 's')), (SV, gprev(it, 'v'))]
                ev.add("tensor", lambda e, it=it, c=c, buf=buf, sl=sl: e.matmul(
                    pg[buf][:, 0:512], qwgT[:, c * I + it * 128: c * I + (it + 1) * 128],
                    xqT[:, sl + c * 512:sl + (c + 1) * 512], start=(c == 0), stop=(c == DC - 1)),
                    waits=w, inc=(SP if c == DC - 1 else None))
            for c in range(DC):
                ev.add("tensor", lambda e, it=it, c=c, buf=buf, sl=sl: e.matmul(
                    pu[buf][:, 0:512], qwuT[:, c * I + it * 128: c * I + (it + 1) * 128],
                    xqT[:, sl + c * 512:sl + (c + 1) * 512], start=(c == 0), stop=(c == DC - 1)),
                    inc=(SP if c == DC - 1 else None))
            ev.add("scalar", lambda e, it=it, buf=buf: e.activation(
                gsil[:, buf * 512:(buf + 1) * 512], pg[buf][:, 0:512], AF.Silu,
                bias=0.0, scale=sgx_cols[:, it:it + 1]),
                waits=[(SP, None), (SV, gprev(it, 'v'))], inc=SC)
            gu_marks[('g', t5 * IT + it, 's')] = SC.n
            hw = [(SC, None), (SG, gprev(it, 'g'))]
            ev.add("vector", lambda e, it=it, buf=buf: e.scalar_tensor_tensor(
                hpre_t[:, buf * 512:(buf + 1) * 512], pu[buf][:, 0:512], 1.0,
                gsil[:, buf * 512:(buf + 1) * 512], OP.bypass, OP.mult),
                waits=hw, inc=SV)
            ev.add("vector", lambda e, buf=buf: e.reduce_max(
                mq_cols[:, 2:3], hpre_t[:, buf * 512:(buf + 1) * 512], axis=AX,
                apply_absolute_value=True), inc=SV)
            ev.add("vector", lambda e, it=it: e.tensor_max(
                hmax_cols[:, it:it + 1], hmax_cols[:, it:it + 1], mq_cols[:, 2:3]), inc=SV)
            gu_marks[('g', t5 * IT + it, 'v')] = SV.n
            dma(ev, "gpsimd", SG, hpre_dram.ap()[it * 128:(it + 1) * 128, r0:r0 + 512],
                hpre_t[:, buf * 512:(buf + 1) * 512], waits=[(SV, None)])
            gu_marks[('g', t5 * IT + it, 'g')] = SG.n
        d_marks[(t5, 'p')] = SP.n

    for t5 in range(NT512 + 1):
        if t5 < NT512:
            d_quant(t5)
        if t5 >= 1:
            d_mms(t5 - 1)
    ev.add("vector", lambda e: e.tensor_mul(hmax_cols[:, 0:IT], hmax_cols[:, 0:IT], su_cols[:, 0:IT]),
           waits=[(SG, None)], inc=SV)
    ev.add("vector", lambda e: e.tensor_scalar(hmax_cols[:, 0:IT], hmax_cols[:, 0:IT],
                                               sx_col[:, 0:1], None, OP.mult), inc=SV)
    ev.add("vector", lambda e: e.reduce_max(mq_cols[:, 2:3], hmax_cols[:, 0:IT], axis=AX), inc=SV)
    run_phase(ev, psD_)
    esD.close()

    esE = ExitStack()
    esF = ExitStack()
    gbeff = sb("gbeff", [128, B * D], F32, esF)
    psF_ = ExitStack()
    qwdT = sb("qwdT", [128, IT * D], BF16, esF)
    hpf = sb("hpf", [128, 2 * 512], F32, esF)
    hqf = sb("hqf", [128, 2 * IT * 512], BF16, esF)
    x1rt = sb("x1rt", [128, 2 * D], BF16, esF)
    outt = sb("outt", [128, 2 * D], F32, esF)
    tqf = sb("tqf", [128, 2 * 512], F32, esF)
    pd_ = [psF_.enter_context(nc.psum_tensor(f"pd{i}", [128, 512], F32)) for i in range(4)]
    pbc3 = psF_.enter_context(nc.psum_tensor("pbc3", [128, 512], F32))
    sdrow = sb("sdrow", [1, D], F32, esE)
    gerow = sb("gerow", [1, D], F32, esE)
    ev = EV()
    for c in range(IT):
        dma(ev, "gpsimd", SG, qwdT[:, c * D:(c + 1) * D], qwdT_dram.ap()[c * 128:(c + 1) * 128, :])
    dw_loaded = SG.n
    dma(ev, "sync", SS, pmax_dram.ap()[0:1, :].rearrange("a b -> (a b)"), mq_cols[:, 2:3])
    dma(ev, "sync", SS, agrow[0:1, 0:128], pmax_dram.ap()[0:1, :], waits=[(SS, None)])
    ev.add("vector", lambda e: e.reduce_max(srow[0:1, 0:1], agrow[0:1, 0:128], axis=AX),
           waits=[(SS, None)], inc=SV)
    ev.add("vector", lambda e: e.memset(srow[0:1, 1:8], 0.0), inc=SV)
    dma(ev, "sync", SS, ag2_in.ap(), srow[0:1, 0:8], waits=[(SV, None)])
    ev.add("gpsimd", lambda e: e.collective_compute(
        "AllGather", OP.bypass, replica_groups=[list(range(NCORES))],
        ins=[ag2_in.ap().opt()], outs=[ag2_out.ap().opt()]),
        waits=[(SS, None)], inc=SCC)
    dma(ev, "sync", SS, agrow[0:1, 0:8 * NCORES],
        ag2_out.ap().rearrange("a b -> (a b)").rearrange("(a b) -> a b", a=1), waits=[(SCC, None)])
    ev.add("vector", lambda e: e.reduce_max(srow[0:1, 1:2], agrow[0:1, 0:8 * NCORES], axis=AX),
           waits=[(SS, None)], inc=SV)
    ev.add("scalar", lambda e: e.activation(srow[0:1, 2:3], clip_sb[0:1, 1:2], AF.Sigmoid),
           waits=[(SV, None)], inc=SC)
    ev.add("vector", lambda e: e.tensor_mul(srow[0:1, 3:4], srow[0:1, 1:2], srow[0:1, 2:3]),
           waits=[(SC, None)], inc=SV)
    ev.add("vector", lambda e: e.tensor_scalar_mul(srow[0:1, 3:4], srow[0:1, 3:4], 1.0 / QMAX), inc=SV)
    ev.add("vector", lambda e: e.reciprocal(srow[0:1, 4:5], srow[0:1, 3:4]), inc=SV)
    ev.add("tensor", lambda e: e.matmul(pbc3[:, 0:1], onesr[0:1, 0:128], srow[0:1, 4:5],
                                        start=True, stop=True), waits=[(SV, None)], inc=SP)
    ev.add("vector", lambda e: e.tensor_copy(mq_cols[:, 3:4], pbc3[:, 0:1]), waits=[(SP, None)], inc=SV)
    ev.add("vector", lambda e: e.tensor_scalar(q2s_cols[:, 0:IT], su_cols[:, 0:IT],
                                               sx_col[:, 0:1], None, OP.mult), inc=SV)
    ev.add("vector", lambda e: e.tensor_scalar(q2s_cols[:, 0:IT], q2s_cols[:, 0:IT],
                                               mq_cols[:, 3:4], None, OP.mult), inc=SV)
    q2s_done = SV.n
    dma(ev, "sync", SS, sdrow[0:1, 0:D],
        sd_dram.ap().rearrange("c p -> (c p)").rearrange("(a b) -> a b", a=1))
    for b in range(B):
        dma(ev, "sync", SS, rowbuf[0:1, 0:D],
            mod_dram.ap()[b, 5 * D:6 * D].rearrange("(a c) -> a c", a=1), waits=[(SV, None)])
        ev.add("vector", lambda e: e.tensor_mul(gerow[0:1, 0:D], rowbuf[0:1, 0:D], sdrow[0:1, 0:D]),
               waits=[(SS, None), (SP, None)], inc=SV)
        ev.add("vector", lambda e: e.tensor_scalar(gerow[0:1, 0:D], gerow[0:1, 0:D],
                                                   srow[0:1, 3:4], None, OP.mult), inc=SV)
        for j in range(D // JW):
            ev.add("tensor", lambda e, j=j: e.matmul(
                pbc3[:, 0:JW], onesr[0:1, 0:128], gerow[0:1, j * JW:(j + 1) * JW],
                start=True, stop=True), waits=[(SV, None)], inc=SP)
            ev.add("vector", lambda e, b=b, j=j: e.tensor_copy(
                gbeff[:, b * D + j * JW: b * D + (j + 1) * JW], pbc3[:, 0:JW]),
                waits=[(SP, None)], inc=SV)
    f_marks = {}

    def f_quant(t5):
        r0 = t5 * 512
        sl = (t5 % 2) * IT * 512
        for it in range(IT):
            g = t5 * IT + it
            buf = g % 2
            w = [(SC, f_marks.get(('fc', g - 2), 0))]
            dma(ev, "gpsimd", SG, hpf[:, buf * 512:(buf + 1) * 512],
                hpre_dram.ap()[it * 128:(it + 1) * 128, r0:r0 + 512], waits=w)
            wv = [(SV, f_marks.get(('fv', g - 2), 0))]
            ev.add("scalar", lambda e, it=it, buf=buf: e.activation(
                tqf[:, buf * 512:(buf + 1) * 512], hpf[:, buf * 512:(buf + 1) * 512], AF.Identity,
                bias=konst[:, 0:1], scale=q2s_cols[:, it:it + 1]),
                waits=[(SG, None), (SV, q2s_done)] + wv, inc=SC)
            f_marks[('fc', g)] = SC.n
            ev.add("vector", lambda e, buf=buf: e.tensor_scalar(
                tqf[:, buf * 512:(buf + 1) * 512], tqf[:, buf * 512:(buf + 1) * 512],
                CM - 128.0, CM + 127.0, OP.max, OP.min), waits=[(SC, None)], inc=SV)
            ev.add("vector", lambda e, it=it, buf=buf, sl=sl: e.tensor_scalar(
                hqf[:, sl + it * 512: sl + (it + 1) * 512], tqf[:, buf * 512:(buf + 1) * 512],
                CM, None, OP.subtract),
                waits=[(SP, f_marks[(t5 - 2, 'p')])] if (t5 >= 2 and it < 2) else [], inc=SV)
            f_marks[('fv', g)] = SV.n
        f_marks[(t5, 'hq')] = SV.n

    def f_mms(t5):
        r0 = t5 * 512
        b = r0 // S
        sl = (t5 % 2) * IT * 512
        for tq in range(4):
            bufx = tq % 2
            wx = [(SV, f_marks[(t5, tq - 2, 'a')])] if (tq >= 2) else (
                [(SV, f_marks[(t5 - 1, tq + 2, 'a')])] if t5 >= 1 else [])
            dma(ev, "gpsimd", SG, x1rt[:, bufx * D:(bufx + 1) * D],
                x1_dram.ap()[r0 + tq * 128: r0 + (tq + 1) * 128, :], waits=wx)
            for jh in range(JH):
                gj = (t5 * 4 + tq) * JH + jh
                po = pd_[gj % 4]
                for it in range(IT):
                    w2 = []
                    if it == 0:
                        w2 = [(SV, f_marks[(t5, 'hq')]), (SG, dw_loaded),
                              (SV, f_marks.get(('fm', gj - 4), 0))]
                    ev.add("tensor", lambda e, tq=tq, jh=jh, it=it, po=po, sl=sl: e.matmul(
                        po[:, 0:JW], hqf[:, sl + it * 512 + tq * 128: sl + it * 512 + (tq + 1) * 128],
                        qwdT[:, it * D + jh * JW: it * D + (jh + 1) * JW],
                        start=(it == 0), stop=(it == IT - 1)),
                        waits=w2, inc=(SP if it == IT - 1 else None))
                ev.add("vector", lambda e, jh=jh, po=po, b=b: e.tensor_mul(
                    outt[:, jh * JW:(jh + 1) * JW], po[:, 0:JW],
                    gbeff[:, b * D + jh * JW: b * D + (jh + 1) * JW]),
                    waits=[(SP, None), (SST, None)], inc=SV)
                ev.add("vector", lambda e, jh=jh, bufx=bufx: e.tensor_add(
                    outt[:, D + jh * JW: D + (jh + 1) * JW], outt[:, jh * JW:(jh + 1) * JW],
                    x1rt[:, bufx * D + jh * JW: bufx * D + (jh + 1) * JW]),
                    waits=[(SG, None)], inc=SV)
                f_marks[('fm', gj)] = SV.n
            f_marks[(t5, tq, 'a')] = SV.n
            ev.add_dma("sync", SST, lambda e, b=b, r0=r0, tq=tq: e.dma_start(
                out=outd.ap()[b, (r0 % S) + tq * 128: (r0 % S) + (tq + 1) * 128, :],
                in_=outt[:, D:2 * D]), waits=[(SV, None)])
        f_marks[(t5, 'p')] = SP.n

    for t5 in range(NT512 + 1):
        if t5 < NT512:
            f_quant(t5)
        if t5 >= 1:
            f_mms(t5 - 1)
    ev.add("sync", lambda e: e.nop(), waits=[(SST, None)])
    run_phase(ev, psF_)
    esE.close()
    esF.close()
    esg.close()
    return nc


def _host_consts(cfg: Cfg):
    S = cfg.S
    t = np.arange(S, dtype=np.float64)
    fdn = np.exp(-LAM * t).astype(np.float32)
    fup = np.exp(LAM * t).astype(np.float32)
    isq = 1.0 / math.sqrt(cfg.D)
    gdn = (np.exp(LAM * t) * isq).astype(np.float32)
    gup = (np.exp(-LAM * t) * isq).astype(np.float32)
    rep = lambda v: np.broadcast_to(v[None, :], (128, S)).copy()
    p = np.arange(128.0)[:, None]
    f = np.arange(128.0)[None, :]
    rdiag = np.exp(-2 * LAM * np.maximum(0.0, p - f)).astype(np.float32)
    konst = np.zeros((128, 8), np.float32)
    konst[:, 0] = CM
    konst[:, 1] = EPS
    konst[:, 2] = 1.0
    return {
        "idm": np.eye(128, dtype=np.float32),
        "fdn": rep(fdn), "fup": rep(fup), "gdn": rep(gdn), "gup": rep(gup),
        "rdiag": rdiag,
        "onesr": np.ones((1, 128), np.float32),
        "konst": konst,
    }


_NC_CACHE = {}


def kernel(L0_attn_x, L0_attn_W_q, L0_attn_W_k, L0_attn_W_v, L0_attn_W_o,
           L0_attn_W_down_kv, L0_attn_W_up_k, L0_attn_W_up_v,
           L0_ffn_clip_x, L0_ffn_clip_down, L0_ffn_w_gate, L0_ffn_w_up,
           L0_ffn_w_down, L0_c, L0_wmod):
    cfg = Cfg()
    if "main" not in _NC_CACHE:
        _NC_CACHE["main"] = build_program(cfg)
    nc = _NC_CACHE["main"]
    x = np.ascontiguousarray(np.asarray(L0_attn_x, dtype=np.float32))
    shared = {
        "wq": np.ascontiguousarray(L0_attn_W_q, dtype=np.float32),
        "wk": np.ascontiguousarray(L0_attn_W_k, dtype=np.float32),
        "wv": np.ascontiguousarray(L0_attn_W_v, dtype=np.float32),
        "wo": np.ascontiguousarray(L0_attn_W_o, dtype=np.float32),
        "wdn": np.ascontiguousarray(L0_attn_W_down_kv, dtype=np.float32),
        "wuk": np.ascontiguousarray(L0_attn_W_up_k, dtype=np.float32),
        "wuv": np.ascontiguousarray(L0_attn_W_up_v, dtype=np.float32),
        "wmod": np.ascontiguousarray(L0_wmod, dtype=np.float32),
        "wg": np.ascontiguousarray(L0_ffn_w_gate, dtype=np.float32),
        "wu": np.ascontiguousarray(L0_ffn_w_up, dtype=np.float32),
        "wdw": np.ascontiguousarray(L0_ffn_w_down, dtype=np.float32),
        "clip2": np.stack([np.asarray(L0_ffn_clip_x, np.float32).reshape(1),
                           np.asarray(L0_ffn_clip_down, np.float32).reshape(1)],
                          axis=1).reshape(1, 2),
        **_host_consts(cfg),
    }
    cvals = np.ascontiguousarray(L0_c, dtype=np.float32)
    in_maps = []
    for i in range(cfg.n_cores):
        m = dict(shared)
        m["x"] = x[2 * i:2 * i + 2]
        m["cmod"] = cvals[2 * i:2 * i + 2]
        in_maps.append(m)
    res = run_bass_kernel_spmd(nc, in_maps, core_ids=list(range(cfg.n_cores)))
    out = np.concatenate([res.results[i]["out"] for i in range(cfg.n_cores)], axis=0)
    return out.astype(np.float32)


# version 29
# speedup vs baseline: 1.0567x; 1.0307x over previous
import math
import numpy as np
import ml_dtypes
from contextlib import ExitStack

import concourse.bass as bass
import concourse.mybir as mybir
from concourse.bass_utils import run_bass_kernel_spmd

F32 = mybir.dt.float32
BF16 = mybir.dt.bfloat16
AF = mybir.ActivationFunctionType
OP = mybir.AluOpType
AX = mybir.AxisListType.X

CM = 12582912.0
LAM = 0.01
EPS = 1e-5
QMAX = 127.0

ENGINES = ("tensor", "vector", "scalar", "gpsimd", "sync")


class Cfg:
    def __init__(self, B=2, S=2048, D=1024, H=16, HD=64, DL=256, I=2816, n_cores=8):
        self.B, self.S, self.D, self.H, self.HD, self.DL, self.I = B, S, D, H, HD, DL, I
        self.n_cores = n_cores
        self.QT = min(512, S)
        self.DC = D // 128
        self.LC = DL // 128
        self.IT = I // 128
        self.NKT = S // 128
        self.NQT = S // self.QT
        self.NTT = S // 128
        self.PAIRS = H // 2
        self.DMOD = 6 * D
        self.TOK = B * S
        self.NT512 = self.TOK // 512
        self.JW = min(512, D)
        self.JH = D // self.JW
        self.NQSUB = self.QT // 128
        assert S % 128 == 0 and D % 128 == 0 and DL % 128 == 0 and I % 128 == 0
        assert H % 2 == 0 and HD == 64 and S % self.QT == 0 and D % self.JW == 0


class Sem:
    def __init__(self, nc, name):
        self.h = nc.alloc_semaphore(name)
        self.n = 0
        self.emitted = 0
        self.name = name


class QueueCtr:
    R = 4

    def __init__(self, nc, name):
        self.name = name
        self.sems = [Sem(nc, f"{name}{r}") for r in range(self.R)]
        self.n = 0

    def expand(self, ndmas):
        w = []
        for r in range(self.R):
            nr = (ndmas - r + self.R - 1) // self.R if ndmas > r else 0
            if nr > 0:
                w.append((self.sems[r], 16 * nr))
        return w


class EV:
    def __init__(self):
        self.events = []

    def add(self, eng, fn, waits=(), inc=None, k=1):
        flat = []
        for (s, v) in waits:
            v = v if v is not None else s.n
            if isinstance(s, QueueCtr):
                flat.extend(s.expand(v))
            else:
                flat.append((s, v))
        if inc is not None:
            inc.n += k
        self.events.append((eng, tuple(flat), fn, inc, k))

    def add_dma(self, eng, qc, fn, waits=()):
        j = qc.n
        r = j % QueueCtr.R
        s = qc.sems[r]
        self.add(eng, fn, waits=tuple(waits) + ((s, 16 * (j // QueueCtr.R)),),
                 inc=s, k=16)
        qc.n += 1

    SELF_SERIAL = ("vector", "scalar")

    @staticmethod
    def _val_after(sem, k):
        sem.emitted += k
        return sem.emitted

    def emit(self, eng_name, engine):
        hw = {}
        last_inc = None
        for (e, waits, fn, inc, k) in self.events:
            if e != eng_name:
                continue
            allw = list(waits)
            if eng_name in self.SELF_SERIAL and last_inc is not None:
                allw.append(last_inc)
            for (s, v) in allw:
                if v > 0 and hw.get(s.name, -1) < v:
                    engine.wait_ge(s.h, v)
                    hw[s.name] = v
            instr = fn(engine)
            if inc is not None:
                instr.then_inc(inc.h, k)
                if eng_name in self.SELF_SERIAL:
                    last_inc = (inc, self._val_after(inc, k))


def build_program(cfg: Cfg):
    nc = bass.Bass('TRN2', target_bir_lowering=False)
    B, S, D, H, HD, DL, I = cfg.B, cfg.S, cfg.D, cfg.H, cfg.HD, cfg.DL, cfg.I
    QT, DC, LC, IT = cfg.QT, cfg.DC, cfg.LC, cfg.IT
    NKT, NQT, NTT, PAIRS = cfg.NKT, cfg.NQT, cfg.NTT, cfg.PAIRS
    TOK, NT512, JH, JW, NQSUB = cfg.TOK, cfg.NT512, cfg.JH, cfg.JW, cfg.NQSUB
    DMOD = cfg.DMOD
    NCORES = cfg.n_cores
    PW = 193
    KW = PAIRS * PW

    xd = nc.dram_tensor("x", [B, S, D], F32, kind="ExternalInput")
    wq_d = nc.dram_tensor("wq", [D, D], F32, kind="ExternalInput")
    wk_d = nc.dram_tensor("wk", [D, D], F32, kind="ExternalInput")
    wv_d = nc.dram_tensor("wv", [D, D], F32, kind="ExternalInput")
    wo_d = nc.dram_tensor("wo", [D, D], F32, kind="ExternalInput")
    wdn_d = nc.dram_tensor("wdn", [D, DL], F32, kind="ExternalInput")
    wuk_d = nc.dram_tensor("wuk", [DL, D], F32, kind="ExternalInput")
    wuv_d = nc.dram_tensor("wuv", [DL, D], F32, kind="ExternalInput")
    wmod_d = nc.dram_tensor("wmod", [D, DMOD], F32, kind="ExternalInput")
    cmod_d = nc.dram_tensor("cmod", [B, D], F32, kind="ExternalInput")
    clip_d = nc.dram_tensor("clip2", [1, 2], F32, kind="ExternalInput")
    wg_d = nc.dram_tensor("wg", [I, D], F32, kind="ExternalInput")
    wu_d = nc.dram_tensor("wu", [I, D], F32, kind="ExternalInput")
    wdw_d = nc.dram_tensor("wdw", [D, I], F32, kind="ExternalInput")
    id_d = nc.dram_tensor("idm", [128, 128], F32, kind="ExternalInput")
    fdn_d = nc.dram_tensor("fdn", [128, S], F32, kind="ExternalInput")
    fup_d = nc.dram_tensor("fup", [128, S], F32, kind="ExternalInput")
    gdn_d = nc.dram_tensor("gdn", [128, S], F32, kind="ExternalInput")
    gup_d = nc.dram_tensor("gup", [128, S], F32, kind="ExternalInput")
    rdiag_d = nc.dram_tensor("rdiag", [128, 128], F32, kind="ExternalInput")
    ones_d = nc.dram_tensor("onesr", [1, 128], F32, kind="ExternalInput")
    konst_d = nc.dram_tensor("konst", [128, 8], F32, kind="ExternalInput")
    outd = nc.dram_tensor("out", [B, S, D], F32, kind="ExternalOutput")

    x1_dram = nc.dram_tensor("x1_scr", [TOK, D], BF16)
    h2_dram = nc.dram_tensor("h2_scr", [TOK, D], F32)
    hpre_dram = nc.dram_tensor("hpre_scr", [I, TOK], F32)
    qwgT_dram = nc.dram_tensor("qwgT_scr", [D, I], BF16)
    qwuT_dram = nc.dram_tensor("qwuT_scr", [D, I], BF16)
    qwdT_dram = nc.dram_tensor("qwdT_scr", [I, D], BF16)
    silc_dram = nc.dram_tensor("silc_scr", [B, D], BF16)
    mod_dram = nc.dram_tensor("mod_scr", [B, DMOD], F32)
    qdn_dram = nc.dram_tensor("qdn_scr", [DC * 128, S], BF16)
    qup_dram = nc.dram_tensor("qup_scr", [DC * 128, S], BF16)
    kdn_dram = nc.dram_tensor("kdn_scr", [DC * 128, S], BF16)
    kup_dram = nc.dram_tensor("kup_scr", [DC * 128, S], BF16)
    vaug_dram = nc.dram_tensor("vaug_scr", [NKT * 128, H * HD], BF16)
    pmax_dram = nc.dram_tensor("pmax_scr", [4, 128], F32)
    lrow_dram = nc.dram_tensor("lrow_scr", [2 * PAIRS, QT], F32)
    rrow_dram = nc.dram_tensor("rrow_scr", [2 * PAIRS, QT], F32)
    sd_dram = nc.dram_tensor("sd_scr", [DC, 128], F32)
    ag1_in = nc.dram_tensor("ag1_in", [1, 8], F32)
    ag1_out = nc.dram_tensor("ag1_out", [NCORES, 8], F32)
    ag2_in = nc.dram_tensor("ag2_in", [1, 8], F32)
    ag2_out = nc.dram_tensor("ag2_out", [NCORES, 8], F32)

    esg = ExitStack()

    def sb(name, shape, dt, stack=None):
        return (stack or esg).enter_context(nc.sbuf_tensor(name, shape, dt))

    konst = sb("konstt", [128, 8], F32)
    idmf = sb("idmf", [128, 128], F32)
    onesr = sb("onesrt", [1, 128], F32)
    idbf = sb("idbft", [128, 128], BF16)
    wqs = sb("wqs", [128, DC * D], BF16)
    wcomb = sb("wcomb", [128, DC * DL], BF16)
    wuks = sb("wuks", [128, LC * D], BF16)
    wuvs = sb("wuvs", [128, LC * D], BF16)
    wos = sb("wos", [128, DC * D], BF16)
    fdn = sb("fdns", [128, S], BF16)
    fup = sb("fups", [128, S], BF16)
    gdn = sb("gdns", [128, S], BF16)
    gup = sb("gups", [128, S], BF16)
    rdiag = sb("rdiagt", [128, 128], F32)
    sg_cols = sb("sg_cols", [128, IT], F32)
    su_cols = sb("su_cols", [128, IT], F32)
    sd_cols = sb("sd_cols", [128, DC], F32)
    sgx_cols = sb("sgx_cols", [128, IT], F32)
    q2s_cols = sb("q2s_cols", [128, IT], F32)
    negM = sb("negM", [128, 1], F32)
    sx_col = sb("sx_col", [128, 1], F32)
    sxr_col = sb("sxr_col", [128, 1], F32)
    srow = sb("srow", [1, 16], F32)
    rowbuf = sb("rowbuf", [1, max(512, D, 2 * QT)], F32)
    clip_sb = sb("clip_sb", [1, 2], F32)
    gmax_cols = sb("gmax_cols", [128, 2], F32)
    mq_cols = sb("mq_cols", [128, 4], F32)
    hmax_cols = sb("hmax_cols", [128, IT], F32)
    agrow = sb("agrow", [1, max(128, 8 * NCORES)], F32)

    SG = QueueCtr(nc, "Qg")
    SS = QueueCtr(nc, "Qs")
    ST = QueueCtr(nc, "Qt")
    SP, SV, SC, SCC = (Sem(nc, "S" + n) for n in ["p", "v", "c", "cc"])
    SST = SS
    allsems = [SP, SV, SC, SCC] + SG.sems + SS.sems + ST.sems

    def run_phase(ev, psum_stack=None):
        snap = [(s, s.n) for s in allsems if s.n > 0]
        for e in ENGINES:
            ev.add(e, lambda eng: eng.nop(), waits=snap)
        with nc.Block() as blk:
            blk.tensor(lambda e: ev.emit("tensor", e))
            blk.vector(lambda e: ev.emit("vector", e))
            blk.scalar(lambda e: ev.emit("scalar", e))
            blk.gpsimd(lambda e: ev.emit("gpsimd", e))
            blk.sync(lambda e: ev.emit("sync", e))
        if psum_stack is not None:
            psum_stack.close()

    def dma(ev, eng, qc, out, in_, waits=()):
        ev.add_dma(eng, qc, lambda e, o=out, i=in_: e.dma_start(out=o, in_=i), waits=waits)

    def dma_t(ev, eng, qc, out, in_, waits=()):
        ev.add_dma(eng, qc, lambda e, o=out, i=in_: e.dma_start_transpose(o, i), waits=waits)

    esA = ExitStack()
    psA_ = ExitStack()
    wmj = sb("wmj", [128, 2 * DC * 512], BF16, esA)
    cmod_sb = sb("cmod_sb", [B, D], F32, esA)
    silc = sb("silc", [B, D], BF16, esA)
    silcT = sb("silcT", [128, DC * B], BF16, esA)
    grow_t = sb("grow_t", [1, 3 * D], F32, esA)
    wvtmp = sb("wvtmp", [128, DC * D], BF16, esA)
    wkv = sb("wkv", [128, DC * D], BF16, esA)
    wkvT = sb("wkvT", [128, DC * D], BF16, esA)
    wdnb = sb("wdnb", [128, DC * DL], BF16, esA)

    pmod = psA_.enter_context(nc.psum_tensor("pmod", [128, 512], F32))
    ptr = [psA_.enter_context(nc.psum_tensor(f"ptr{i}", [128, 128], BF16)) for i in range(4)]
    pcmb = psA_.enter_context(nc.psum_tensor("pcmb", [128, 512], F32))

    ev = EV()
    for (dst, src) in ((konst, konst_d), (onesr, ones_d), (rdiag, rdiag_d), (clip_sb, clip_d)):
        dma(ev, "sync", SS, dst[:, :], src.ap())
    dma(ev, "sync", SS, cmod_sb[:, :], cmod_d.ap())
    s_basic = SS.n
    dma(ev, "gpsimd", SG, idbf[:, :], id_d.ap())
    dma(ev, "gpsimd", SG, idmf[:, :], id_d.ap())
    for (dst, src) in ((fdn, fdn_d), (fup, fup_d), (gdn, gdn_d), (gup, gup_d)):
        dma(ev, "gpsimd", SG, dst[:, :], src.ap())
    for c in range(DC):
        dma(ev, "gpsimd", SG, wqs[:, c * D:(c + 1) * D], wq_d.ap()[c * 128:(c + 1) * 128, :])
        dma(ev, "gpsimd", SG, wkv[:, c * D:(c + 1) * D], wk_d.ap()[c * 128:(c + 1) * 128, :])
        dma(ev, "gpsimd", SG, wvtmp[:, c * D:(c + 1) * D], wv_d.ap()[c * 128:(c + 1) * 128, :])
        dma(ev, "gpsimd", SG, wdnb[:, c * DL:(c + 1) * DL], wdn_d.ap()[c * 128:(c + 1) * 128, :])
        dma(ev, "gpsimd", SG, wos[:, c * D:(c + 1) * D], wo_d.ap()[c * 128:(c + 1) * 128, :])
    for c in range(LC):
        dma(ev, "gpsimd", SG, wuks[:, c * D:(c + 1) * D], wuk_d.ap()[c * 128:(c + 1) * 128, :])
        dma(ev, "gpsimd", SG, wuvs[:, c * D:(c + 1) * D], wuv_d.ap()[c * 128:(c + 1) * 128, :])
    g_loads = SG.n

    ev.add("scalar", lambda e: e.activation(silc[:, :], cmod_sb[:, :], AF.Silu),
           waits=[(SS, s_basic)], inc=SC)
    dma(ev, "sync", SS, silc_dram.ap(), silc[:, :], waits=[(SC, None)])
    for c in range(DC):
        for b in range(B):
            dma(ev, "sync", SS, silcT[:, c * B + b:c * B + b + 1],
                silc_dram.ap()[b, c * 128:(c + 1) * 128].rearrange("(p a) -> p a", a=1),
                waits=[(SS, None)])
    silcT_ld = SS.n
    for c in range(DC):
        ev.add("vector", lambda e, c=c: e.tensor_add(
            wkv[:, c * D:(c + 1) * D], wkv[:, c * D:(c + 1) * D], wvtmp[:, c * D:(c + 1) * D]),
            waits=[(SG, g_loads)], inc=SV)
    wkv_done_v = SV.n
    NJ = DMOD // 512
    wm_marks = {}
    for j in range(NJ):
        slot = j % 2
        w = [(SP, wm_marks[j - 2])] if j >= 2 else []
        for c in range(DC):
            dma(ev, "gpsimd", SG, wmj[:, slot * DC * 512 + c * 512: slot * DC * 512 + (c + 1) * 512],
                wmod_d.ap()[c * 128:(c + 1) * 128, j * 512:(j + 1) * 512], waits=w if c == 0 else [])
        for b in range(B):
            for c in range(DC):
                ev.add("tensor", lambda e, b=b, j=j, c=c, slot=slot: e.matmul(
                    pmod[0:1, 0:512], silcT[:, c * B + b: c * B + b + 1],
                    wmj[:, slot * DC * 512 + c * 512: slot * DC * 512 + (c + 1) * 512],
                    start=(c == 0), stop=(c == DC - 1)),
                    waits=([(SS, silcT_ld), (SG, None), (SV, None)] if c == 0 else []),
                    inc=(SP if c == DC - 1 else None))
            ev.add("vector", lambda e: e.tensor_copy(rowbuf[0:1, 0:512], pmod[0:1, 0:512]),
                   waits=[(SP, None), (SS, None)], inc=SV)
            dma(ev, "sync", SS, mod_dram.ap()[b, j * 512:(j + 1) * 512].rearrange("(a c) -> a c", a=1),
                rowbuf[0:1, 0:512], waits=[(SV, None)])
        wm_marks[j] = SP.n
    mod_rows_done = SS.n
    for b in range(B):
        for gofs in (2 * D, 5 * D):
            dma(ev, "sync", SS, grow_t[0:1, 2 * D:3 * D],
                mod_dram.ap()[b, gofs:gofs + D].rearrange("(a c) -> a c", a=1),
                waits=[(SS, mod_rows_done), (SV, None)])
            src_row = grow_t[0:1, 2 * D:3 * D]
            ev.add("vector", lambda e, src_row=src_row: e.tensor_scalar_max(
                grow_t[0:1, 0:D], src_row, 0.25), waits=[(SS, None)], inc=SV)
            ev.add("vector", lambda e, src_row=src_row: e.tensor_scalar_min(
                grow_t[0:1, D:2 * D], src_row, -0.25), inc=SV)
            ev.add("vector", lambda e, src_row=src_row: e.tensor_single_scalar(
                grow_t[0:1, 2 * D:3 * D], src_row, 0.0, OP.is_ge), inc=SV)
            ev.add("vector", lambda e: e.tensor_sub(grow_t[0:1, 0:D], grow_t[0:1, 0:D], grow_t[0:1, D:2 * D]), inc=SV)
            ev.add("vector", lambda e: e.tensor_mul(grow_t[0:1, 0:D], grow_t[0:1, 0:D], grow_t[0:1, 2 * D:3 * D]), inc=SV)
            ev.add("vector", lambda e: e.tensor_add(grow_t[0:1, 0:D], grow_t[0:1, D:2 * D], grow_t[0:1, 0:D]), inc=SV)
            dma(ev, "sync", SS, mod_dram.ap()[b, gofs:gofs + D].rearrange("(a c) -> a c", a=1),
                grow_t[0:1, 0:D], waits=[(SV, None)])
    for c in range(DC):
        for cc in range(DC):
            idx = c * DC + cc
            pt = ptr[idx % 4]
            ev.add("tensor", lambda e, c=c, cc=cc, pt=pt: e.transpose(
                pt[:, :], wkv[:, c * D + cc * 128: c * D + (cc + 1) * 128], idbf[:, :]),
                waits=[(SV, wkv_done_v), (SG, None), (SV, None)], inc=SP)
            ev.add("vector", lambda e, c=c, cc=cc, pt=pt: e.tensor_copy(
                wkvT[:, cc * D + c * 128: cc * D + (c + 1) * 128], pt[:, :]),
                waits=[(SP, None)], inc=SV)
    for dt in range(DC):
        for ic in range(DC):
            ev.add("tensor", lambda e, dt=dt, ic=ic: e.matmul(
                pcmb[:, 0:DL], wkvT[:, ic * D + dt * 128: ic * D + (dt + 1) * 128],
                wdnb[:, ic * DL:(ic + 1) * DL], start=(ic == 0), stop=(ic == DC - 1)),
                waits=[(SV, None)] if ic == 0 else [], inc=(SP if ic == DC - 1 else None))
        ev.add("vector", lambda e, dt=dt: e.tensor_copy(wcomb[:, dt * DL:(dt + 1) * DL], pcmb[:, 0:DL]),
               waits=[(SP, None)], inc=SV)
    run_phase(ev)
    esA.close()
    psA_.close()

    esA2 = ExitStack()
    psA2_ = ExitStack()
    WMAXF = max(D, I)
    wfp = sb("wfp", [128, 3 * WMAXF], F32, esA2)
    tq_st = sb("tq_st", [128, 2 * WMAXF], F32, esA2)
    wint = sb("wint", [128, WMAXF], BF16, esA2)
    qT_stage = sb("qT_stage", [128, DC * I], BF16, esA2)
    acols = sb("acols", [128, 4], F32, esA2)
    ptr2 = [psA2_.enter_context(nc.psum_tensor(f"pt2_{i}", [128, 1024], BF16)) for i in range(2)]

    ev = EV()
    qtiles = []

    def quant_weight(src_d, nrows, ncols, scol, qT_dram):
        ntiles = nrows // 128
        for it in range(ntiles):
            qtiles.append((src_d, it, nrows, ncols, scol, qT_dram,
                           it == 0, it == ntiles - 1))

    def emit_quant_all():
        qgrp = [0]
        gev = {}
        gtp = {}
        id_mark = {}
        vq_mark = {}

        def head(gt):
            (src_d, it, nrows, ncols, scol, qT_dram, first, last) = qtiles[gt]
            buf = gt % 3
            par = (gt % 2) * 2
            dma(ev, "gpsimd", SG, wfp[:, buf * WMAXF: buf * WMAXF + ncols],
                src_d.ap()[it * 128:(it + 1) * 128, :],
                waits=[(SP, gtp[gt - 3])] if gt >= 3 else [])
            ev.add("vector", lambda e, buf=buf, ncols=ncols, par=par: e.reduce_max(
                acols[:, par:par + 1], wfp[:, buf * WMAXF: buf * WMAXF + ncols], axis=AX,
                apply_absolute_value=True),
                waits=[(SG, None), (SC, id_mark.get(gt - 2, 0))], inc=SV)
            ev.add("vector", lambda e, par=par: e.reciprocal(
                acols[:, par + 1:par + 2], acols[:, par:par + 1]), inc=SV)
            ev.add("vector", lambda e, par=par: e.tensor_scalar_mul(
                acols[:, par + 1:par + 2], acols[:, par + 1:par + 2], QMAX), inc=SV)
            ev.add("vector", lambda e, scol=scol, it=it, par=par: e.tensor_scalar_mul(
                scol[:, it:it + 1], acols[:, par:par + 1], 1.0 / QMAX), inc=SV)
            ev.add("scalar", lambda e, buf=buf, ncols=ncols, par=par, gt=gt: e.activation(
                tq_st[:, (gt % 2) * WMAXF:(gt % 2) * WMAXF + ncols],
                wfp[:, buf * WMAXF: buf * WMAXF + ncols], AF.Identity,
                bias=konst[:, 0:1], scale=acols[:, par + 1:par + 2]),
                waits=[(SV, None)], inc=SC)
            id_mark[gt] = SC.n

        def tail(gt):
            (src_d, it, nrows, ncols, scol, qT_dram, first, last) = qtiles[gt]
            nblk = ncols // 128
            ev.add("vector", lambda e, ncols=ncols, gt=gt: e.tensor_scalar(
                wint[:, 0:ncols], tq_st[:, (gt % 2) * WMAXF:(gt % 2) * WMAXF + ncols],
                CM, None, OP.subtract),
                waits=[(SC, id_mark[gt]), (SP, gtp.get(gt - 1, 0))], inc=SV)
            vq = SV.n
            done = 0
            while done < nblk:
                g = qgrp[0]
                qgrp[0] += 1
                nb_ = min(8, nblk - done)
                pt = ptr2[g % 2]
                for j in range(nb_):
                    bkt = done + j
                    ev.add("tensor", lambda e, bkt=bkt, j=j, pt=pt: e.transpose(
                        pt[:, j * 128:(j + 1) * 128], wint[:, bkt * 128:(bkt + 1) * 128], idbf[:, :]),
                        waits=([(SV, vq), (SV, gev.get(g - 2, 0))] if j == 0 else []),
                        inc=(SP if j == nb_ - 1 else None))
                dstap = bass.AP(qT_stage[:, 0:1].tensor, done * nrows + it * 128,
                                [[DC * I, 128], [nrows, nb_], [1, 128]])
                evw2 = [(SP, None)] + ([(SS, None)] if (first and done == 0) else [])
                ev.add("vector", lambda e, dstap=dstap, pt=pt, nb_=nb_: e.tensor_copy(
                    dstap, pt[:, 0:nb_ * 128]), waits=evw2, inc=SV)
                gev[g] = SV.n
                done += nb_
            gtp[gt] = SP.n
            if last:
                for bkt in range(nblk):
                    dma(ev, "sync", SS, qT_dram.ap()[bkt * 128:(bkt + 1) * 128, :],
                        qT_stage[:, bkt * nrows:(bkt + 1) * nrows], waits=[(SV, None)])

        for gt in range(len(qtiles)):
            head(gt)
            if gt >= 1:
                tail(gt - 1)
        tail(len(qtiles) - 1)

    quant_weight(wg_d, I, D, sg_cols, qwgT_dram)
    quant_weight(wu_d, I, D, su_cols, qwuT_dram)
    quant_weight(wdw_d, D, I, sd_cols, qwdT_dram)
    emit_quant_all()
    for c in range(DC):
        dma(ev, "sync", SS, sd_dram.ap()[c, :].rearrange("(p a) -> p a", a=1),
            sd_cols[:, c:c + 1], waits=[(SV, None)])
    run_phase(ev)
    esA2.close()
    psA2_.close()

    esB = ExitStack()
    h1T = sb("h1T", [128, 2 * DC * 512], BF16, esB)
    xbt = sb("xbt", [128, 4 * D], BF16, esB)
    h2f = sb("h2f", [128, 2 * D], F32, esB)
    st_sum = sb("st_sum", [128, NTT], F32, esB)
    st_sq = sb("st_sq", [128, NTT], F32, esB)
    st_tmp = sb("st_tmp", [128, NTT], F32, esB)
    st_negm = sb("st_negm", [128, NTT], F32, esB)
    st_inv = sb("st_inv", [128, NTT], F32, esB)
    car = sb("car", [128, D], BF16, esB)
    sar = sb("sar", [128, D], BF16, esB)
    cbr = sb("cbr", [128, D], BF16, esB)
    sbr = sb("sbr", [128, D], BF16, esB)
    gar = sb("gar", [128, D], BF16, esB)
    qstage = sb("qstage", [128, DC * 512], BF16, esB)
    kstage = sb("kstage", [128, DC * 512], BF16, esB)
    latr = sb("latr", [128, 2 * LC * 512], BF16, esB)
    vstage = sb("vstage", [128, 2 * JW], BF16, esB)
    qdnwin = sb("qdnwin", [128, DC * 512], BF16, esB)
    qupwin = sb("qupwin", [128, DC * 512], BF16, esB)
    VW = NKT * 193
    vwin = sb("vwin", [128, VW], BF16, esB)
    kdn_c = sb("kdn_c", [128, 2 * S], BF16, esB)
    kup_c = sb("kup_c", [128, 2 * S], BF16, esB)
    PAb = sb("PAb", [128, 2 * 2 * QT], BF16, esB)
    PBb = sb("PBb", [128, 2 * 2 * QT], BF16, esB)
    attnraw = sb("attnraw", [128, PAIRS * QT], BF16, esB)
    lrows = sb("lrows", [128, QT], F32, esB)
    sqb = sb("sqb", [128, 2 * 512], BF16, esB)
    lst = None
    rip = lrows
    rbuf = sb("rbuf", [128, 2 * QT], F32, esB)
    lst = rbuf
    x1t = sb("x1t", [128, 2 * D], BF16, esB)
    ot_t = None
    qsq = h2f
    ot_t = h2f[:, D:2 * D]

    psB_ = ExitStack()
    psS = [psB_.enter_context(nc.psum_tensor(f"psS{i}", [128, 1024], F32)) for i in range(3)]
    psS_A, psS_B = psS[0], psS[1]
    pv2 = [psB_.enter_context(nc.psum_tensor(f"pv2_{i}", [128, 512], F32)) for i in range(2)]
    pvA, pvB = pv2[0], pv2[1]
    prc = [psS[2][:, 0:512], psS[2][:, 512:1024]]

    for b in range(B):
        ev = EV()
        brow = b * S

        NB = D // JW
        for (dst, ofs, p1) in ((car, 1 * D, True), (sar, 0 * D, False), (cbr, 4 * D, True),
                               (sbr, 3 * D, False), (gar, 2 * D, False)):
            for j in range(NB):
                dma(ev, "sync", SS, rowbuf[0:1, 0:JW],
                    mod_dram.ap()[b, ofs + j * JW: ofs + (j + 1) * JW].rearrange("(a c) -> a c", a=1),
                    waits=[(SP, None)])
                ev.add("tensor", lambda e: e.matmul(
                    prc[0][:, 0:JW], onesr[0:1, 0:128], rowbuf[0:1, 0:JW],
                    start=True, stop=True), waits=[(SS, None), (SV, None)], inc=SP)
                ev.add("vector", (lambda e, dst=dst, j=j: e.tensor_scalar(
                    dst[:, j * JW:(j + 1) * JW], prc[0][:, 0:JW], 1.0, None, OP.add)) if p1 else
                    (lambda e, dst=dst, j=j: e.tensor_copy(
                        dst[:, j * JW:(j + 1) * JW], prc[0][:, 0:JW])),
                    waits=[(SP, None)], inc=SV)
        ev.add("vector", lambda e: e.memset(
            bass.AP(vwin[:, 0:1].tensor, 64, [[VW, 128], [193, NKT], [1, 2]]), 1.0), inc=SV)
        ev.add("vector", lambda e: e.memset(
            bass.AP(vwin[:, 0:1].tensor, 66, [[VW, 128], [193, NKT], [1, 63]]), 0.0), inc=SV)

        ln_marks = {}
        for tt in range(NTT):
            buf = tt % 4
            w = []
            if tt >= 4:
                w = [(SV, ln_marks[tt - 4][0]), (SC, ln_marks[tt - 4][1])]
            dma(ev, "gpsimd", SG, xbt[:, buf * D:(buf + 1) * D],
                xd.ap()[b, tt * 128:(tt + 1) * 128, :], waits=w)
            ev.add("vector", lambda e, tt=tt, buf=buf: e.reduce_sum(
                st_sum[:, tt:tt + 1], xbt[:, buf * D:(buf + 1) * D], axis=AX),
                waits=[(SG, None)], inc=SV)
            ev.add("scalar", lambda e, tt=tt, buf=buf: e.activation(
                h2f[:, 0:D], xbt[:, buf * D:(buf + 1) * D], AF.Square,
                accum_out=st_sq[:, tt:tt + 1]), waits=[(SG, None), (SV, None)], inc=SC)
            ln_marks[tt] = (SV.n, SC.n)
        ev.add("vector", lambda e: e.tensor_scalar_mul(st_negm[:, 0:NTT], st_sum[:, 0:NTT], -1.0 / D),
               waits=[(SC, None)], inc=SV)
        ev.add("vector", lambda e: e.tensor_mul(st_tmp[:, 0:NTT], st_negm[:, 0:NTT], st_negm[:, 0:NTT]), inc=SV)
        ev.add("vector", lambda e: e.scalar_tensor_tensor(
            st_tmp[:, 0:NTT], st_sq[:, 0:NTT], 1.0 / D, st_tmp[:, 0:NTT], OP.mult, OP.subtract), inc=SV)
        ev.add("scalar", lambda e: e.activation(st_tmp[:, 0:NTT], st_tmp[:, 0:NTT], AF.Sqrt,
                                                bias=konst[:, 1:2], scale=1.0),
               waits=[(SV, None)], inc=SC)
        ev.add("vector", lambda e: e.reciprocal(st_inv[:, 0:NTT], st_tmp[:, 0:NTT]),
               waits=[(SC, None)], inc=SV)
        stats1 = SV.n

        NS = S // 512
        ts_marks = {}
        ap_marks = {}
        nrm_first = [True, True]
        nbm, nbx, nbi = {}, {}, [0]
        onesc_bf = vwin[:, 64:65]

        def norm_bound(stage, scol, nw):
            for dt in range(DC):
                gi = nbi[0]
                nbi[0] += 1
                sq = sqb[:, (gi % 2) * 512:(gi % 2 + 1) * 512]
                pr = prc[gi % 2]
                ev.add("scalar", lambda e, dt=dt, stage=stage, sq=sq: e.activation(
                    sq, stage[:, dt * 512:(dt + 1) * 512], AF.Square),
                    waits=[(SV, nw), (SP, nbm.get(gi - 2, 0))], inc=SC)
                ev.add("tensor", lambda e, sq=sq, pr=pr: e.matmul(
                    pr[0:1, 0:512], onesc_bf, sq,
                    start=True, stop=True),
                    waits=[(SC, None), (SV, nbx.get(gi - 2, 0))], inc=SP)
                nbm[gi] = SP.n
                ev.add("vector", lambda e, pr=pr: e.reduce_max(srow[0:1, 5:6], pr[0:1, 0:512], axis=AX),
                       waits=[(SP, None)], inc=SV)
                first = nrm_first[scol]
                ev.add("vector", (lambda e, scol=scol: e.tensor_copy(
                    srow[0:1, scol:scol + 1], srow[0:1, 5:6])) if first else
                    (lambda e, scol=scol: e.tensor_max(
                        srow[0:1, scol:scol + 1], srow[0:1, scol:scol + 1], srow[0:1, 5:6])), inc=SV)
                nbx[gi] = SV.n
                nrm_first[scol] = False

        b2m = {}
        trm = {}
        tr_ev = {}
        for ts in range(NS):
            slot = ts % 2
            t0 = ts * 512
            for tq in range(4):
                tt = ts * 4 + tq
                buf = tt % 4
                w = [(SV, ap_marks[tt - 4])] if tt >= 4 else [(SV, stats1)]
                dma(ev, "gpsimd", SG, xbt[:, buf * D:(buf + 1) * D],
                    xd.ap()[b, tt * 128:(tt + 1) * 128, :], waits=w)
                rs = h2f[:, D:2 * D] if tt % 2 == 0 else rbuf[:, 0:D]
                w2 = [(SG, None), (SV, stats1)]
                ev.add("vector", lambda e, tt=tt, buf=buf: e.tensor_scalar(
                    h2f[:, 0:D], xbt[:, buf * D:(buf + 1) * D],
                    st_negm[:, tt:tt + 1], st_inv[:, tt:tt + 1], OP.add, OP.mult),
                    waits=w2, inc=SV)
                ev.add("vector", lambda e, rs=rs: e.tensor_mul(rs, h2f[:, 0:D], car[:, 0:D]),
                       waits=[(SP, trm.get(tt - 2, 0))], inc=SV)
                ev.add("vector", lambda e, rs=rs: e.tensor_add(rs, rs, sar[:, 0:D]), inc=SV)
                ap_marks[tt] = SV.n
                wtr = ([(SP, ts_marks[ts - 2])] if ts >= 2 else [])
                NGRP = (DC + 3) // 4
                for grp in range(NGRP):
                    nbk = min(4, DC - grp * 4)
                    g2 = tt * NGRP + grp
                    pvt = pv2[g2 % 2]
                    for j in range(nbk):
                        c = grp * 4 + j
                        ev.add("tensor", lambda e, j=j, c=c, pvt=pvt, rs=rs: e.transpose(
                            pvt[:, j * 128:(j + 1) * 128], rs[:, c * 128:(c + 1) * 128], idmf[:, :]),
                            waits=([(SV, None), (SV, tr_ev.get(g2 - 2, 0))] + wtr) if j == 0 else [],
                            inc=(SP if j == nbk - 1 else None))
                    dstap = bass.AP(h1T[:, 0:1].tensor, slot * DC * 512 + grp * 4 * 512 + tq * 128,
                                    [[2 * DC * 512, 128], [512, nbk], [1, 128]])
                    ev.add("vector", lambda e, dstap=dstap, pvt=pvt, nbk=nbk: e.tensor_copy(
                        dstap, pvt[:, 0:nbk * 128]), waits=[(SP, None)], inc=SV)
                    tr_ev[g2] = SV.n
                trm[tt] = SP.n
            h1T_ts = SV.n
            psum_last = {0: None, 1: None, 2: None, 3: None}
            pquads = [psS[0][:, 0:512], psS[0][:, 512:1024],
                      psS[1][:, 0:512], psS[1][:, 512:1024]]
            acc_gi = [0]

            def acc_proj(groups, nchunks, mm_fn, evac_fn, first_waits, evw=()):
                for gkey in groups:
                    par = acc_gi[0] % 4
                    acc_gi[0] += 1
                    pj = pquads[par]
                    for c in range(nchunks):
                        w3 = []
                        if c == 0:
                            if psum_last[par] is not None:
                                w3.append((SV, psum_last[par]))
                            w3 += first_waits
                        ev.add("tensor", lambda e, gkey=gkey, c=c, pj=pj: mm_fn(e, gkey, c, pj),
                               waits=w3, inc=(SP if c == nchunks - 1 else None))
                    ev.add("vector", lambda e, gkey=gkey, pj=pj: evac_fn(e, gkey, pj),
                           waits=[(SP, None)] + list(evw), inc=SV)
                    psum_last[par] = SV.n

            fw = [(SV, h1T_ts)]
            sof = slot * DC * 512
            acc_proj(list(range(DC)), DC,
                     lambda e, dt, c, pj, sof=sof: e.matmul(
                         pj[:, 0:512], wqs[:, c * D + dt * 128:c * D + (dt + 1) * 128],
                         h1T[:, sof + c * 512: sof + (c + 1) * 512],
                         start=(c == 0), stop=(c == DC - 1)),
                     lambda e, dt, pj: e.tensor_copy(
                         qstage[:, dt * 512:(dt + 1) * 512], pj[:, 0:512]), fw,
                     evw=[(SS, b2m.get(('k', ts - 1), 0)), (SC, b2m.get(('nq', ts - 1), 0))])
            q_ev = SV.n
            for (fr, qd_) in ((fdn, qdn_dram), (fup, qup_dram)):
                ev.add("vector", lambda e, fr=fr, t0=t0: e.tensor_tensor(
                    kstage[:, 0:DC * 512], qstage[:, 0:DC * 512],
                    bass.AP(fr[:, 0:1].tensor, t0, [[S, 128], [0, DC], [1, 512]]),
                    OP.mult), waits=[(SV, q_ev), (SS, None), (SC, b2m.get(('nk', ts - 1), 0))], inc=SV)
                dma(ev, "sync", SS,
                    qd_.ap().rearrange("(c p) s -> p c s", p=128)[:, :, t0:t0 + 512],
                    kstage[:, 0:DC * 512], waits=[(SV, None)])
            b2m[('q', ts)] = SS.n
            norm_bound(qstage, 0, q_ev)
            b2m[('nq', ts)] = SC.n
            acc_proj(list(range(LC)), DC,
                     lambda e, lt, c, pj, sof=sof: e.matmul(
                         pj[:, 0:512], wcomb[:, c * DL + lt * 128:c * DL + (lt + 1) * 128],
                         h1T[:, sof + c * 512: sof + (c + 1) * 512],
                         start=(c == 0), stop=(c == DC - 1)),
                     lambda e, lt, pj, slot=slot: e.tensor_copy(
                         latr[:, slot * LC * 512 + lt * 512: slot * LC * 512 + (lt + 1) * 512],
                         pj[:, 0:512]), [])
            acc_proj(list(range(DC)), LC,
                     lambda e, dt, c, pj, slot=slot: e.matmul(
                         pj[:, 0:512], wuks[:, c * D + dt * 128:c * D + (dt + 1) * 128],
                         latr[:, slot * LC * 512 + c * 512: slot * LC * 512 + (c + 1) * 512],
                         start=(c == 0), stop=(c == LC - 1)),
                     lambda e, dt, pj: e.tensor_copy(
                         kstage[:, dt * 512:(dt + 1) * 512], pj[:, 0:512]), [(SV, None)],
                     evw=[(SS, b2m[('q', ts)]), (SC, b2m.get(('nk', ts - 1), 0))])
            k_ev = SV.n
            for (gr, kd_) in ((gdn, kdn_dram), (gup, kup_dram)):
                ev.add("vector", lambda e, gr=gr, t0=t0: e.tensor_tensor(
                    qstage[:, 0:DC * 512], kstage[:, 0:DC * 512],
                    bass.AP(gr[:, 0:1].tensor, t0, [[S, 128], [0, DC], [1, 512]]),
                    OP.mult), waits=[(SV, k_ev), (SS, None), (SP, None), (SC, None)], inc=SV)
                dma(ev, "sync", SS,
                    kd_.ap().rearrange("(c p) s -> p c s", p=128)[:, :, t0:t0 + 512],
                    qstage[:, 0:DC * 512], waits=[(SV, None)])
            b2m[('k', ts)] = SS.n
            norm_bound(kstage, 1, k_ev)
            b2m[('nk', ts)] = SC.n
            for tq in range(4):
                kt = ts * 4 + tq
                for jh in range(JH):
                    vb = (tq * JH + jh) % 2
                    acc_proj([(tq, jh)], LC,
                             lambda e, k_, c, pj, slot=slot: e.matmul(
                                 pj[:, 0:JW],
                                 latr[:, slot * LC * 512 + c * 512 + k_[0] * 128:
                                      slot * LC * 512 + c * 512 + k_[0] * 128 + 128],
                                 wuvs[:, c * D + k_[1] * JW:c * D + (k_[1] + 1) * JW],
                                 start=(c == 0), stop=(c == LC - 1)),
                             lambda e, k_, pj, vb=vb: e.tensor_copy(
                                 vstage[:, vb * JW:(vb + 1) * JW], pj[:, 0:JW]),
                             [], evw=[(SS, b2m.get(('v', vb), 0))])
                    dma(ev, "sync", SS,
                        vaug_dram.ap().rearrange("(k p) hd -> p k hd", p=128)[:, kt, jh * JW:(jh + 1) * JW],
                        vstage[:, vb * JW:(vb + 1) * JW], waits=[(SV, None)])
                    b2m[('v', vb)] = SS.n
            ts_marks[ts] = SP.n
        ev.add("vector", lambda e: e.tensor_mul(srow[0:1, 2:3], srow[0:1, 0:1], srow[0:1, 1:2]), inc=SV)
        ev.add("scalar", lambda e: e.activation(srow[0:1, 2:3], srow[0:1, 2:3], AF.Sqrt,
                                                bias=konst[:, 1:2][0:1, :], scale=1.0),
               waits=[(SV, None)], inc=SC)
        ev.add("vector", lambda e: e.tensor_scalar_mul(srow[0:1, 2:3], srow[0:1, 2:3], -(1.0 / math.sqrt(D))),
               waits=[(SC, None)], inc=SV)
        ev.add("tensor", lambda e: e.matmul(prc[0][:, 0:1], onesr[0:1, 0:128], srow[0:1, 2:3],
                                            start=True, stop=True), waits=[(SV, None)], inc=SP)
        ev.add("vector", lambda e: e.tensor_copy(negM[:, 0:1], prc[0][:, 0:1]),
               waits=[(SP, None)], inc=SV)

        win_marks = {}
        pv_g = {}
        next_lds = {}
        last_rbc = [None]

        def load_qwin(qi_i2, w):
            qi2 = qi_i2 * QT
            dma(ev, "sync", SS, qdnwin[:, 0:DC * 512],
                qdn_dram.ap().rearrange("(c p) s -> p c s", p=128)[:, :, qi2:qi2 + QT], waits=w)
            dma(ev, "sync", SS, qupwin[:, 0:DC * 512],
                qup_dram.ap().rearrange("(c p) s -> p c s", p=128)[:, :, qi2:qi2 + QT])
            return SS.n

        def load_pair_kv(pair, w):
            kb = pair % 2
            dma(ev, "sync", SS, kdn_c[:, kb * S:(kb + 1) * S],
                kdn_dram.ap()[pair * 128:(pair + 1) * 128, :], waits=w)
            dma(ev, "sync", SS, kup_c[:, kb * S:(kb + 1) * S],
                kup_dram.ap()[pair * 128:(pair + 1) * 128, :])
            return SS.n

        for qi_i in range(NQT):
            qi = qi_i * QT
            if last_rbc[0] is None:
                last_rbc[0] = SP.n
            if qi_i == 0:
                qwin_ld = load_qwin(0, [(SS, None)])
                kv_marks = {0: load_pair_kv(0, [(SS, None)])}
            else:
                qwin_ld = next_lds['q']
                kv_marks = {0: next_lds['k']}
            nkp = NKT // 2
            slot_free = {}
            exp_marks = {}
            vfix = {}
            pctx = {}
            vscale = SV.n

            def emit_stile(pair, kp):
                kv_ld = kv_marks[pair]
                kb = pair % 2
                qo = pair * 512
                g = pair * nkp + kp
                for half in range(2):
                    kt = kp * 2 + half
                    kj = kt * 128
                    for hh in range(2):
                        sl = (2 * g + hh) % 3
                        ps = psS[sl]
                        rl, rh = (0, 64) if hh == 0 else (64, 128)
                        tp = (0, 0) if hh == 0 else (64, 0)
                        co = half * QT
                        ko = kb * S
                        mmw = [(SS, kv_ld), (SS, qwin_ld), (SC, slot_free.get(sl, 0))] + (
                            [(SV, vscale)] if pair == 0 else [])
                        def mk(e, kd, qd, n0, n1, ps=ps, rl=rl, rh=rh, tp=tp, co=co, kj=kj, ko=ko, qo=qo):
                            return e.matmul(ps[:, co + n0:co + n1],
                                            kd[rl:rh, ko + kj:ko + kj + 128], qd[rl:rh, qo + n0:qo + n1],
                                            start=True, stop=True, tile_position=tp)
                        if kj < qi:
                            ev.add("tensor", lambda e, f=mk: f(e, kdn_c, qdnwin, 0, QT), waits=mmw, inc=SP)
                        elif kj >= qi + QT:
                            ev.add("tensor", lambda e, f=mk: f(e, kup_c, qupwin, 0, QT), waits=mmw, inc=SP)
                        else:
                            j = (kj - qi) // 128
                            if j > 0:
                                ev.add("tensor", lambda e, f=mk, j=j: f(e, kup_c, qupwin, 0, j * 128),
                                       waits=mmw, inc=SP)
                            ev.add("tensor", lambda e, f=mk, j=j: f(e, kdn_c, qdnwin, j * 128, (j + 1) * 128),
                                   waits=mmw, inc=SP)
                            if j < NQSUB - 1:
                                ev.add("tensor", lambda e, f=mk, j=j: f(e, kdn_c, qdnwin, (j + 1) * 128, QT),
                                       waits=mmw, inc=SP)
                            ev.add("vector", lambda e, ps=ps, co=co, j=j: e.tensor_mul(
                                ps[:, co + j * 128:co + (j + 1) * 128],
                                ps[:, co + j * 128:co + (j + 1) * 128], rdiag[:, :]),
                                waits=[(SP, None)], inc=SV)
                            vfix[sl] = SV.n

            def emit_exp(pair, kp, smark):
                g = pair * nkp + kp
                pb = (g % 2) * 2 * QT
                gkp = (qi_i * PAIRS + pair) * nkp + kp
                bw = pv_g.get(gkp - 2, 0)
                for hh in range(2):
                    sl = (2 * g + hh) % 3
                    Pb = PAb if hh == 0 else PBb
                    ev.add("scalar", lambda e, sl=sl, Pb=Pb, pb=pb: e.activation(
                        Pb[:, pb:pb + 2 * QT], psS[sl][:, 0:2 * QT], AF.Exp,
                        bias=negM[:, 0:1], scale=1.0),
                        waits=[(SP, smark), (SP, bw), (SV, vfix.get(sl, 0))], inc=SC)
                    slot_free[sl] = SC.n
                exp_marks[(pair, kp)] = SC.n

            def emit_pv(pair, kp):
                g = pair * nkp + kp
                pb = (g % 2) * 2 * QT
                vv = pctx[pair][1]
                for half in range(2):
                    kt = kp * 2 + half
                    vw_w = []
                    if kt == 0:
                        vw_w = [(SV, None), (SS, vv[0])]
                    elif kt == NKT // 2:
                        vw_w = [(SS, vv[1])]
                    for hh in range(2):
                        Pb = PAb if hh == 0 else PBb
                        base = kt * 193 + (0 if hh == 0 else 65)
                        wid = 65 if hh == 0 else 128
                        pv = pvA if hh == 0 else pvB
                        pvsl = pv[0:65, 0:QT] if hh == 0 else pv[:, 0:QT]
                        ev.add("tensor", lambda e, base=base, wid=wid, pvsl=pvsl, Pb=Pb, half=half, kt=kt, pb=pb: e.matmul(
                            pvsl, vwin[:, base:base + wid],
                            Pb[:, pb + half * QT: pb + (half + 1) * QT],
                            start=(kt == 0), stop=(kt == NKT - 1)),
                            waits=[(SC, exp_marks[(pair, kp)])] + vw_w, inc=SP)
                pv_g[(qi_i * PAIRS + pair) * nkp + kp] = SP.n

            def emit_ltail(pair):
                pv_done = SP.n
                win_marks[(qi_i * PAIRS + pair, 'k')] = SP.n
                ev.add("vector", lambda e: e.tensor_copy(
                    lst[64:65, 0:QT], pvA[64:65, 0:QT]),
                    waits=[(SP, pv_done), (SS, None)], inc=SV)
                dma(ev, "sync", SS, lrow_dram.ap()[2 * pair:2 * pair + 1, :],
                    lst[64:65, 0:QT], waits=[(SV, None)])
                ev.add("vector", lambda e: e.tensor_copy(
                    lst[0:1, 0:QT], pvB[0:1, 0:QT]), inc=SV)
                dma(ev, "sync", SS, lrow_dram.ap()[2 * pair + 1:2 * pair + 2, :],
                    lst[0:1, 0:QT], waits=[(SV, None)])
                dma(ev, "sync", SS, lrows[2 * pair:2 * pair + 2, 0:QT],
                    lrow_dram.ap()[2 * pair:2 * pair + 2, :], waits=[(SS, None)])
                ev.add("vector", lambda e, pair=pair: e.tensor_copy(
                    attnraw[0:64, pair * QT:(pair + 1) * QT], pvA[0:64, 0:QT]), inc=SV)
                ev.add("vector", lambda e, pair=pair: e.tensor_copy(
                    attnraw[64:128, pair * QT:(pair + 1) * QT], pvB[64:128, 0:QT]), inc=SV)

            def load_vwin_half(pair, h, w):
                NH = NKT // 2
                k0 = h * NH
                dma(ev, "sync", SS,
                    bass.AP(vwin[:, 0:1].tensor, k0 * 193,
                            [[VW, 128], [193, NH], [1, 64]]),
                    vaug_dram.ap().rearrange("(k p) hd -> p k hd", p=128)[:, k0:k0 + NH, pair * 128:pair * 128 + 64],
                    waits=w)
                dma(ev, "sync", SS,
                    bass.AP(vwin[:, 0:1].tensor, k0 * 193 + 129,
                            [[VW, 128], [193, NH], [1, 64]]),
                    vaug_dram.ap().rearrange("(k p) hd -> p k hd", p=128)[:, k0:k0 + NH, pair * 128 + 64:(pair + 1) * 128])
                return SS.n

            for pair in range(PAIRS):
                pidx = qi_i * PAIRS + pair
                kb = pair % 2
                if pair == 0:
                    wk_w = [(SP, win_marks[(pidx - 1, 'k')])] if pidx >= 1 else [(SS, None)]
                    v0 = load_vwin_half(pair, 0, wk_w)
                    v1 = load_vwin_half(pair, 1, [])
                    pctx[pair] = (kv_marks[pair], [v0, v1], kb)
                for kp in range(nkp):
                    emit_stile(pair, kp)
                    smark = SP.n
                    if kp > 0:
                        emit_pv(pair, kp - 1)
                        if kp == nkp // 2 and pair + 1 < PAIRS:
                            v0n = load_vwin_half(pair + 1, 0,
                                                 [(SP, pv_g[pidx * nkp + nkp // 2 - 1])])
                            pctx[pair + 1] = (None, [v0n, None], kb ^ 1)
                    elif pair > 0:
                        emit_pv(pair - 1, nkp - 1)
                        emit_ltail(pair - 1)
                        v1n = load_vwin_half(pair, 1,
                                             [(SP, pv_g[(pidx - 1) * nkp + nkp - 1])])
                        pctx[pair] = (kv_marks[pair], [pctx[pair][1][0], v1n], kb)
                    if kp == 0 and pair + 1 < PAIRS:
                        kv_marks[pair + 1] = load_pair_kv(
                            pair + 1, [(SP, win_marks.get((pidx - 1, 'k'), 0) if pair >= 1 else smark)])
                    emit_exp(pair, kp, smark)
            emit_pv(PAIRS - 1, nkp - 1)
            emit_ltail(PAIRS - 1)
            win_marks[(qi_i, 'q')] = SP.n
            if qi_i + 1 < NQT:
                next_lds['q'] = load_qwin(qi_i + 1, [(SP, win_marks[(qi_i, 'q')])])
                next_lds['k'] = load_pair_kv(0, [(SP, win_marks[((qi_i + 1) * PAIRS - 2, 'k')])])
            ev.add("vector", lambda e: e.reciprocal(rip[0:2 * PAIRS, 0:QT], lrows[0:2 * PAIRS, 0:QT]),
                   waits=[(SS, None)], inc=SV)
            dma(ev, "sync", SS, rrow_dram.ap()[:, :], rip[0:2 * PAIRS, 0:QT], waits=[(SV, None)])
            fin_marks = {}
            rbc_marks = {}
            for pair in range(PAIRS):
                rsl = rbuf if pair % 2 == 0 else rowbuf
                w4 = [(SP, rbc_marks[pair - 2])] if pair >= 2 else [(SP, last_rbc[0])]
                dma(ev, "sync", SS, rsl[0:1, 0:QT],
                    rrow_dram.ap()[2 * pair:2 * pair + 1, :],
                    waits=([(SS, None)] if pair < 2 else []) + w4)
                dma(ev, "sync", SS, rsl[0:1, QT:2 * QT],
                    rrow_dram.ap()[2 * pair + 1:2 * pair + 2, :])
                pr = prc[pair % 2]
                w4b = [(SV, fin_marks[pair - 2])] if pair >= 2 else []
                ev.add("tensor", lambda e, pr=pr, rsl=rsl: e.matmul(
                    pr[0:64, 0:QT], onesr[0:1, 0:64], rsl[0:1, 0:QT],
                    start=True, stop=True, tile_position=(0, 0)),
                    waits=[(SS, None)] + w4b, inc=SP)
                ev.add("tensor", lambda e, pr=pr, rsl=rsl: e.matmul(
                    pr[64:128, 0:QT], onesr[0:1, 64:128], rsl[0:1, QT:2 * QT],
                    start=True, stop=True, tile_position=(0, 64)), inc=SP)
                rbc_marks[pair] = SP.n
                ev.add("vector", lambda e, pr=pr, pair=pair: e.tensor_mul(
                    attnraw[:, pair * QT:(pair + 1) * QT], attnraw[:, pair * QT:(pair + 1) * QT],
                    pr[:, 0:QT]), waits=[(SP, None)], inc=SV)
                fin_marks[pair] = SV.n
            last_rbc[0] = SP.n
            attn_done = SV.n
            ep_marks = {}
            for tq in range(NQSUB):
                tt = qi_i * NQSUB + tq
                buf = tq % 2
                if tq >= 2:
                    w5 = [(SV, ep_marks[(tq - 2, 'x')])]
                elif (tq + NQSUB - 2, 'x') in ep_marks:
                    w5 = [(SV, ep_marks[(tq + NQSUB - 2, 'x')])]
                else:
                    w5 = [(SV, ap_marks[NTT - 2 + tq])]
                dma(ev, "gpsimd", SG, xbt[:, buf * D:(buf + 1) * D],
                    xd.ap()[b, qi + tq * 128: qi + (tq + 1) * 128, :], waits=w5)
                for jh in range(JH):
                    po = prc[jh % 2] if JH > 1 else prc[tq % 2]
                    for c in range(DC):
                        w2 = []
                        if c == 0:
                            w2 = [(SV, attn_done)]
                            prevk = (tq - 1, 'e' + str(jh)) if JH >= 2 else (tq - 2, 'e' + str(jh))
                            if prevk in ep_marks:
                                w2.append((SV, ep_marks[prevk]))
                        ev.add("tensor", lambda e, tq=tq, jh=jh, c=c, po=po: e.matmul(
                            po[:, 0:JW], attnraw[:, c * QT + tq * 128: c * QT + (tq + 1) * 128],
                            wos[:, c * D + jh * JW: c * D + (jh + 1) * JW],
                            start=(c == 0), stop=(c == DC - 1)),
                            waits=w2, inc=(SP if c == DC - 1 else None))
                    ev.add("vector", lambda e, jh=jh, po=po: e.tensor_mul(
                        ot_t[:, jh * JW:(jh + 1) * JW], po[:, 0:JW],
                        gar[:, jh * JW:(jh + 1) * JW]),
                        waits=[(SP, None)], inc=SV)
                    ev.add("vector", lambda e, jh=jh, buf=buf: e.tensor_add(
                        x1t[:, buf * D + jh * JW: buf * D + (jh + 1) * JW],
                        ot_t[:, jh * JW:(jh + 1) * JW], xbt[:, buf * D + jh * JW: buf * D + (jh + 1) * JW]),
                        waits=[(SG, None)], inc=SV)
                    ep_marks[(tq, 'e' + str(jh))] = SV.n
                ev.add("vector", lambda e, tt=tt, buf=buf: e.reduce_sum(
                    st_sum[:, tt:tt + 1], x1t[:, buf * D:(buf + 1) * D], axis=AX), inc=SV)
                ep_marks[(tq, 'x')] = SV.n
                ev.add("scalar", lambda e, tt=tt, buf=buf: e.activation(
                    h2f[:, 0:D], x1t[:, buf * D:(buf + 1) * D], AF.Square,
                    accum_out=st_sq[:, tt:tt + 1]), waits=[(SV, None)], inc=SC)
                dma(ev, "gpsimd", SG, x1_dram.ap()[brow + qi + tq * 128: brow + qi + (tq + 1) * 128, :],
                    x1t[:, buf * D:(buf + 1) * D], waits=[(SV, None), (SC, None)])
        ev.add("vector", lambda e: e.tensor_scalar_mul(st_negm[:, 0:NTT], st_sum[:, 0:NTT], -1.0 / D),
               waits=[(SC, None), (SG, None)], inc=SV)
        ev.add("vector", lambda e: e.tensor_mul(st_tmp[:, 0:NTT], st_negm[:, 0:NTT], st_negm[:, 0:NTT]), inc=SV)
        ev.add("vector", lambda e: e.scalar_tensor_tensor(
            st_tmp[:, 0:NTT], st_sq[:, 0:NTT], 1.0 / D, st_tmp[:, 0:NTT], OP.mult, OP.subtract), inc=SV)
        ev.add("scalar", lambda e: e.activation(st_tmp[:, 0:NTT], st_tmp[:, 0:NTT], AF.Sqrt,
                                                bias=konst[:, 1:2], scale=1.0), waits=[(SV, None)], inc=SC)
        ev.add("vector", lambda e: e.reciprocal(st_inv[:, 0:NTT], st_tmp[:, 0:NTT]),
               waits=[(SC, None)], inc=SV)
        stats2 = SV.n
        h2_marks = {}
        h2_store = {}
        for tt in range(NTT):
            buf = tt % 2
            rs = h2f[:, buf * D:(buf + 1) * D]
            w6 = [(SV, h2_marks[tt - 2])] if tt >= 2 else [(SC, None), (SG, None), (SV, None)]
            dma(ev, "gpsimd", SG, x1t[:, buf * D:(buf + 1) * D],
                x1_dram.ap()[brow + tt * 128: brow + (tt + 1) * 128, :], waits=w6)
            ev.add("vector", lambda e, tt=tt, buf=buf, rs=rs: e.tensor_scalar(
                rs, x1t[:, buf * D:(buf + 1) * D],
                st_negm[:, tt:tt + 1], st_inv[:, tt:tt + 1], OP.add, OP.mult),
                waits=[(SG, h2_store.get(tt - 2, 0)), (SG, None), (SV, stats2), (SC, None)], inc=SV)
            ev.add("vector", lambda e, rs=rs: e.tensor_mul(rs, rs, cbr[:, 0:D]), inc=SV)
            ev.add("vector", lambda e, rs=rs: e.tensor_add(rs, rs, sbr[:, 0:D]), inc=SV)
            ev.add("vector", lambda e, rs=rs: e.reduce_max(mq_cols[:, 2:3], rs, axis=AX,
                                                           apply_absolute_value=True), inc=SV)
            first = (b == 0 and tt == 0)
            ev.add("vector", (lambda e: e.tensor_copy(gmax_cols[:, 0:1], mq_cols[:, 2:3])) if first
                   else (lambda e: e.tensor_max(gmax_cols[:, 0:1], gmax_cols[:, 0:1], mq_cols[:, 2:3])), inc=SV)
            h2_marks[tt] = SV.n
            dma(ev, "gpsimd", SG, h2_dram.ap()[brow + tt * 128: brow + (tt + 1) * 128, :],
                rs, waits=[(SV, None)])
            h2_store[tt] = SG.n
        run_phase(ev)

    esB.close()
    psB_.close()

    esD = ExitStack()
    psD_ = ExitStack()
    qwgT = sb("qwgT", [128, DC * I], BF16, esD)
    qwuT = sb("qwuT", [128, DC * I], BF16, esD)
    h2t = sb("h2t", [128, 2 * D], F32, esD)
    xqd = sb("xqd", [128, 2 * D], BF16, esD)
    xqT = sb("xqT", [128, 2 * DC * 512], BF16, esD)
    gsil = sb("gsil", [128, 4 * 512], F32, esD)
    hpre_t = sb("hpre_t", [128, 4 * 512], F32, esD)
    pg = [psD_.enter_context(nc.psum_tensor(f"pg{i}", [128, 512], F32)) for i in range(4)]
    pu = [psD_.enter_context(nc.psum_tensor(f"pu{i}", [128, 512], F32)) for i in range(4)]
    pbc2 = pg[0]
    ev = EV()
    for c in range(DC):
        dma(ev, "gpsimd", SG, qwgT[:, c * I:(c + 1) * I], qwgT_dram.ap()[c * 128:(c + 1) * 128, :])
        dma(ev, "gpsimd", SG, qwuT[:, c * I:(c + 1) * I], qwuT_dram.ap()[c * 128:(c + 1) * 128, :])
    gw_loaded = SG.n
    dma(ev, "sync", SS, pmax_dram.ap()[0:1, :].rearrange("a b -> (a b)"), gmax_cols[:, 0:1])
    dma(ev, "sync", SS, agrow[0:1, 0:128], pmax_dram.ap()[0:1, :], waits=[(SS, None)])
    ev.add("vector", lambda e: e.reduce_max(srow[0:1, 0:1], agrow[0:1, 0:128], axis=AX),
           waits=[(SS, None)], inc=SV)
    ev.add("vector", lambda e: e.memset(srow[0:1, 1:8], 0.0), inc=SV)
    dma(ev, "sync", SS, ag1_in.ap(), srow[0:1, 0:8], waits=[(SV, None)])
    ev.add("gpsimd", lambda e: e.collective_compute(
        "AllGather", OP.bypass, replica_groups=[list(range(NCORES))],
        ins=[ag1_in.ap().opt()], outs=[ag1_out.ap().opt()]),
        waits=[(SS, None)], inc=SCC)
    dma(ev, "sync", SS, agrow[0:1, 0:8 * NCORES],
        ag1_out.ap().rearrange("a b -> (a b)").rearrange("(a b) -> a b", a=1), waits=[(SCC, None)])
    ev.add("vector", lambda e: e.reduce_max(srow[0:1, 1:2], agrow[0:1, 0:8 * NCORES], axis=AX),
           waits=[(SS, None)], inc=SV)
    ev.add("scalar", lambda e: e.activation(srow[0:1, 2:3], clip_sb[0:1, 0:1], AF.Sigmoid),
           waits=[(SV, None)], inc=SC)
    ev.add("vector", lambda e: e.tensor_mul(srow[0:1, 3:4], srow[0:1, 1:2], srow[0:1, 2:3]),
           waits=[(SC, None)], inc=SV)
    ev.add("vector", lambda e: e.tensor_scalar_mul(srow[0:1, 3:4], srow[0:1, 3:4], 1.0 / QMAX), inc=SV)
    ev.add("vector", lambda e: e.reciprocal(srow[0:1, 4:5], srow[0:1, 3:4]), inc=SV)
    for (colt, sofs) in ((sx_col, 3), (sxr_col, 4)):
        ev.add("tensor", lambda e, sofs=sofs: e.matmul(pbc2[:, 0:1], onesr[0:1, 0:128],
                                                       srow[0:1, sofs:sofs + 1], start=True, stop=True),
               waits=[(SV, None)], inc=SP)
        ev.add("vector", lambda e, colt=colt: e.tensor_copy(colt[:, 0:1], pbc2[:, 0:1]),
               waits=[(SP, None)], inc=SV)
    ev.add("vector", lambda e: e.tensor_scalar(sgx_cols[:, 0:IT], sg_cols[:, 0:IT],
                                               sx_col[:, 0:1], None, OP.mult), inc=SV)
    ev.add("vector", lambda e: e.memset(hmax_cols[:, 0:IT], 0.0), inc=SV)
    d_marks = {}

    def d_quant(t5):
        r0 = t5 * 512
        sl = (t5 % 2) * DC * 512
        for sub in range(4):
            gs = t5 * 4 + sub
            bs = gs % 2
            dma(ev, "gpsimd", SG, h2t[:, bs * D:(bs + 1) * D],
                h2_dram.ap()[r0 + sub * 128: r0 + (sub + 1) * 128, :],
                waits=[(SV, d_marks.get(('dq', gs - 2), 0))])
            ev.add("vector", lambda e, bs=bs: e.tensor_scalar(
                h2t[:, bs * D:(bs + 1) * D], h2t[:, bs * D:(bs + 1) * D],
                sxr_col[:, 0:1], CM, OP.mult, OP.add), waits=[(SG, None)], inc=SV)
            ev.add("vector", lambda e, bs=bs: e.tensor_scalar(
                h2t[:, bs * D:(bs + 1) * D], h2t[:, bs * D:(bs + 1) * D],
                CM - 128.0, CM + 127.0, OP.max, OP.min), inc=SV)
            ev.add("vector", lambda e, bs=bs: e.tensor_scalar(
                xqd[:, bs * D:(bs + 1) * D], h2t[:, bs * D:(bs + 1) * D],
                CM, None, OP.subtract),
                waits=[(SS, d_marks.get(('dxT', gs - 2), 0))], inc=SV)
            d_marks[('dq', gs)] = SV.n
            for c in range(DC):
                dma_t(ev, "sync", SS, xqT[:, sl + c * 512 + sub * 128: sl + c * 512 + (sub + 1) * 128],
                      xqd[:, bs * D + c * 128: bs * D + (c + 1) * 128],
                      waits=[(SV, None)] + ([(SP, d_marks[(t5 - 2, 'p')])] if t5 >= 2 else []))
            d_marks[('dxT', gs)] = SS.n
        d_marks[(t5, 'hq')] = SV.n
        d_marks[(t5, 'xT')] = SS.n

    def d_mms(t5):
        r0 = t5 * 512
        sl = (t5 % 2) * DC * 512
        gu_marks = d_marks
        def gprev(it, tag):
            return gu_marks.get(('g', t5 * IT + it - 4, tag), 0)
        for it in range(IT):
            buf = (t5 * IT + it) % 4
            for c in range(DC):
                w = []
                if c == 0:
                    w = [(SS, d_marks[(t5, 'xT')]), (SG, gw_loaded),
                         (SC, gprev(it, 's')), (SV, gprev(it, 'v'))]
                ev.add("tensor", lambda e, it=it, c=c, buf=buf, sl=sl: e.matmul(
                    pg[buf][:, 0:512], qwgT[:, c * I + it * 128: c * I + (it + 1) * 128],
                    xqT[:, sl + c * 512:sl + (c + 1) * 512], start=(c == 0), stop=(c == DC - 1)),
                    waits=w, inc=(SP if c == DC - 1 else None))
            for c in range(DC):
                ev.add("tensor", lambda e, it=it, c=c, buf=buf, sl=sl: e.matmul(
                    pu[buf][:, 0:512], qwuT[:, c * I + it * 128: c * I + (it + 1) * 128],
                    xqT[:, sl + c * 512:sl + (c + 1) * 512], start=(c == 0), stop=(c == DC - 1)),
                    inc=(SP if c == DC - 1 else None))
            ev.add("scalar", lambda e, it=it, buf=buf: e.activation(
                gsil[:, buf * 512:(buf + 1) * 512], pg[buf][:, 0:512], AF.Silu,
                bias=0.0, scale=sgx_cols[:, it:it + 1]),
                waits=[(SP, None), (SV, gprev(it, 'v'))], inc=SC)
            gu_marks[('g', t5 * IT + it, 's')] = SC.n
            hw = [(SC, None), (SG, gprev(it, 'g'))]
            ev.add("vector", lambda e, it=it, buf=buf: e.scalar_tensor_tensor(
                hpre_t[:, buf * 512:(buf + 1) * 512], pu[buf][:, 0:512], 1.0,
                gsil[:, buf * 512:(buf + 1) * 512], OP.bypass, OP.mult),
                waits=hw, inc=SV)
            ev.add("vector", lambda e, buf=buf: e.reduce_max(
                mq_cols[:, 2:3], hpre_t[:, buf * 512:(buf + 1) * 512], axis=AX,
                apply_absolute_value=True), inc=SV)
            ev.add("vector", lambda e, it=it: e.tensor_max(
                hmax_cols[:, it:it + 1], hmax_cols[:, it:it + 1], mq_cols[:, 2:3]), inc=SV)
            gu_marks[('g', t5 * IT + it, 'v')] = SV.n
            dma(ev, "gpsimd", SG, hpre_dram.ap()[it * 128:(it + 1) * 128, r0:r0 + 512],
                hpre_t[:, buf * 512:(buf + 1) * 512], waits=[(SV, None)])
            gu_marks[('g', t5 * IT + it, 'g')] = SG.n
        d_marks[(t5, 'p')] = SP.n

    for t5 in range(NT512 + 1):
        if t5 < NT512:
            d_quant(t5)
        if t5 >= 1:
            d_mms(t5 - 1)
    ev.add("vector", lambda e: e.tensor_mul(hmax_cols[:, 0:IT], hmax_cols[:, 0:IT], su_cols[:, 0:IT]),
           waits=[(SG, None)], inc=SV)
    ev.add("vector", lambda e: e.tensor_scalar(hmax_cols[:, 0:IT], hmax_cols[:, 0:IT],
                                               sx_col[:, 0:1], None, OP.mult), inc=SV)
    ev.add("vector", lambda e: e.reduce_max(mq_cols[:, 2:3], hmax_cols[:, 0:IT], axis=AX), inc=SV)
    run_phase(ev, psD_)
    esD.close()

    esE = ExitStack()
    esF = ExitStack()
    gbeff = sb("gbeff", [128, B * D], F32, esF)
    psF_ = ExitStack()
    qwdT = sb("qwdT", [128, IT * D], BF16, esF)
    hpf = sb("hpf", [128, 4 * 512], F32, esF)
    hqf = sb("hqf", [128, 2 * IT * 512], BF16, esF)
    x1rt = sb("x1rt", [128, 2 * D], BF16, esF)
    outt = sb("outt", [128, 2 * D], F32, esF)
    tqf = sb("tqf", [128, 4 * 512], F32, esF)
    pd_ = [psF_.enter_context(nc.psum_tensor(f"pd{i}", [128, 512], F32)) for i in range(4)]
    pbc3 = psF_.enter_context(nc.psum_tensor("pbc3", [128, 512], F32))
    sdrow = sb("sdrow", [1, D], F32, esE)
    gerow = sb("gerow", [1, D], F32, esE)
    ev = EV()
    for c in range(IT):
        dma(ev, "gpsimd", SG, qwdT[:, c * D:(c + 1) * D], qwdT_dram.ap()[c * 128:(c + 1) * 128, :])
    dw_loaded = SG.n
    dma(ev, "sync", SS, pmax_dram.ap()[0:1, :].rearrange("a b -> (a b)"), mq_cols[:, 2:3])
    dma(ev, "sync", SS, agrow[0:1, 0:128], pmax_dram.ap()[0:1, :], waits=[(SS, None)])
    ev.add("vector", lambda e: e.reduce_max(srow[0:1, 0:1], agrow[0:1, 0:128], axis=AX),
           waits=[(SS, None)], inc=SV)
    ev.add("vector", lambda e: e.memset(srow[0:1, 1:8], 0.0), inc=SV)
    dma(ev, "sync", SS, ag2_in.ap(), srow[0:1, 0:8], waits=[(SV, None)])
    ev.add("gpsimd", lambda e: e.collective_compute(
        "AllGather", OP.bypass, replica_groups=[list(range(NCORES))],
        ins=[ag2_in.ap().opt()], outs=[ag2_out.ap().opt()]),
        waits=[(SS, None)], inc=SCC)
    dma(ev, "sync", SS, agrow[0:1, 0:8 * NCORES],
        ag2_out.ap().rearrange("a b -> (a b)").rearrange("(a b) -> a b", a=1), waits=[(SCC, None)])
    ev.add("vector", lambda e: e.reduce_max(srow[0:1, 1:2], agrow[0:1, 0:8 * NCORES], axis=AX),
           waits=[(SS, None)], inc=SV)
    ev.add("scalar", lambda e: e.activation(srow[0:1, 2:3], clip_sb[0:1, 1:2], AF.Sigmoid),
           waits=[(SV, None)], inc=SC)
    ev.add("vector", lambda e: e.tensor_mul(srow[0:1, 3:4], srow[0:1, 1:2], srow[0:1, 2:3]),
           waits=[(SC, None)], inc=SV)
    ev.add("vector", lambda e: e.tensor_scalar_mul(srow[0:1, 3:4], srow[0:1, 3:4], 1.0 / QMAX), inc=SV)
    ev.add("vector", lambda e: e.reciprocal(srow[0:1, 4:5], srow[0:1, 3:4]), inc=SV)
    ev.add("tensor", lambda e: e.matmul(pbc3[:, 0:1], onesr[0:1, 0:128], srow[0:1, 4:5],
                                        start=True, stop=True), waits=[(SV, None)], inc=SP)
    ev.add("vector", lambda e: e.tensor_copy(mq_cols[:, 3:4], pbc3[:, 0:1]), waits=[(SP, None)], inc=SV)
    ev.add("vector", lambda e: e.tensor_scalar(q2s_cols[:, 0:IT], su_cols[:, 0:IT],
                                               sx_col[:, 0:1], None, OP.mult), inc=SV)
    ev.add("vector", lambda e: e.tensor_scalar(q2s_cols[:, 0:IT], q2s_cols[:, 0:IT],
                                               mq_cols[:, 3:4], None, OP.mult), inc=SV)
    q2s_done = SV.n
    dma(ev, "sync", SS, sdrow[0:1, 0:D],
        sd_dram.ap().rearrange("c p -> (c p)").rearrange("(a b) -> a b", a=1))
    for b in range(B):
        dma(ev, "sync", SS, rowbuf[0:1, 0:D],
            mod_dram.ap()[b, 5 * D:6 * D].rearrange("(a c) -> a c", a=1), waits=[(SV, None)])
        ev.add("vector", lambda e: e.tensor_mul(gerow[0:1, 0:D], rowbuf[0:1, 0:D], sdrow[0:1, 0:D]),
               waits=[(SS, None), (SP, None)], inc=SV)
        ev.add("vector", lambda e: e.tensor_scalar(gerow[0:1, 0:D], gerow[0:1, 0:D],
                                                   srow[0:1, 3:4], None, OP.mult), inc=SV)
        for j in range(D // JW):
            ev.add("tensor", lambda e, j=j: e.matmul(
                pbc3[:, 0:JW], onesr[0:1, 0:128], gerow[0:1, j * JW:(j + 1) * JW],
                start=True, stop=True), waits=[(SV, None)], inc=SP)
            ev.add("vector", lambda e, b=b, j=j: e.tensor_copy(
                gbeff[:, b * D + j * JW: b * D + (j + 1) * JW], pbc3[:, 0:JW]),
                waits=[(SP, None)], inc=SV)
    f_marks = {}

    def f_quant(t5):
        r0 = t5 * 512
        sl = (t5 % 2) * IT * 512
        for it in range(IT):
            g = t5 * IT + it
            buf = g % 4
            w = [(SC, f_marks.get(('fc', g - 4), 0))]
            dma(ev, "gpsimd", SG, hpf[:, buf * 512:(buf + 1) * 512],
                hpre_dram.ap()[it * 128:(it + 1) * 128, r0:r0 + 512], waits=w)
            wv = [(SV, f_marks.get(('fv', g - 4), 0))]
            ev.add("scalar", lambda e, it=it, buf=buf: e.activation(
                tqf[:, buf * 512:(buf + 1) * 512], hpf[:, buf * 512:(buf + 1) * 512], AF.Identity,
                bias=konst[:, 0:1], scale=q2s_cols[:, it:it + 1]),
                waits=[(SG, None), (SV, q2s_done)] + wv, inc=SC)
            f_marks[('fc', g)] = SC.n
            ev.add("vector", lambda e, buf=buf: e.tensor_scalar(
                tqf[:, buf * 512:(buf + 1) * 512], tqf[:, buf * 512:(buf + 1) * 512],
                CM - 128.0, CM + 127.0, OP.max, OP.min), waits=[(SC, None)], inc=SV)
            ev.add("vector", lambda e, it=it, buf=buf, sl=sl: e.tensor_scalar(
                hqf[:, sl + it * 512: sl + (it + 1) * 512], tqf[:, buf * 512:(buf + 1) * 512],
                CM, None, OP.subtract),
                waits=[(SP, f_marks[(t5 - 2, 'p')])] if (t5 >= 2 and it < 2) else [], inc=SV)
            f_marks[('fv', g)] = SV.n
        f_marks[(t5, 'hq')] = SV.n

    def f_mms(t5):
        r0 = t5 * 512
        b = r0 // S
        sl = (t5 % 2) * IT * 512
        for tq in range(4):
            bufx = tq % 2
            wx = [(SV, f_marks[(t5, tq - 2, 'a')])] if (tq >= 2) else (
                [(SV, f_marks[(t5 - 1, tq + 2, 'a')])] if t5 >= 1 else [])
            dma(ev, "gpsimd", SG, x1rt[:, bufx * D:(bufx + 1) * D],
                x1_dram.ap()[r0 + tq * 128: r0 + (tq + 1) * 128, :], waits=wx)
            for jh in range(JH):
                gj = (t5 * 4 + tq) * JH + jh
                po = pd_[gj % 4]
                for it in range(IT):
                    w2 = []
                    if it == 0:
                        w2 = [(SG, dw_loaded), (SV, f_marks.get(('fm', gj - 4), 0))]
                    if tq == 0 and jh == 0:
                        w2.append((SV, f_marks[('fv', t5 * IT + it)]))
                    elif it == 0:
                        w2.append((SV, f_marks[('fv', t5 * IT + IT - 1)]))
                    ev.add("tensor", lambda e, tq=tq, jh=jh, it=it, po=po, sl=sl: e.matmul(
                        po[:, 0:JW], hqf[:, sl + it * 512 + tq * 128: sl + it * 512 + (tq + 1) * 128],
                        qwdT[:, it * D + jh * JW: it * D + (jh + 1) * JW],
                        start=(it == 0), stop=(it == IT - 1)),
                        waits=w2, inc=(SP if it == IT - 1 else None))
                ev.add("vector", lambda e, jh=jh, po=po, b=b: e.tensor_mul(
                    outt[:, jh * JW:(jh + 1) * JW], po[:, 0:JW],
                    gbeff[:, b * D + jh * JW: b * D + (jh + 1) * JW]),
                    waits=[(SP, None), (SST, None)], inc=SV)
                ev.add("vector", lambda e, jh=jh, bufx=bufx: e.tensor_add(
                    outt[:, D + jh * JW: D + (jh + 1) * JW], outt[:, jh * JW:(jh + 1) * JW],
                    x1rt[:, bufx * D + jh * JW: bufx * D + (jh + 1) * JW]),
                    waits=[(SG, None)], inc=SV)
                f_marks[('fm', gj)] = SV.n
            f_marks[(t5, tq, 'a')] = SV.n
            ev.add_dma("sync", SST, lambda e, b=b, r0=r0, tq=tq: e.dma_start(
                out=outd.ap()[b, (r0 % S) + tq * 128: (r0 % S) + (tq + 1) * 128, :],
                in_=outt[:, D:2 * D]), waits=[(SV, None)])
        f_marks[(t5, 'p')] = SP.n

    for t5 in range(NT512 + 1):
        if t5 < NT512:
            f_quant(t5)
        if t5 >= 1:
            f_mms(t5 - 1)
    ev.add("sync", lambda e: e.nop(), waits=[(SST, None)])
    run_phase(ev, psF_)
    esE.close()
    esF.close()
    esg.close()
    return nc


def _host_consts(cfg: Cfg):
    S = cfg.S
    t = np.arange(S, dtype=np.float64)
    fdn = np.exp(-LAM * t).astype(np.float32)
    fup = np.exp(LAM * t).astype(np.float32)
    isq = 1.0 / math.sqrt(cfg.D)
    gdn = (np.exp(LAM * t) * isq).astype(np.float32)
    gup = (np.exp(-LAM * t) * isq).astype(np.float32)
    rep = lambda v: np.broadcast_to(v[None, :], (128, S)).copy()
    p = np.arange(128.0)[:, None]
    f = np.arange(128.0)[None, :]
    rdiag = np.exp(-2 * LAM * np.maximum(0.0, p - f)).astype(np.float32)
    konst = np.zeros((128, 8), np.float32)
    konst[:, 0] = CM
    konst[:, 1] = EPS
    konst[:, 2] = 1.0
    return {
        "idm": np.eye(128, dtype=np.float32),
        "fdn": rep(fdn), "fup": rep(fup), "gdn": rep(gdn), "gup": rep(gup),
        "rdiag": rdiag,
        "onesr": np.ones((1, 128), np.float32),
        "konst": konst,
    }


_NC_CACHE = {}


def kernel(L0_attn_x, L0_attn_W_q, L0_attn_W_k, L0_attn_W_v, L0_attn_W_o,
           L0_attn_W_down_kv, L0_attn_W_up_k, L0_attn_W_up_v,
           L0_ffn_clip_x, L0_ffn_clip_down, L0_ffn_w_gate, L0_ffn_w_up,
           L0_ffn_w_down, L0_c, L0_wmod):
    cfg = Cfg()
    if "main" not in _NC_CACHE:
        _NC_CACHE["main"] = build_program(cfg)
    nc = _NC_CACHE["main"]
    x = np.ascontiguousarray(np.asarray(L0_attn_x, dtype=np.float32))
    shared = {
        "wq": np.ascontiguousarray(L0_attn_W_q, dtype=np.float32),
        "wk": np.ascontiguousarray(L0_attn_W_k, dtype=np.float32),
        "wv": np.ascontiguousarray(L0_attn_W_v, dtype=np.float32),
        "wo": np.ascontiguousarray(L0_attn_W_o, dtype=np.float32),
        "wdn": np.ascontiguousarray(L0_attn_W_down_kv, dtype=np.float32),
        "wuk": np.ascontiguousarray(L0_attn_W_up_k, dtype=np.float32),
        "wuv": np.ascontiguousarray(L0_attn_W_up_v, dtype=np.float32),
        "wmod": np.ascontiguousarray(L0_wmod, dtype=np.float32),
        "wg": np.ascontiguousarray(L0_ffn_w_gate, dtype=np.float32),
        "wu": np.ascontiguousarray(L0_ffn_w_up, dtype=np.float32),
        "wdw": np.ascontiguousarray(L0_ffn_w_down, dtype=np.float32),
        "clip2": np.stack([np.asarray(L0_ffn_clip_x, np.float32).reshape(1),
                           np.asarray(L0_ffn_clip_down, np.float32).reshape(1)],
                          axis=1).reshape(1, 2),
        **_host_consts(cfg),
    }
    cvals = np.ascontiguousarray(L0_c, dtype=np.float32)
    in_maps = []
    for i in range(cfg.n_cores):
        m = dict(shared)
        m["x"] = x[2 * i:2 * i + 2]
        m["cmod"] = cvals[2 * i:2 * i + 2]
        in_maps.append(m)
    res = run_bass_kernel_spmd(nc, in_maps, core_ids=list(range(cfg.n_cores)))
    out = np.concatenate([res.results[i]["out"] for i in range(cfg.n_cores)], axis=0)
    return out.astype(np.float32)
